# version 45
# baseline (speedup 1.0000x reference)
import math
import contextlib
import numpy as np
import concourse.bass as bass
import concourse.mybir as mybir
from concourse.bass_utils import run_bass_kernel_spmd

F32 = mybir.dt.float32
BF16 = mybir.dt.bfloat16
AF = mybir.ActivationFunctionType
ALU = mybir.AluOpType
AX = mybir.AxisListType

S = 2048
D = 2048
H = 16
DH = 128
EQ = 2304
FDIM = 6144
NCH = D // 128
NJ = S // 128
NW = S // 512
ISQ = 1.0 / math.sqrt(128.0)
SUB = 8
GF = 4
MASK_NEG = -1e30
EPS = 1e-5

ENGINES = ("tensor", "vector", "scalar", "gpsimd", "sync")
DEBUG_SPILLS = False


class Prog:

    def __init__(self, nc):
        self.nc = nc
        self.items = {e: [] for e in ENGINES}
        self.counts = {}
        self.fence = []

    def sem(self, name):
        self.counts.setdefault(name, 0)
        return name

    def tok(self, name):
        self.sem(name)
        return (name, self.counts[name])

    def set_fence(self):
        self.fence = [(n, v) for n, v in self.counts.items() if v > 0]

    def do(self, eng, fn, waits=(), inc=None, inc_by=1):
        flat = []
        for t in waits:
            if t is None:
                continue
            if isinstance(t, list):
                flat.extend(x for x in t if x is not None)
            else:
                flat.append(t)
        w = flat + list(self.fence)
        tokv = None
        if inc is not None:
            self.sem(inc)
            self.counts[inc] += inc_by
            tokv = (inc, self.counts[inc])
        self.items[eng].append((fn, w, inc, inc_by))
        return tokv

    def dma(self, eng, out_ap, in_ap, waits=(), inc=None, **kw):
        def fn(e, out_ap=out_ap, in_ap=in_ap, kw=kw):
            return e.dma_start(out_ap, in_ap, **kw)
        return self.do(eng, fn, waits=waits, inc=inc, inc_by=16)

    def dmas(self, eng, sem, out_ap, in_ap, waits=(), **kw):
        return self.dma(eng, out_ap, in_ap, waits=waits, inc=sem, **kw)

    def emit(self):
        nc = self.nc
        with contextlib.ExitStack() as es:
            handles = {n: es.enter_context(nc.semaphore(n)) for n in self.counts}
            with nc.Block() as block:
                for eng_name in ENGINES:
                    items = self.items[eng_name]
                    if not items:
                        continue

                    def body(eng, items=items):
                        waited = {}
                        for fn, waits, inc, inc_by in items:
                            for (sname, val) in waits:
                                if waited.get(sname, 0) < val:
                                    eng.wait_ge(handles[sname], val)
                                    waited[sname] = val
                            ins = fn(eng)
                            if inc is not None:
                                ins.then_inc(handles[inc], inc_by)
                    getattr(block, eng_name)(body)


def build_program():
    nc = bass.Bass("TRN2", target_bir_lowering=False)

    x_d = nc.dram_tensor("x", [S, D], F32, kind="ExternalInput")
    wqkv_d = nc.dram_tensor("wqkv", [EQ, D], F32, kind="ExternalInput")
    bqkv_d = nc.dram_tensor("bqkv", [1, EQ], F32, kind="ExternalInput")
    wout_d = nc.dram_tensor("wout", [D, D], F32, kind="ExternalInput")
    bout_d = nc.dram_tensor("bout", [1, D], F32, kind="ExternalInput")
    wa_d = nc.dram_tensor("wa", [D, FDIM], F32, kind="ExternalInput")
    wp_d = nc.dram_tensor("wp", [D, D], F32, kind="ExternalInput")
    wstd_d = nc.dram_tensor("wstd", [1, H], F32, kind="ExternalInput")
    wrec_d = nc.dram_tensor("wrec", [1, H], F32, kind="ExternalInput")
    out_d = nc.dram_tensor("out", [S, D], F32, kind="ExternalOutput")

    spill_kind = "ExternalOutput" if DEBUG_SPILLS else "Internal"
    qf1_d = nc.dram_tensor("qf1_spill", [D, S], BF16, kind=spill_kind)
    ctx1_d = nc.dram_tensor("ctx1_spill", [D, S], BF16, kind=spill_kind)
    x2_d = nc.dram_tensor("x2_spill", [S, D], F32, kind=spill_kind)
    x2t_d = nc.dram_tensor("x2t_spill", [D, S], BF16, kind=spill_kind)
    qf2_d = nc.dram_tensor("qf2_spill", [D, S], BF16, kind=spill_kind)
    ctx2_d = nc.dram_tensor("ctx2_spill", [D, S], BF16, kind=spill_kind)

    P = Prog(nc)
    root = contextlib.ExitStack()

    def mk_alloc(stack, side=None):
        def sb(name, shape, dt):
            return stack.enter_context(nc.sbuf_tensor(name, shape, dt, side=side))

        def ps(name, shape, dt=F32):
            return stack.enter_context(nc.psum_tensor(name, shape, dt))
        return sb, ps

    sb0, ps0 = mk_alloc(root)

    ident = sb0("ident", [128, 128], BF16)
    identf = sb0("identf", [128, 128], F32)
    maskT = sb0("maskT", [128, 128], BF16)
    mask01 = sb0("mask01", [128, 128], BF16)
    ones_r512 = sb0("ones_r512", [1, 512], BF16)
    ones_c128 = sb0("ones_c128", [128, 1], BF16)
    ones_rf = sb0("ones_rf", [1, 128], F32)
    wstd_sb = sb0("wstd_sb", [1, H], F32)
    wrec_sb = sb0("wrec_sb", [1, H], F32)
    sstd_sb = sb0("sstd_sb", [1, H], F32)
    srec_sb = sb0("srec_sb", [1, H], F32)
    m01_sb = sb0("m01_sb", [1, H], F32)
    m01b = sb0("m01b", [128, H], F32)
    colscale = sb0("colscale", [128, 32], F32)


    for s in ("g", "sy", "scq", "mm", "ac", "ve", "gp", "so"):
        P.sem(s)


    kv1_stack = contextlib.ExitStack()
    sbKV1, _ = mk_alloc(kv1_stack)
    kt1 = sbKV1("kt1", [128, S], BF16)
    vtok1 = sbKV1("vtok1", [128, NJ * DH], BF16)
    bq_cols = sbKV1("bq_cols", [128, 18], F32)
    tok_bq = P.dmas("gpsimd", "ini0",
                    bq_cols[:, :],
                    bqkv_d[:, :].rearrange("o (c p) -> (o p) c", p=128),
                    allow_slow_non_contiguous=True)
    a_stack = contextlib.ExitStack()
    sbA, psA = mk_alloc(a_stack)
    xT = sbA("xT", [128, NCH * S], BF16)
    xrows_bf = [sbA(f"xrows_bf{i}", [128, D], BF16) for i in range(2)]
    wqrows = [sbA(f"wqrows{i}", [128, D], BF16) for i in range(3)]
    wqT = [sbA(f"wqT{i}", [128, NCH * 128], BF16) for i in range(3)]
    vstage1 = sbA("vstage1", [128, S], BF16)
    right_root = contextlib.ExitStack()
    sbR, _ = mk_alloc(right_root, side="right")
    qstage = [sbR(f"qstage{i}", [128, 512], BF16) for i in range(2)]

    tok_xrow = [None, None]
    tok_xtp = [None, None]
    xT_chunk_tok = [None] * NJ

    def emit_xT(j):
        b = j % 2
        tok_xrow[b] = P.dmas("gpsimd", f"xr{b}", xrows_bf[b][:, :], x_d[j * 128:(j + 1) * 128, :],
                             waits=[tok_xtp[b]])
        tok_xtp[b] = P.dmas(
            "sync", f"xt{b}",
            xT[:, :].rearrange("p (c s) -> p c s", c=NCH)[:, :, j * 128:(j + 1) * 128],
            xrows_bf[b][:, :].rearrange("p (c f) -> p c f", c=NCH),
            waits=[tok_xrow[b]], transpose=True)
        xT_chunk_tok[j] = tok_xtp[b]

    for j in range(4):
        emit_xT(j)

    t_ms = [None]

    def g_op(fn):
        tok = P.do("gpsimd", (lambda g, fn=fn: fn(g)), waits=[t_ms[0]], inc="gp")
        t_ms[0] = tok
        return tok

    g_op(lambda g: g.memset(ident[:, :], 0.0))
    g_op(lambda g: g.affine_select(out=ident[:, :], in_=ident[:, :], compare_op=ALU.not_equal,
                                   fill=1.0, base=0, pattern=[[-1, 128]], channel_multiplier=1))
    g_op(lambda g: g.memset(identf[:, :], 0.0))
    g_op(lambda g: g.affine_select(out=identf[:, :], in_=identf[:, :], compare_op=ALU.not_equal,
                                   fill=1.0, base=0, pattern=[[-1, 128]], channel_multiplier=1))
    g_op(lambda g: g.memset(maskT[:, :], 0.0))
    g_op(lambda g: g.affine_select(out=maskT[:, :], in_=maskT[:, :], compare_op=ALU.is_ge,
                                   fill=MASK_NEG, base=0, pattern=[[1, 128]], channel_multiplier=-1))
    g_op(lambda g: g.memset(mask01[:, :], 1.0))
    g_op(lambda g: g.affine_select(out=mask01[:, :], in_=mask01[:, :], compare_op=ALU.is_ge,
                                   fill=0.0, base=0, pattern=[[1, 128]], channel_multiplier=-1))
    g_op(lambda g: g.memset(ones_r512[:, :], 1.0))
    g_op(lambda g: g.memset(ones_c128[:, :], 1.0))
    tok_consts = g_op(lambda g: g.memset(ones_rf[:, :], 1.0))
    tok_ws = P.dmas("gpsimd", "ini2", wstd_sb[:, :], wstd_d[:, :])
    tok_wr = P.dmas("gpsimd", "ini3", wrec_sb[:, :], wrec_d[:, :])

    t1 = P.do("vector", lambda v: v.tensor_scalar_max(sstd_sb[:, :], wstd_sb[:, :], 1e-8),
              waits=[tok_ws], inc="ve")
    t2 = P.do("vector", lambda v: v.tensor_scalar_max(srec_sb[:, :], wrec_sb[:, :], 1e-8),
              waits=[tok_wr], inc="ve")
    tok_m01 = P.do("vector", lambda v: v.tensor_single_scalar(m01_sb[:, :], wrec_sb[:, :], 0.1, ALU.is_gt),
                   waits=[tok_wr], inc="ve")
    tok_ss = P.do("scalar", lambda s: s.activation(sstd_sb[:, :], sstd_sb[:, :], AF.Sqrt),
                  waits=[t1], inc="ac")
    tok_sr = P.do("scalar", lambda s: s.activation(srec_sb[:, :], srec_sb[:, :], AF.Sqrt),
                  waits=[t2], inc="ac")

    pro_stack = contextlib.ExitStack()
    sbP, psP = mk_alloc(pro_stack)
    u_row = sbP("u_row", [1, 128], F32)
    v_row = sbP("v_row", [1, 128], F32)
    sstd2 = sbP("sstd2", [1, 32], F32)
    srec2 = sbP("srec2", [1, 32], F32)
    cs_ps = psP("cs_ps", [128, 32])
    mb_ps = psP("mb_ps", [128, H])

    g_op(lambda g: g.memset(u_row[:, :], 0.0))
    g_op(lambda g: g.affine_select(out=u_row[:, :], in_=u_row[:, :], compare_op=ALU.is_ge,
                                   fill=1.0, base=-120, pattern=[[1, 128]], channel_multiplier=0))
    g_op(lambda g: g.memset(v_row[:, :], 1.0))
    tok_uv = g_op(lambda g: g.affine_select(out=v_row[:, :], in_=v_row[:, :], compare_op=ALU.is_ge,
                                            fill=0.0, base=-120, pattern=[[1, 128]], channel_multiplier=0))

    tuv3 = P.do("vector", lambda v: v.tensor_copy(sstd2[0:1, 0:16], sstd_sb[:, :]),
                waits=[tok_ss], inc="ve")
    tuv4 = P.do("vector", lambda v: v.tensor_copy(sstd2[0:1, 16:32], sstd_sb[:, :]),
                waits=[tok_ss], inc="ve")
    tuv5 = P.do("vector", lambda v: v.tensor_copy(srec2[0:1, 0:16], srec_sb[:, :]),
                waits=[tok_sr], inc="ve")
    tuv6 = P.do("vector", lambda v: v.tensor_copy(srec2[0:1, 16:32], srec_sb[:, :]),
                waits=[tok_sr], inc="ve")

    def build_colscale(t):
        t.matmul(cs_ps[:, :], u_row[:, :], sstd2[:, :], start=True, stop=False)
        return t.matmul(cs_ps[:, :], v_row[:, :], srec2[:, :], start=False, stop=True)

    tok_csm = P.do("tensor", build_colscale,
                   waits=[tuv3, tuv4, tuv5, tuv6, tok_consts, tok_uv], inc="mm")
    tok_cs1 = P.do("scalar", lambda s: s.mul(colscale[:, 0:16], cs_ps[:, 0:16], ISQ),
                   waits=[tok_csm], inc="ac")
    tok_cs2 = P.do("scalar", lambda s: s.copy(colscale[:, 16:32], cs_ps[:, 16:32]),
                   waits=[tok_csm], inc="ac")
    tok_mbm = P.do("tensor", lambda t: t.matmul(mb_ps[:, :], ones_rf[0:1, 0:128], m01_sb[:, :],
                                                start=True, stop=True),
                   waits=[tok_m01, tok_consts], inc="mm")
    tok_m01b = P.do("scalar", lambda s: s.copy(m01b[:, :], mb_ps[:, :]),
                    waits=[tok_mbm], inc="ac")
    colscale_ready = [tok_cs1, tok_cs2]
    pro_stack.close()

    proj_ps = [psA(f"proj_ps{i}", [128, 512]) for i in range(4)]

    ec_order = [16, 17] + list(range(16))
    tok_wqrow = [None, None, None]
    tok_wqtp = [None, None, None]
    tok_wq_used = [None, None, None]
    tok_evac = [None] * 4
    tok_qspill = [None, None]
    kt1_ready = []
    vtok1_ready = None
    q1_spill_tok = [None] * H

    for i, ec in enumerate(ec_order):
        b = i % 3
        tok_wqrow[b] = P.dmas("gpsimd", f"wqr{b}", wqrows[b][:, :], wqkv_d[ec * 128:(ec + 1) * 128, :],
                              waits=[tok_wqtp[b]])
        tok_wqtp[b] = P.dmas("sync", f"wqt{b}",
                             wqT[b][:, :].rearrange("p (c e) -> p c e", c=NCH),
                             wqrows[b][:, :].rearrange("p (c f) -> p c f", c=NCH),
                             waits=[tok_wqrow[b], tok_wq_used[b]], transpose=True)
        for w in range(NW):
            pb = (i * NW + w) % 4
            pst = proj_ps[pb]
            wq_b = wqT[b]

            def mmg(t, pst=pst, wq_b=wq_b, ec=ec, w=w):
                last = None
                for c in range(NCH):
                    last = t.matmul(pst[:, :], wq_b[:, c * 128:(c + 1) * 128],
                                    xT[:, c * S + w * 512: c * S + (w + 1) * 512],
                                    start=(c == 0), stop=(c == NCH - 1))
                return last
            tok_mm = P.do("tensor", mmg,
                          waits=[tok_wqtp[b], tok_evac[pb], tok_consts, tok_bq,
                                 tok_cs1, tok_cs2, tok_m01b,
                                 xT_chunk_tok[4 * w + 3], xT_chunk_tok[4 * w + 2]],
                          inc="mm")
            if ec == 16:
                tok_ev = P.do("vector", (lambda v, pst=pst, w=w: v.tensor_scalar(
                    kt1[:, w * 512:(w + 1) * 512], pst[:, :],
                    bq_cols[:, 16:17], None, op0=ALU.add)),
                    waits=[tok_mm], inc="ve")
                kt1_ready.append(tok_ev)
            elif ec == 17:
                tok_ev = P.do("vector", (lambda v, pst=pst, w=w: v.tensor_scalar(
                    vstage1[:, w * 512:(w + 1) * 512], pst[:, :],
                    bq_cols[:, 17:18], None, op0=ALU.add)),
                    waits=[tok_mm], inc="ve")
                if w == NW - 1:
                    vtok1_ready = P.dmas(
                        "sync", "v1t",
                        vtok1[:, :].rearrange("p (c f) -> p c f", c=NJ),
                        vstage1[:, :].rearrange("p (c s) -> p c s", c=NJ),
                        waits=[tok_ev, P.tok("ac")], transpose=True)
            else:
                qb = (i * NW + w) % 2
                tok_ev = P.do("vector", (lambda v, pst=pst, qb=qb, ec=ec: v.tensor_scalar(
                    qstage[qb][:, :], pst[:, :],
                    bq_cols[:, ec:ec + 1], ISQ, op0=ALU.add, op1=ALU.mult)),
                    waits=[tok_mm, tok_qspill[qb]], inc="ve")
                tok_qspill[qb] = P.dmas("sync", f"q1s{qb}",
                                        qf1_d[ec * 128:(ec + 1) * 128, w * 512:(w + 1) * 512],
                                        qstage[qb][:, :], waits=[tok_ev])
                q1_spill_tok[ec] = [tok_qspill[0], tok_qspill[1]]
            tok_evac[pb] = tok_ev
            tok_wq_used[b] = tok_mm
            if i == 0 and w < NW - 1:
                for jj in range(4 * (w + 1), 4 * (w + 1) + 4):
                    emit_xT(jj)

    P1A_done = [P.tok("mm"), P.tok("ac")]
    a_stack.close()
    P.set_fence()
    ow_stack = contextlib.ExitStack()
    sbW, _ = mk_alloc(ow_stack, side="right")
    outwT = sbW("outwT", [128, NCH * D], BF16)
    bo_bf = sbW("bo_bf", [1, D], BF16)
    tok_bo = P.dmas("gpsimd", "ini1", bo_bf[:, :], bout_d[:, :])
    worows = [sbW(f"worows{i}", [128, D], BF16) for i in range(2)]
    tok_worow = [None, None]
    tok_wotp = [None, None]
    for ecb in range(16):
        b = ecb % 2
        tok_worow[b] = P.dmas("gpsimd", f"wor{b}", worows[b][:, :], wout_d[ecb * 128:(ecb + 1) * 128, :],
                              waits=[tok_wotp[b]] + P1A_done)
        tok_wotp[b] = P.dmas(
            "sync", f"wot{b}",
            outwT[:, :].rearrange("p (c e) -> p c e", c=NCH)[:, :, ecb * 128:(ecb + 1) * 128],
            worows[b][:, :].rearrange("p (c f) -> p c f", c=NCH),
            waits=[tok_worow[b]], transpose=True)
    outwT_ready = [tok_wotp[0], tok_wotp[1]]

    def attn_block(tag, kf_slice, ksub_src, v_slice, qspill_d, ctxspill_d,
                   q_ready, kv_ready, att_stack):
        sbT, psT = mk_alloc(att_stack)
        qh = [sbT(f"{tag}qh{i}", [128, S], BF16) for i in range(2)]
        ksub = [sbT(f"{tag}ksub{i}", [128, S // SUB], BF16) for i in range(2)]
        qsub = [sbT(f"{tag}qsub{i}", [128, 128], BF16) for i in range(2)]
        mhat = [sbT(f"{tag}mhat{i}", [128, 1], F32) for i in range(2)]
        negmrow = [sbT(f"{tag}negmrow{i}", [1, 128], F32) for i in range(2)]
        mmin = [sbT(f"{tag}mmin{i}", [1, 1], F32) for i in range(2)]
        mbias = sbT(f"{tag}mbias", [128, 2], F32)
        ptile = [sbT(f"{tag}ptile{i}", [128, 512], BF16) for i in range(3)]
        recl = sbT(f"{tag}recl", [1, 512], F32)
        bcast_sb = sbT(f"{tag}bcast_sb", [128, 512], F32)
        ctx_sb = [sbT(f"{tag}ctx_sb{i}", [128, 512], BF16) for i in range(2)]
        s_ps = [psT(f"{tag}s_ps0", [128, 512]), psT(f"{tag}s_ps1", [128, 512]),
                psT(f"{tag}s_ps2", [128, 512])]
        l_ps = [psT(f"{tag}l_ps0", [1, 512]), psT(f"{tag}l_ps1", [1, 512])]
        c_ps = [psT(f"{tag}c_ps0", [128, 512]), psT(f"{tag}c_ps1", [128, 512])]
        bc_ps = psT(f"{tag}bc_ps", [128, 512])
        st_ps = bc_ps[:, 0:256]
        mt_view = s_ps[0][0:1, 0:128]

        tok_qh = [None, None]
        tok_qh_freed = [None, None]
        tok_ctxsp = [None, None]
        tok_ctx_tt = [None, None]
        tok_bcast_copy = None
        tok_stats_red = None
        tok_negm_evac = None
        tok_exp_bank = [None, None, None]
        tok_lpv_bank = [None, None, None]
        tok_recip_bank = [None, None]
        ctx_done = []
        pending_norm = []

        def flush_norm():
            while pending_norm:
                (lwb, cp, wb, hh, ww, tok_recip) = pending_norm.pop(0)
                tok_bcmm = P.do("tensor", (lambda t: t.matmul(
                    bc_ps[:, :], ones_rf[0:1, 0:128], recl[:, :], start=True, stop=True)),
                    waits=[tok_recip, tok_bcast_copy, P.tok("ac")], inc="mm")
                tbc = P.do("vector", (lambda v: v.tensor_copy(bcast_sb[:, :], bc_ps[:, :])),
                           waits=[tok_bcmm], inc="ve")
                ttt = P.do("vector", (lambda v, cp=cp, wb=wb: v.tensor_mul(
                    ctx_sb[wb][:, :], cp[:, :], bcast_sb[:, :])),
                    waits=[tbc, tok_ctxsp[wb]], inc="ve")
                tok_ctx_tt[wb] = ttt
                tok_ctxsp[wb] = P.dmas(
                    "sync", f"{tag}cs{wb}",
                    ctxspill_d[hh * 128:(hh + 1) * 128, ww * 512:(ww + 1) * 512],
                    ctx_sb[wb][:, :], waits=[ttt])
                ctx_done.append(tok_ctxsp[wb])
                return tbc
            return None

        def emit_mhat_chain(h):
            hb = h % 2
            st.chain_toks[hb] = {}
            tok_ksub = P.do("vector", (lambda v, h=h, hb=hb: v.tensor_copy(
                ksub[hb][:, :], ksub_src(h))),
                waits=kv_ready + [P.tok("mm")], inc="ve")
            tok_qsub = P.do("vector", (lambda v, hb=hb: v.tensor_copy(
                qsub[hb][:, :], qh[hb][:, 0:S:16])),
                waits=[tok_qh[hb], P.tok("mm")], inc="ve")
            tok_stats = P.do("tensor", (lambda t, hb=hb: t.matmul(
                st_ps[:, 0:S // SUB], qsub[hb][:, :], ksub[hb][:, :], start=True, stop=True)),
                waits=[tok_ksub, tok_qsub, st.tok_stats_red, st.tok_bcast_copy, P.tok("ac")],
                inc="mm")
            st.tok_stats_red = P.do("vector", (lambda v, hb=hb: v.reduce_max(
                mhat[hb][:, 0:1], st_ps[:, 0:S // SUB], axis=AX.X, negate=True)),
                waits=[tok_stats], inc="ve")
            extra_mt = [tok_exp_bank[0], tok_lpv_bank[0], tok_lpv_bank[1], tok_lpv_bank[2]]
            tok_mt = P.do("tensor", (lambda t, hb=hb: t.transpose(
                mt_view, mhat[hb][:, 0:1], identf[:, :])),
                waits=[st.tok_stats_red, st.tok_negm_evac] + extra_mt, inc="mm")
            st.tok_negm_evac = P.do("scalar", (lambda s, hb=hb: s.copy(
                negmrow[hb][0:1, 0:128], mt_view)),
                waits=[tok_mt], inc="ac")
            tok_mmin = P.do("vector", (lambda v, hb=hb: v.tensor_reduce(
                mmin[hb][:, :], negmrow[hb][0:1, 0:128], axis=AX.X, op=ALU.min)),
                waits=[st.tok_negm_evac], inc="ve")
            tok_mbm = P.do("tensor", (lambda t, hb=hb: t.matmul(
                bc_ps[0:128, 0:1], ones_rf[0:1, 0:128], mmin[hb][:, :], start=True, stop=True)),
                waits=[tok_mmin, st.tok_bcast_copy], inc="mm")
            st.tok_mbias[hb] = P.do("scalar", (lambda s, hb=hb: s.copy(
                mbias[:, hb:hb + 1], bc_ps[0:128, 0:1])),
                waits=[tok_mbm, st.tok_exp_done[hb]], inc="ac")

        class st:
            tok_stats_red = None
            tok_negm_evac = None
            tok_bcast_copy = None
            tok_mbias = [None, None]
            tok_exp_done = [None, None]
            chain_toks = [None, None]

        for h in range(H):
            hb = h % 2
            if h == 0:
                tok_qh[0] = P.dmas("scalar", f"{tag}qh0", qh[0][:, :], qspill_d[0:128, :],
                                   waits=q_ready(0))
                emit_mhat_chain(0)
            if h + 1 < H:
                nhb = (h + 1) % 2
                tok_qh[nhb] = P.dmas("scalar", f"{tag}qh{nhb}", qh[nhb][:, :],
                                     qspill_d[(h + 1) * 128:(h + 2) * 128, :],
                                     waits=[tok_qh_freed[nhb]] + q_ready(h + 1))
            tok_mbias = st.tok_mbias[hb]

            for w in range(NW):
                wb = w % 2
                lp = l_ps[wb]
                cp = c_ps[wb]
                kmax = 4 * w + 4
                tok_lpv = None
                for cc in range(kmax + 2):
                    if cc < kmax:
                        c = cc
                        sb_i = c % 3
                        sp = s_ps[sb_i]
                        q0 = max(w * 512, c * 128)
                        q1 = (w + 1) * 512
                        n = q1 - q0

                        def smm(t, sp=sp, c=c, w=w, q0=q0, q1=q1, h=h, hb=hb, n=n):
                            return t.matmul(sp[:, (q0 - w * 512):(q1 - w * 512)],
                                            kf_slice(h, c), qh[hb][:, q0:q1],
                                            start=True, stop=True)
                        tok_smm = P.do("tensor", smm,
                                       waits=[tok_mbias, tok_exp_bank[sb_i]] +
                                             ([st.tok_negm_evac] if sb_i == 0 else []),
                                       inc="mm")
                        tok_exp_bank[sb_i] = P.do(
                            "scalar", (lambda s, sp=sp, sb_i=sb_i, q0=q0, w=w, n=n, hb=hb:
                                       s.activation(ptile[sb_i][:, 0:n],
                                                    sp[:, (q0 - w * 512):(q0 - w * 512) + n],
                                                    AF.Exp, bias=mbias[:, hb:hb + 1])),
                            waits=[tok_smm, tok_lpv_bank[sb_i]], inc="ac")
                        if c >= 4 * w:
                            tok_exp_bank[sb_i] = P.do(
                                "vector", (lambda v, sb_i=sb_i: v.tensor_mul(
                                    ptile[sb_i][:, 0:128], ptile[sb_i][:, 0:128], mask01[:, :])),
                                waits=[tok_exp_bank[sb_i]], inc="ve")
                    if cc >= 2:
                        c = cc - 2
                        sb_i = c % 3
                        q0 = max(w * 512, c * 128)
                        q1 = (w + 1) * 512
                        n = q1 - q0

                        def lpv(t, lp=lp, cp=cp, sb_i=sb_i, c=c, q0=q0, w=w, n=n, h=h, kmax=kmax):
                            t.matmul(lp[0:1, (q0 - w * 512):(q0 - w * 512) + n],
                                     ones_c128[:, :], ptile[sb_i][:, 0:n],
                                     start=(c == 0), stop=(c == kmax - 1))
                            return t.matmul(cp[:, (q0 - w * 512):(q0 - w * 512) + n],
                                            v_slice(h, c), ptile[sb_i][:, 0:n],
                                            start=(c == 0), stop=(c == kmax - 1))
                        extra = [tok_ctx_tt[wb], tok_recip_bank[wb]] if c == 0 else []
                        tok_lpv = P.do("tensor", lpv,
                                       waits=[tok_exp_bank[sb_i]] + extra, inc="mm")
                        tok_lpv_bank[sb_i] = tok_lpv
                    if cc == 3 and pending_norm:
                        tok_bcast_copy = flush_norm()
                        st.tok_bcast_copy = tok_bcast_copy
                tok_recip_bank[wb] = P.do(
                    "vector", (lambda v, lp=lp: v.reciprocal(recl[:, :], lp[0:1, :])),
                    waits=[tok_lpv, tok_bcast_copy], inc="ve")
                pending_norm.append((wb, cp, wb, h, w, tok_recip_bank[wb]))
                if w == 0 and h + 1 < H:
                    emit_mhat_chain(h + 1)
            tok_qh_freed[hb] = P.tok("mm")
            st.tok_exp_done[hb] = P.tok("ac")
        tok_bcast_copy = flush_norm()
        return ctx_done[-2:]

    def outproj_ln(tag, ctxspill_d, resid_d, w2_slice, bias_row, dest_writer,
                   ctx_ready, w2_ready, resid_ready, c_stack):
        sbC, psC = mk_alloc(c_stack)
        NBUF = 3
        ctx_win = [sbC(f"{tag}ctx_win{i}", [128, H * 128], BF16) for i in range(NBUF)]
        x_win = [sbC(f"{tag}x_win{i}", [128, D], F32) for i in range(NBUF)]
        z_sb = sbC(f"{tag}z_sb", [128, D], F32)
        sq_scr = sbC(f"{tag}sq_scr", [128, 2048], BF16)
        zsum4 = sbC(f"{tag}zsum4", [128, 4], F32)
        zsq4 = sbC(f"{tag}zsq4", [128, 4], F32)
        ln_s = sbC(f"{tag}ln_s", [128, 1], F32)
        ln_q = sbC(f"{tag}ln_q", [128, 1], F32)
        invstd = sbC(f"{tag}invstd", [128, 1], F32)
        nbias = sbC(f"{tag}nbias", [128, 1], F32)
        y_rows = sbC(f"{tag}y_rows", [128, D], F32)
        op_ps = [[psC(f"{tag}op_ps{pp}_{ew}", [128, 512]) for ew in range(4)] for pp in range(2)]

        tok_cw = [None] * NBUF
        tok_xw = [None] * NBUF
        tok_opfree = [[None] * 4 for _ in range(2)]
        tok_win_used = [None] * NBUF
        tok_r1_prev = None
        tok_r2_prev = None
        tok_sq_prev = [None] * 4
        tok_aff_prev = None
        tok_y_free = None
        dones = []
        def emit_win_load(j):
            wn = j % NBUF
            tok_cw[wn] = P.dmas("scalar", f"{tag}cw{wn}",
                                ctx_win[wn][:, :].rearrange("p (h s) -> p h s", h=H),
                                ctxspill_d[:, j * 128:(j + 1) * 128].rearrange("(h p) s -> p h s", p=128),
                                waits=[tok_win_used[wn]] + ctx_ready)
            tok_xw[wn] = P.dmas("scalar", f"{tag}xw{wn}", x_win[wn][:, :],
                                resid_d[j * 128:(j + 1) * 128, :],
                                waits=[tok_win_used[wn]] + resid_ready)

        emit_win_load(0)
        emit_win_load(1)
        for j in range(NJ):
            jb = j % 2
            wn = j % NBUF
            tok_cw_j = tok_cw[wn]
            tok_xw_j = tok_xw[wn]
            if j + 2 < NJ:
                emit_win_load(j + 2)
            tok_z = None
            for ew in range(4):
                opp = op_ps[jb][ew]

                def opmm(t, opp=opp, wn=wn, ew=ew):
                    first = True
                    if bias_row is not None:
                        t.matmul(opp[:, :], ones_r512[0:1, 0:128],
                                 bias_row[0:1, ew * 512:(ew + 1) * 512],
                                 start=True, stop=False)
                        first = False
                    last = None
                    for hh in range(H):
                        last = t.matmul(opp[:, :], ctx_win[wn][:, hh * 128:(hh + 1) * 128],
                                        w2_slice(hh, ew),
                                        start=first, stop=(hh == H - 1))
                        first = False
                    return last
                w2r = w2_ready(ew) if callable(w2_ready) else w2_ready
                tok_opmm = P.do("tensor", opmm,
                                waits=[tok_cw_j, tok_opfree[jb][ew], tok_bo] + w2r,
                                inc="mm")
                tok_z = P.do("vector", (lambda v, opp=opp, ew=ew, wn=wn: v.scalar_tensor_tensor(
                    z_sb[:, ew * 512:(ew + 1) * 512], opp[:, :], 1.0,
                    x_win[wn][:, ew * 512:(ew + 1) * 512],
                    op0=ALU.mult, op1=ALU.add, accum_out=zsum4[:, ew:ew + 1])),
                    waits=[tok_opmm, tok_xw_j, tok_aff_prev, tok_r1_prev], inc="ve")
                tok_opfree[jb][ew] = tok_z
                if ew == 3:
                    tok_win_used[wn] = tok_z
                tok_sq_prev[ew] = P.do("scalar", (lambda s, ew=ew: s.activation(
                    sq_scr[:, ew * 512:(ew + 1) * 512], z_sb[:, ew * 512:(ew + 1) * 512],
                    AF.Square, accum_out=zsq4[:, ew:ew + 1])),
                    waits=[tok_z, tok_r2_prev, tok_sq_prev[ew]], inc="ac")
            tok_sq_all = P.tok("ac")
            tok_r1_prev = P.do("vector", lambda v: v.reduce_sum(ln_s[:, :], zsum4[:, :], axis=AX.X),
                               waits=[tok_z], inc="ve")
            tok_r2_prev = P.do("vector", lambda v: v.reduce_sum(ln_q[:, :], zsq4[:, :], axis=AX.X),
                               waits=[tok_sq_all], inc="ve")
            tm = P.do("vector", lambda v: v.tensor_scalar_mul(ln_s[:, :], ln_s[:, :], 1.0 / D),
                      waits=[tok_r1_prev], inc="ve")
            te = P.do("vector", lambda v: v.tensor_scalar_mul(ln_q[:, :], ln_q[:, :], 1.0 / D),
                      waits=[tok_r2_prev], inc="ve")
            tv = P.do("vector", lambda v: v.scalar_tensor_tensor(
                ln_q[:, :], ln_s[:, :], ln_s[:, :], ln_q[:, :],
                op0=ALU.mult, op1=ALU.subtract),
                waits=[tm, te], inc="ve")
            tv2 = P.do("vector", lambda v: v.tensor_scalar_add(ln_q[:, :], ln_q[:, :], -EPS),
                       waits=[tv], inc="ve")
            tv3 = P.do("vector", lambda v: v.reciprocal(ln_q[:, :], ln_q[:, :]),
                       waits=[tv2], inc="ve")
            tv4 = P.do("vector", lambda v: v.tensor_scalar_mul(ln_q[:, :], ln_q[:, :], -1.0),
                       waits=[tv3], inc="ve")
            ti = P.do("scalar", lambda s: s.activation(invstd[:, :], ln_q[:, :], AF.Sqrt),
                      waits=[tv4], inc="ac")
            tb = P.do("vector", lambda v: v.scalar_tensor_tensor(
                nbias[:, :], ln_s[:, :], -1.0, invstd[:, :],
                op0=ALU.mult, op1=ALU.mult),
                waits=[ti], inc="ve")
            tok_aff = P.do("vector", (lambda v: v.tensor_scalar(
                y_rows[:, :], z_sb[:, :], invstd[:, :], nbias[:, :],
                op0=ALU.mult, op1=ALU.add)),
                waits=[tb] + (tok_y_free or []), inc="ve")
            tok_y_free = dest_writer(j, tok_aff, y_rows)
            tok_aff_prev = tok_aff
            dones.append(tok_y_free[0])
        return dones

    att1_stack = contextlib.ExitStack()

    def kf1_slice(h, c):
        return kt1[:, c * 128:(c + 1) * 128]

    def ksub1_src(h):
        return kt1[:, 0:S:SUB]

    def v1_slice(h, c):
        return vtok1[:, c * DH:(c + 1) * DH]

    ctx1_done = attn_block("a1", kf1_slice, ksub1_src, v1_slice, qf1_d, ctx1_d,
                           q_ready=lambda h: list(q1_spill_tok[h] or []),
                           kv_ready=kt1_ready[-1:] + [vtok1_ready],
                           att_stack=att1_stack)
    att1_stack.close()
    kv1_stack.close()

    P.set_fence()
    qtail = sb0("qtail", [128, NCH * 128], BF16)
    ktail = sb0("ktail", [128, NCH * 128], BF16)
    ltmp = sb0("ltmp", [128, NCH * 8], F32)
    SW2 = 512
    NW2 = S // SW2
    lows_stack = contextlib.ExitStack()
    sbL, _ = mk_alloc(lows_stack)
    lows = sbL("lows", [128, NCH * 256], BF16)

    tok_lows = P.dmas("gpsimd", "low",
                      lows[:, :].rearrange("p (c f) -> p c f", c=NCH),
                      wa_d[:, 5888:6144].rearrange("(c p) f -> p c f", p=128))
    tok_tails = []
    prev_tail = [None, None]
    for h in range(H):
        qlo = lows[:, :].rearrange("p (c f) -> p c f", c=NCH)[:, :, h * 8:(h + 1) * 8]
        klo = lows[:, :].rearrange("p (c f) -> p c f", c=NCH)[:, :, 128 + h * 8:128 + (h + 1) * 8]
        qta = qtail[:, :].rearrange("p (c f) -> p c f", c=NCH)[:, :, h * 8:(h + 1) * 8]
        kta = ktail[:, :].rearrange("p (c f) -> p c f", c=NCH)[:, :, h * 8:(h + 1) * 8]
        lt = ltmp[:, :].rearrange("p (c f) -> p c f", c=NCH)
        td = P.do("vector", (lambda v, klo=klo, qlo=qlo, lt=lt: v.tensor_sub(lt, klo, qlo)),
                  waits=[tok_lows, tok_m01b] + prev_tail, inc="ve")
        tdm = P.do("vector", (lambda v, lt=lt, h=h: v.tensor_scalar_mul(lt, lt, m01b[:, h:h + 1])),
                   waits=[td], inc="ve")
        tq = P.do("vector", (lambda v, qta=qta, qlo=qlo, lt=lt: v.tensor_add(qta, qlo, lt)),
                  waits=[tdm], inc="ve")
        tk = P.do("vector", (lambda v, kta=kta, klo=klo, lt=lt: v.tensor_sub(kta, klo, lt)),
                  waits=[tdm], inc="ve")
        tok_tails += [tq, tk]
        prev_tail = [tq, tk]
    lows_stack.close()

    chunks = [("k", h) for h in range(H)] + [("v", h) for h in range(H)] + \
             [("q", h) for h in range(H)]
    ngroups = len(chunks) // GF
    tok_grp_used = [None, None]
    grp_weight_toks = [None] * ngroups

    def _unused_emit_group_weights(gi):
        gb = gi % 2
        grp = chunks[gi * GF:(gi + 1) * GF]
        kind = grp[0][0]
        h0 = grp[0][1]
        wg = wagrp[gb]
        wg4 = wg[:, :].rearrange("p (c g f) -> p c g f", c=NCH, g=GF)
        if kind == "v":
            tok_w = P.dmas("gpsimd", f"wag{gb}", wg4,
                           wa_d[:, 3840 + h0 * 128: 3840 + (h0 + GF) * 128]
                           .rearrange("(c p) (g f) -> p c g f", p=128, g=GF),
                           waits=[tok_grp_used[gb]] + tok_tails)
            grp_weight_toks[gi] = [tok_w]
        else:
            base = 1920 if kind == "k" else 0
            for g_in in range(GF):
                tok_w = P.dmas("gpsimd", f"wag{gb}", wg4[:, :, g_in, 0:120],
                               wa_d[:, base + (h0 + g_in) * 120: base + (h0 + g_in + 1) * 120]
                               .rearrange("(c p) f -> p c f", p=128),
                               waits=[tok_grp_used[gb]] + tok_tails)
            tailsrc = qtail if kind == "q" else ktail
            tok_tc = P.do("vector", (lambda v, wg4=wg4, tailsrc=tailsrc, h0=h0: v.tensor_copy(
                wg4[:, :, :, 120:128],
                tailsrc[:, :].rearrange("p (c f) -> p c f", c=NCH)[:, :, h0 * 8:(h0 + GF) * 8]
                .rearrange("p c (g f) -> p c g f", g=GF))),
                waits=[tok_w] + tok_tails + [tok_grp_used[gb]], inc="ve")
            grp_weight_toks[gi] = [tok_w, tok_tc]

    c1_stack = contextlib.ExitStack()
    sbC1, _ = mk_alloc(c1_stack)
    x2bf = [sbC1(f"x2bf{i}", [128, D], BF16) for i in range(2)]
    x2t_stage = [sbC1(f"x2t_stage{i}", [128, NCH * 128], BF16) for i in range(2)]

    def w2_slice_1(h, ew):
        return outwT[:, h * D + ew * 512: h * D + (ew + 1) * 512]

    x2_stored = []
    x2t_stored = []
    x2t_j_tok = [None] * NJ
    x2_j_tok = [None] * NJ
    _d1 = {"bf": [None, None], "tp": [None, None], "st2": [None, None]}

    def dest1(j, tok_y, y_rows):
        jb = j % 2
        tok_store = P.dmas("sync", "x2s", x2_d[j * 128:(j + 1) * 128, :], y_rows[:, :],
                           waits=[tok_y])
        tok_bf = P.do("scalar", (lambda s, jb=jb, y_rows=y_rows: s.copy(x2bf[jb][:, :], y_rows[:, :])),
                      waits=[tok_y, _d1["tp"][jb]], inc="ac")
        tok_tp = P.dmas("sync", f"x2t{jb}",
                        x2t_stage[jb][:, :].rearrange("p (c s) -> p c s", c=NCH),
                        x2bf[jb][:, :].rearrange("p (c f) -> p c f", c=NCH),
                        waits=[tok_bf, _d1["st2"][jb]], transpose=True)
        tok_st2 = P.dmas("sync", f"x2u{jb}",
                         x2t_d[:, j * 128:(j + 1) * 128].rearrange("(c p) s -> p c s", p=128),
                         x2t_stage[jb][:, :].rearrange("p (c s) -> p c s", c=NCH),
                         waits=[tok_tp])
        _d1["tp"][jb] = tok_tp
        _d1["st2"][jb] = tok_st2
        x2_stored.append(tok_store)
        x2t_stored.append(tok_st2)
        x2t_j_tok[j] = tok_st2
        x2_j_tok[j] = tok_store
        return [tok_store, tok_bf]

    P.set_fence()
    outproj_ln("c1", ctx1_d, x_d, w2_slice_1, bo_bf, dest1,
               ctx_ready=ctx1_done, w2_ready=outwT_ready, resid_ready=[],
               c_stack=c1_stack)
    x2_ready = [x2_stored[-1], P.tok("ac")]
    x2t_ready = [x2t_stored[-1]]
    c1_stack.close()
    ow_stack.close()

    kv2_stack = contextlib.ExitStack()
    sbKV2, _ = mk_alloc(kv2_stack)
    kf2 = sbKV2("kf2", [128, H * S], BF16)
    vtok2 = sbKV2("vtok2", [128, NJ * D], BF16)
    p2a_stack = contextlib.ExitStack()
    sb2, ps2 = mk_alloc(p2a_stack)
    proj2_ps = [ps2(f"proj2_ps{i}", [128, 512]) for i in range(4)]
    wagrp = [sb2(f"wagrp{i}", [128, NCH * GF * 128], BF16) for i in range(2)]
    x2t_win = [sb2(f"x2t_win{i}", [128, NCH * SW2], BF16) for i in range(2)]
    vstage2 = sb2("vstage2", [128, SW2], BF16)

    P.set_fence()

    def emit_group_weights(gi):
        return _unused_emit_group_weights(gi)

    emit_group_weights(0)
    emit_group_weights(1)
    tok_x2w = [None, None]
    tok_x2w_used = [None, None]
    tok_ev2 = [None] * 4
    tok_qsp2 = [None, None]
    tok_vtp2 = None
    kf2_ready = []
    vtok2_ready = []
    q2_spill_tok = [None] * H

    items = [(gi, w) for gi in range(ngroups) for w in range(NW2)]

    def emit_x2t_load(idx):
        gi, w = items[idx]
        wbx = idx % 2
        tok_x2w[wbx] = P.dmas("scalar", f"x2w{wbx}",
                              x2t_win[wbx][:, :].rearrange("p (c s) -> p c s", c=NCH),
                              x2t_d[:, w * SW2:(w + 1) * SW2].rearrange("(c p) s -> p c s", p=128),
                              waits=[tok_x2w_used[wbx], x2t_j_tok[4 * w + 3], x2t_j_tok[4 * w + 2]])

    emit_x2t_load(0)
    emit_x2t_load(1)
    for idx, (gi, w) in enumerate(items):
        gb = gi % 2
        wbx = idx % 2
        grp = chunks[gi * GF:(gi + 1) * GF]
        wg = wagrp[gb]
        tok_wready = grp_weight_toks[gi]
        tok_this_win = tok_x2w[wbx]
        for g_in, (knd, hh) in enumerate(grp):
            pb = (idx * GF + g_in) % 4
            pst = proj2_ps[pb]

            def mmg2(t, pst=pst, wg=wg, g_in=g_in, wbx=wbx):
                last = None
                for c in range(NCH):
                    last = t.matmul(pst[:, :],
                                    wg[:, c * (GF * 128) + g_in * 128: c * (GF * 128) + (g_in + 1) * 128],
                                    x2t_win[wbx][:, c * SW2:(c + 1) * SW2],
                                    start=(c == 0), stop=(c == NCH - 1))
                return last
            tok_mm2 = P.do("tensor", mmg2,
                           waits=tok_wready + [tok_this_win, tok_ev2[pb]], inc="mm")
            tok_x2w_used[wbx] = tok_mm2
            tok_grp_used[gb] = tok_mm2
            if knd == "k":
                tok_ev = P.do("scalar", (lambda s, pst=pst, hh=hh, w=w: s.activation(
                    kf2[:, hh * S + w * SW2: hh * S + (w + 1) * SW2], pst[:, :],
                    AF.Copy, scale=colscale[:, 16 + hh:17 + hh])),
                    waits=[tok_mm2] + colscale_ready, inc="ac")
                kf2_ready.append(tok_ev)
            elif knd == "q":
                qb2 = (idx * GF + g_in) % 2
                tok_ev = P.do("scalar", (lambda s, pst=pst, hh=hh, qb2=qb2: s.activation(
                    qstage[qb2][:, 0:SW2], pst[:, :], AF.Copy, scale=colscale[:, hh:hh + 1])),
                    waits=[tok_mm2, tok_qsp2[qb2]] + colscale_ready, inc="ac")
                tok_qsp2[qb2] = P.dmas("sync", f"q2s{qb2}",
                                       qf2_d[hh * 128:(hh + 1) * 128, w * SW2:(w + 1) * SW2],
                                       qstage[qb2][:, 0:SW2], waits=[tok_ev])
                if w == NW2 - 1:
                    q2_spill_tok[hh] = [tok_qsp2[0], tok_qsp2[1]]
            else:
                tok_ev = P.do("scalar", (lambda s, pst=pst: s.copy(vstage2[:, :], pst[:, :])),
                              waits=[tok_mm2, tok_vtp2], inc="ac")
                tok_vtp2 = P.dmas(
                    "sync", "v2t",
                    vtok2[:, :].rearrange("p (c f) -> p c f", c=NJ)[:, 4 * w:4 * w + 4,
                                                                    hh * 128:(hh + 1) * 128],
                    vstage2[:, :].rearrange("p (t s) -> p t s", t=4),
                    waits=[tok_ev], transpose=True)
                vtok2_ready.append(tok_vtp2)
            tok_ev2[pb] = tok_ev
        if idx + 2 < len(items):
            emit_x2t_load(idx + 2)
        if w == NW2 - 1 and gi + 2 < ngroups:
            emit_group_weights(gi + 2)

    p2a_stack.close()
    att2_stack = contextlib.ExitStack()

    def kf2_slice(h, c):
        return kf2[:, h * S + c * 128: h * S + (c + 1) * 128]

    def ksub2_src(h):
        return kf2[:, h * S: (h + 1) * S: SUB]

    def v2_slice(h, c):
        return vtok2[:, c * D + h * 128: c * D + (h + 1) * 128]

    P.set_fence()
    wp_a = sbR("wp_a", [128, NCH * 1024], BF16)
    wp_a_tok = [None, None]
    for ew in range(2):
        for c in range(NCH):
            wp_a_tok[ew] = P.dmas("gpsimd", f"wpa{ew}",
                                  wp_a[:, c * 1024 + ew * 512: c * 1024 + (ew + 1) * 512],
                                  wp_d[c * 128:(c + 1) * 128, ew * 512:(ew + 1) * 512])
    ctx2_done = attn_block("a2", kf2_slice, ksub2_src, v2_slice, qf2_d, ctx2_d,
                           q_ready=lambda h: list(q2_spill_tok[h] or []),
                           kv_ready=kf2_ready[-1:] + vtok2_ready[-1:],
                           att_stack=att2_stack)
    att2_stack.close()
    kv2_stack.close()

    c2_stack = contextlib.ExitStack()
    sbC2, _ = mk_alloc(c2_stack)
    wp_b = sbC2("wp_b", [128, NCH * 1024], BF16)
    P.set_fence()
    wp_ew_tok = [None] * 4
    wp_ew_tok[0] = wp_a_tok[0]
    wp_ew_tok[1] = wp_a_tok[1]
    for ew in range(2, 4):
        for c in range(NCH):
            wp_ew_tok[ew] = P.dmas("gpsimd", f"wpb{ew}",
                                   wp_b[:, c * 1024 + (ew - 2) * 512: c * 1024 + (ew - 1) * 512],
                                   wp_d[c * 128:(c + 1) * 128, ew * 512:(ew + 1) * 512])

    def w2_slice_2(h, ew):
        if ew < 2:
            return wp_a[:, h * 1024 + ew * 512: h * 1024 + (ew + 1) * 512]
        return wp_b[:, h * 1024 + (ew - 2) * 512: h * 1024 + (ew - 1) * 512]

    out_stored = []

    def dest2(j, tok_y, y_rows):
        tok_store = P.dmas("sync", "outs", out_d[j * 128:(j + 1) * 128, :], y_rows[:, :],
                           waits=[tok_y])
        out_stored.append(tok_store)
        return [tok_store]

    outproj_ln("c2", ctx2_d, x2_d, w2_slice_2, None, dest2,
               ctx_ready=ctx2_done, w2_ready=lambda ew: [wp_ew_tok[ew]],
               resid_ready=x2_ready, c_stack=c2_stack)

    P.do("sync", lambda e: None, waits=[out_stored[-1]])
    P.do("gpsimd", lambda e: None, waits=[out_stored[-1]])

    P.emit()
    c2_stack.close()
    root.close()
    return nc


_NC_CACHE = None


def kernel(L0_attn_x, L0_attn_Wqkv_w, L0_attn_Wqkv_b, L0_attn_out_w, L0_attn_out_b,
           L0_ffn_Wa, L0_ffn_Wp, L0_ffn_w_std, L0_ffn_w_rec):
    global _NC_CACHE
    if _NC_CACHE is None:
        _NC_CACHE = build_program()
    nc = _NC_CACHE

    x = np.ascontiguousarray(np.asarray(L0_attn_x, dtype=np.float32))
    base = {
        "wqkv": np.ascontiguousarray(np.asarray(L0_attn_Wqkv_w, dtype=np.float32)),
        "bqkv": np.ascontiguousarray(np.asarray(L0_attn_Wqkv_b, dtype=np.float32)).reshape(1, EQ),
        "wout": np.ascontiguousarray(np.asarray(L0_attn_out_w, dtype=np.float32)),
        "bout": np.ascontiguousarray(np.asarray(L0_attn_out_b, dtype=np.float32)).reshape(1, D),
        "wa": np.ascontiguousarray(np.asarray(L0_ffn_Wa, dtype=np.float32)),
        "wp": np.ascontiguousarray(np.asarray(L0_ffn_Wp, dtype=np.float32)),
        "wstd": np.ascontiguousarray(np.asarray(L0_ffn_w_std, dtype=np.float32)).reshape(1, H),
        "wrec": np.ascontiguousarray(np.asarray(L0_ffn_w_rec, dtype=np.float32)).reshape(1, H),
    }
    in_maps = []
    for b in range(8):
        m = dict(base)
        m["x"] = np.ascontiguousarray(x[b])
        in_maps.append(m)
    res = run_bass_kernel_spmd(nc, in_maps, core_ids=list(range(8)))
    out = np.stack([res.results[b]["out"] for b in range(8)], axis=0).astype(np.float32)
    return out


# version 47
# speedup vs baseline: 1.0378x; 1.0378x over previous
import math
import contextlib
import numpy as np
import concourse.bass as bass
import concourse.mybir as mybir
from concourse.bass_utils import run_bass_kernel_spmd

F32 = mybir.dt.float32
BF16 = mybir.dt.bfloat16
AF = mybir.ActivationFunctionType
ALU = mybir.AluOpType
AX = mybir.AxisListType

S = 2048
D = 2048
H = 16
DH = 128
EQ = 2304
FDIM = 6144
NCH = D // 128
NJ = S // 128
NW = S // 512
ISQ = 1.0 / math.sqrt(128.0)
SUB = 8
GF = 4
MASK_NEG = -1e30
EPS = 1e-5

ENGINES = ("tensor", "vector", "scalar", "gpsimd", "sync")
DEBUG_SPILLS = False


class Prog:

    def __init__(self, nc):
        self.nc = nc
        self.items = {e: [] for e in ENGINES}
        self.counts = {}
        self.fence = []

    def sem(self, name):
        self.counts.setdefault(name, 0)
        return name

    def tok(self, name):
        self.sem(name)
        return (name, self.counts[name])

    def set_fence(self):
        self.fence = [(n, v) for n, v in self.counts.items() if v > 0]

    def do(self, eng, fn, waits=(), inc=None, inc_by=1):
        flat = []
        for t in waits:
            if t is None:
                continue
            if isinstance(t, list):
                flat.extend(x for x in t if x is not None)
            else:
                flat.append(t)
        w = flat + list(self.fence)
        tokv = None
        if inc is not None:
            self.sem(inc)
            self.counts[inc] += inc_by
            tokv = (inc, self.counts[inc])
        self.items[eng].append((fn, w, inc, inc_by))
        return tokv

    def dma(self, eng, out_ap, in_ap, waits=(), inc=None, **kw):
        def fn(e, out_ap=out_ap, in_ap=in_ap, kw=kw):
            return e.dma_start(out_ap, in_ap, **kw)
        return self.do(eng, fn, waits=waits, inc=inc, inc_by=16)

    def dmas(self, eng, sem, out_ap, in_ap, waits=(), **kw):
        return self.dma(eng, out_ap, in_ap, waits=waits, inc=sem, **kw)

    def emit(self):
        nc = self.nc
        with contextlib.ExitStack() as es:
            handles = {n: es.enter_context(nc.semaphore(n)) for n in self.counts}
            with nc.Block() as block:
                for eng_name in ENGINES:
                    items = self.items[eng_name]
                    if not items:
                        continue

                    def body(eng, items=items):
                        waited = {}
                        for fn, waits, inc, inc_by in items:
                            for (sname, val) in waits:
                                if waited.get(sname, 0) < val:
                                    eng.wait_ge(handles[sname], val)
                                    waited[sname] = val
                            ins = fn(eng)
                            if inc is not None:
                                ins.then_inc(handles[inc], inc_by)
                    getattr(block, eng_name)(body)


def build_program():
    nc = bass.Bass("TRN2", target_bir_lowering=False)

    x_d = nc.dram_tensor("x", [S, D], F32, kind="ExternalInput")
    wqkv_d = nc.dram_tensor("wqkv", [EQ, D], F32, kind="ExternalInput")
    bqkv_d = nc.dram_tensor("bqkv", [1, EQ], F32, kind="ExternalInput")
    wout_d = nc.dram_tensor("wout", [D, D], F32, kind="ExternalInput")
    bout_d = nc.dram_tensor("bout", [1, D], F32, kind="ExternalInput")
    wa_d = nc.dram_tensor("wa", [D, FDIM], F32, kind="ExternalInput")
    wp_d = nc.dram_tensor("wp", [D, D], F32, kind="ExternalInput")
    wstd_d = nc.dram_tensor("wstd", [1, H], F32, kind="ExternalInput")
    wrec_d = nc.dram_tensor("wrec", [1, H], F32, kind="ExternalInput")
    out_d = nc.dram_tensor("out", [S, D], F32, kind="ExternalOutput")

    spill_kind = "ExternalOutput" if DEBUG_SPILLS else "Internal"
    qf1_d = nc.dram_tensor("qf1_spill", [D, S], BF16, kind=spill_kind)
    ctx1_d = nc.dram_tensor("ctx1_spill", [D, S], BF16, kind=spill_kind)
    x2_d = nc.dram_tensor("x2_spill", [S, D], F32, kind=spill_kind)
    x2t_d = nc.dram_tensor("x2t_spill", [D, S], BF16, kind=spill_kind)
    qf2_d = nc.dram_tensor("qf2_spill", [D, S], BF16, kind=spill_kind)
    ctx2_d = nc.dram_tensor("ctx2_spill", [D, S], BF16, kind=spill_kind)

    P = Prog(nc)
    root = contextlib.ExitStack()

    def mk_alloc(stack, side=None):
        def sb(name, shape, dt):
            return stack.enter_context(nc.sbuf_tensor(name, shape, dt, side=side))

        def ps(name, shape, dt=F32):
            return stack.enter_context(nc.psum_tensor(name, shape, dt))
        return sb, ps

    sb0, ps0 = mk_alloc(root)

    ident = sb0("ident", [128, 128], BF16)
    identf = sb0("identf", [128, 128], F32)
    maskT = sb0("maskT", [128, 128], BF16)
    mask01 = sb0("mask01", [128, 128], BF16)
    ones_r512 = sb0("ones_r512", [1, 512], BF16)
    ones_c128 = sb0("ones_c128", [128, 1], BF16)
    ones_rf = sb0("ones_rf", [1, 128], F32)
    wstd_sb = sb0("wstd_sb", [1, H], F32)
    wrec_sb = sb0("wrec_sb", [1, H], F32)
    sstd_sb = sb0("sstd_sb", [1, H], F32)
    srec_sb = sb0("srec_sb", [1, H], F32)
    m01_sb = sb0("m01_sb", [1, H], F32)
    m01b = sb0("m01b", [128, H], F32)
    colscale = sb0("colscale", [128, 32], F32)


    for s in ("g", "sy", "scq", "mm", "ac", "ve", "gp", "so"):
        P.sem(s)


    kv1_stack = contextlib.ExitStack()
    sbKV1, _ = mk_alloc(kv1_stack)
    kt1 = sbKV1("kt1", [128, S], BF16)
    vtok1 = sbKV1("vtok1", [128, NJ * DH], BF16)
    bq_cols = sbKV1("bq_cols", [128, 18], F32)
    tok_bq = P.dmas("gpsimd", "ini0",
                    bq_cols[:, :],
                    bqkv_d[:, :].rearrange("o (c p) -> (o p) c", p=128),
                    allow_slow_non_contiguous=True)
    a_stack = contextlib.ExitStack()
    sbA, psA = mk_alloc(a_stack)
    xT = sbA("xT", [128, NCH * S], BF16)
    xrows_bf = [sbA(f"xrows_bf{i}", [128, D], BF16) for i in range(2)]
    wqrows = [sbA(f"wqrows{i}", [128, D], BF16) for i in range(2)]
    wqT = [sbA(f"wqT{i}", [128, NCH * 128], BF16) for i in range(2)]
    vstage1 = sbA("vstage1", [128, S], BF16)
    right_root = contextlib.ExitStack()
    sbR, _ = mk_alloc(right_root, side="right")
    qstage = [sbR(f"qstage{i}", [128, 512], BF16) for i in range(2)]

    tok_xrow = [None, None]
    tok_xtp = [None, None]
    xT_chunk_tok = [None] * NJ

    def emit_xT(j):
        b = j % 2
        tok_xrow[b] = P.dmas("gpsimd", f"xr{b}", xrows_bf[b][:, :], x_d[j * 128:(j + 1) * 128, :],
                             waits=[tok_xtp[b]])
        tok_xtp[b] = P.dmas(
            "sync", f"xt{b}",
            xT[:, :].rearrange("p (c s) -> p c s", c=NCH)[:, :, j * 128:(j + 1) * 128],
            xrows_bf[b][:, :].rearrange("p (c f) -> p c f", c=NCH),
            waits=[tok_xrow[b]], transpose=True)
        xT_chunk_tok[j] = tok_xtp[b]

    for j in range(4):
        emit_xT(j)

    t_ms = [None]

    def g_op(fn):
        tok = P.do("gpsimd", (lambda g, fn=fn: fn(g)), waits=[t_ms[0]], inc="gp")
        t_ms[0] = tok
        return tok

    g_op(lambda g: g.memset(ident[:, :], 0.0))
    g_op(lambda g: g.affine_select(out=ident[:, :], in_=ident[:, :], compare_op=ALU.not_equal,
                                   fill=1.0, base=0, pattern=[[-1, 128]], channel_multiplier=1))
    g_op(lambda g: g.memset(identf[:, :], 0.0))
    g_op(lambda g: g.affine_select(out=identf[:, :], in_=identf[:, :], compare_op=ALU.not_equal,
                                   fill=1.0, base=0, pattern=[[-1, 128]], channel_multiplier=1))
    g_op(lambda g: g.memset(maskT[:, :], 0.0))
    g_op(lambda g: g.affine_select(out=maskT[:, :], in_=maskT[:, :], compare_op=ALU.is_ge,
                                   fill=MASK_NEG, base=0, pattern=[[1, 128]], channel_multiplier=-1))
    g_op(lambda g: g.memset(mask01[:, :], 1.0))
    g_op(lambda g: g.affine_select(out=mask01[:, :], in_=mask01[:, :], compare_op=ALU.is_ge,
                                   fill=0.0, base=0, pattern=[[1, 128]], channel_multiplier=-1))
    g_op(lambda g: g.memset(ones_r512[:, :], 1.0))
    g_op(lambda g: g.memset(ones_c128[:, :], 1.0))
    tok_consts = g_op(lambda g: g.memset(ones_rf[:, :], 1.0))
    tok_ws = P.dmas("gpsimd", "ini2", wstd_sb[:, :], wstd_d[:, :])
    tok_wr = P.dmas("gpsimd", "ini3", wrec_sb[:, :], wrec_d[:, :])

    t1 = P.do("vector", lambda v: v.tensor_scalar_max(sstd_sb[:, :], wstd_sb[:, :], 1e-8),
              waits=[tok_ws], inc="ve")
    t2 = P.do("vector", lambda v: v.tensor_scalar_max(srec_sb[:, :], wrec_sb[:, :], 1e-8),
              waits=[tok_wr], inc="ve")
    tok_m01 = P.do("vector", lambda v: v.tensor_single_scalar(m01_sb[:, :], wrec_sb[:, :], 0.1, ALU.is_gt),
                   waits=[tok_wr], inc="ve")
    tok_ss = P.do("scalar", lambda s: s.activation(sstd_sb[:, :], sstd_sb[:, :], AF.Sqrt),
                  waits=[t1], inc="ac")
    tok_sr = P.do("scalar", lambda s: s.activation(srec_sb[:, :], srec_sb[:, :], AF.Sqrt),
                  waits=[t2], inc="ac")

    pro_stack = contextlib.ExitStack()
    sbP, psP = mk_alloc(pro_stack)
    u_row = sbP("u_row", [1, 128], F32)
    v_row = sbP("v_row", [1, 128], F32)
    sstd2 = sbP("sstd2", [1, 32], F32)
    srec2 = sbP("srec2", [1, 32], F32)
    cs_ps = psP("cs_ps", [128, 32])
    mb_ps = psP("mb_ps", [128, H])

    g_op(lambda g: g.memset(u_row[:, :], 0.0))
    g_op(lambda g: g.affine_select(out=u_row[:, :], in_=u_row[:, :], compare_op=ALU.is_ge,
                                   fill=1.0, base=-120, pattern=[[1, 128]], channel_multiplier=0))
    g_op(lambda g: g.memset(v_row[:, :], 1.0))
    tok_uv = g_op(lambda g: g.affine_select(out=v_row[:, :], in_=v_row[:, :], compare_op=ALU.is_ge,
                                            fill=0.0, base=-120, pattern=[[1, 128]], channel_multiplier=0))

    tuv3 = P.do("vector", lambda v: v.tensor_copy(sstd2[0:1, 0:16], sstd_sb[:, :]),
                waits=[tok_ss], inc="ve")
    tuv4 = P.do("vector", lambda v: v.tensor_copy(sstd2[0:1, 16:32], sstd_sb[:, :]),
                waits=[tok_ss], inc="ve")
    tuv5 = P.do("vector", lambda v: v.tensor_copy(srec2[0:1, 0:16], srec_sb[:, :]),
                waits=[tok_sr], inc="ve")
    tuv6 = P.do("vector", lambda v: v.tensor_copy(srec2[0:1, 16:32], srec_sb[:, :]),
                waits=[tok_sr], inc="ve")

    def build_colscale(t):
        t.matmul(cs_ps[:, :], u_row[:, :], sstd2[:, :], start=True, stop=False)
        return t.matmul(cs_ps[:, :], v_row[:, :], srec2[:, :], start=False, stop=True)

    tok_csm = P.do("tensor", build_colscale,
                   waits=[tuv3, tuv4, tuv5, tuv6, tok_consts, tok_uv], inc="mm")
    tok_cs1 = P.do("scalar", lambda s: s.mul(colscale[:, 0:16], cs_ps[:, 0:16], ISQ),
                   waits=[tok_csm], inc="ac")
    tok_cs2 = P.do("scalar", lambda s: s.copy(colscale[:, 16:32], cs_ps[:, 16:32]),
                   waits=[tok_csm], inc="ac")
    tok_mbm = P.do("tensor", lambda t: t.matmul(mb_ps[:, :], ones_rf[0:1, 0:128], m01_sb[:, :],
                                                start=True, stop=True),
                   waits=[tok_m01, tok_consts], inc="mm")
    tok_m01b = P.do("scalar", lambda s: s.copy(m01b[:, :], mb_ps[:, :]),
                    waits=[tok_mbm], inc="ac")
    colscale_ready = [tok_cs1, tok_cs2]
    pro_stack.close()

    proj_ps = [psA(f"proj_ps{i}", [128, 512]) for i in range(4)]

    ec_order = [16, 17] + list(range(16))
    tok_wqrow = [None, None]
    tok_wqtp = [None, None]
    tok_wq_used = [None, None]
    tok_evac = [None] * 4
    tok_qspill = [None, None]
    kt1_ready = []
    vtok1_ready = None
    q1_spill_tok = [None] * H

    for i, ec in enumerate(ec_order):
        b = i % 2
        tok_wqrow[b] = P.dmas("gpsimd", f"wqr{b}", wqrows[b][:, :], wqkv_d[ec * 128:(ec + 1) * 128, :],
                              waits=[tok_wqtp[b]])
        tok_wqtp[b] = P.dmas("sync", f"wqt{b}",
                             wqT[b][:, :].rearrange("p (c e) -> p c e", c=NCH),
                             wqrows[b][:, :].rearrange("p (c f) -> p c f", c=NCH),
                             waits=[tok_wqrow[b], tok_wq_used[b]], transpose=True)
        for w in range(NW):
            pb = (i * NW + w) % 4
            pst = proj_ps[pb]
            wq_b = wqT[b]

            def mmg(t, pst=pst, wq_b=wq_b, ec=ec, w=w):
                last = None
                for c in range(NCH):
                    last = t.matmul(pst[:, :], wq_b[:, c * 128:(c + 1) * 128],
                                    xT[:, c * S + w * 512: c * S + (w + 1) * 512],
                                    start=(c == 0), stop=(c == NCH - 1))
                return last
            tok_mm = P.do("tensor", mmg,
                          waits=[tok_wqtp[b], tok_evac[pb], tok_consts, tok_bq,
                                 tok_cs1, tok_cs2, tok_m01b,
                                 xT_chunk_tok[4 * w + 3], xT_chunk_tok[4 * w + 2]],
                          inc="mm")
            if ec == 16:
                tok_ev = P.do("vector", (lambda v, pst=pst, w=w: v.tensor_scalar(
                    kt1[:, w * 512:(w + 1) * 512], pst[:, :],
                    bq_cols[:, 16:17], None, op0=ALU.add)),
                    waits=[tok_mm], inc="ve")
                kt1_ready.append(tok_ev)
            elif ec == 17:
                tok_ev = P.do("vector", (lambda v, pst=pst, w=w: v.tensor_scalar(
                    vstage1[:, w * 512:(w + 1) * 512], pst[:, :],
                    bq_cols[:, 17:18], None, op0=ALU.add)),
                    waits=[tok_mm], inc="ve")
                if w == NW - 1:
                    vtok1_ready = P.dmas(
                        "sync", "v1t",
                        vtok1[:, :].rearrange("p (c f) -> p c f", c=NJ),
                        vstage1[:, :].rearrange("p (c s) -> p c s", c=NJ),
                        waits=[tok_ev, P.tok("ac")], transpose=True)
            else:
                qb = (i * NW + w) % 2
                tok_ev = P.do("vector", (lambda v, pst=pst, qb=qb, ec=ec: v.tensor_scalar(
                    qstage[qb][:, :], pst[:, :],
                    bq_cols[:, ec:ec + 1], ISQ, op0=ALU.add, op1=ALU.mult)),
                    waits=[tok_mm, tok_qspill[qb]], inc="ve")
                tok_qspill[qb] = P.dmas("sync", f"q1s{qb}",
                                        qf1_d[ec * 128:(ec + 1) * 128, w * 512:(w + 1) * 512],
                                        qstage[qb][:, :], waits=[tok_ev])
                q1_spill_tok[ec] = [tok_qspill[0], tok_qspill[1]]
            tok_evac[pb] = tok_ev
            tok_wq_used[b] = tok_mm
            if i == 0 and w < NW - 1:
                for jj in range(4 * (w + 1), 4 * (w + 1) + 4):
                    emit_xT(jj)

    P1A_done = [P.tok("mm"), P.tok("ac")]
    a_stack.close()
    P.set_fence()
    ow_stack = contextlib.ExitStack()
    sbW, _ = mk_alloc(ow_stack, side="right")
    outwT = sbW("outwT", [128, NCH * D], BF16)
    bo_bf = sbW("bo_bf", [1, D], BF16)
    tok_bo = P.dmas("gpsimd", "ini1", bo_bf[:, :], bout_d[:, :])
    worows = [sbW(f"worows{i}", [128, D], BF16) for i in range(2)]
    tok_worow = [None, None]
    tok_wotp = [None, None]
    for ecb in range(16):
        b = ecb % 2
        tok_worow[b] = P.dmas("gpsimd", f"wor{b}", worows[b][:, :], wout_d[ecb * 128:(ecb + 1) * 128, :],
                              waits=[tok_wotp[b]] + P1A_done)
        tok_wotp[b] = P.dmas(
            "sync", f"wot{b}",
            outwT[:, :].rearrange("p (c e) -> p c e", c=NCH)[:, :, ecb * 128:(ecb + 1) * 128],
            worows[b][:, :].rearrange("p (c f) -> p c f", c=NCH),
            waits=[tok_worow[b]], transpose=True)
    outwT_ready = [tok_wotp[0], tok_wotp[1]]

    def attn_block(tag, kf_slice, ksub_src, v_slice, qspill_d, ctxspill_d,
                   q_ready, kv_ready, att_stack):
        sbT, psT = mk_alloc(att_stack)
        qh = [sbT(f"{tag}qh{i}", [128, S], BF16) for i in range(2)]
        ksub = [sbT(f"{tag}ksub{i}", [128, S // SUB], BF16) for i in range(2)]
        qsub = [sbT(f"{tag}qsub{i}", [128, 128], BF16) for i in range(2)]
        mhat = [sbT(f"{tag}mhat{i}", [128, 1], F32) for i in range(2)]
        negmrow = [sbT(f"{tag}negmrow{i}", [1, 128], F32) for i in range(2)]
        mmin = [sbT(f"{tag}mmin{i}", [1, 1], F32) for i in range(2)]
        mbias = sbT(f"{tag}mbias", [128, 2], F32)
        ptile = [sbT(f"{tag}ptile{i}", [128, 512], BF16) for i in range(3)]
        recl = sbT(f"{tag}recl", [1, 512], F32)
        bcast_sb = sbT(f"{tag}bcast_sb", [128, 512], F32)
        ctx_sb = [sbT(f"{tag}ctx_sb{i}", [128, 512], BF16) for i in range(2)]
        s_ps = [psT(f"{tag}s_ps0", [128, 512]), psT(f"{tag}s_ps1", [128, 512]),
                psT(f"{tag}s_ps2", [128, 512])]
        l_ps = [psT(f"{tag}l_ps0", [1, 512]), psT(f"{tag}l_ps1", [1, 512])]
        c_ps = [psT(f"{tag}c_ps0", [128, 512]), psT(f"{tag}c_ps1", [128, 512])]
        bc_ps = psT(f"{tag}bc_ps", [128, 512])
        st_ps = bc_ps[:, 0:256]
        mt_view = s_ps[0][0:1, 0:128]

        tok_qh = [None, None]
        tok_qh_freed = [None, None]
        tok_ctxsp = [None, None]
        tok_ctx_tt = [None, None]
        tok_bcast_copy = None
        tok_stats_red = None
        tok_negm_evac = None
        tok_exp_bank = [None, None, None]
        tok_lpv_bank = [None, None, None]
        tok_recip_bank = [None, None]
        ctx_done = []
        pending_norm = []

        def flush_norm():
            while pending_norm:
                (lwb, cp, wb, hh, ww, tok_recip) = pending_norm.pop(0)
                tok_bcmm = P.do("tensor", (lambda t: t.matmul(
                    bc_ps[:, :], ones_rf[0:1, 0:128], recl[:, :], start=True, stop=True)),
                    waits=[tok_recip, tok_bcast_copy, P.tok("ac")], inc="mm")
                tbc = P.do("vector", (lambda v: v.tensor_copy(bcast_sb[:, :], bc_ps[:, :])),
                           waits=[tok_bcmm], inc="ve")
                ttt = P.do("vector", (lambda v, cp=cp, wb=wb: v.tensor_mul(
                    ctx_sb[wb][:, :], cp[:, :], bcast_sb[:, :])),
                    waits=[tbc, tok_ctxsp[wb]], inc="ve")
                tok_ctx_tt[wb] = ttt
                tok_ctxsp[wb] = P.dmas(
                    "sync", f"{tag}cs{wb}",
                    ctxspill_d[hh * 128:(hh + 1) * 128, ww * 512:(ww + 1) * 512],
                    ctx_sb[wb][:, :], waits=[ttt])
                ctx_done.append(tok_ctxsp[wb])
                return tbc
            return None

        def emit_mhat_chain(h):
            hb = h % 2
            st.chain_toks[hb] = {}
            tok_ksub = P.do("vector", (lambda v, h=h, hb=hb: v.tensor_copy(
                ksub[hb][:, :], ksub_src(h))),
                waits=kv_ready + [P.tok("mm")], inc="ve")
            tok_qsub = P.do("vector", (lambda v, hb=hb: v.tensor_copy(
                qsub[hb][:, :], qh[hb][:, 0:S:16])),
                waits=[tok_qh[hb], P.tok("mm")], inc="ve")
            tok_stats = P.do("tensor", (lambda t, hb=hb: t.matmul(
                st_ps[:, 0:S // SUB], qsub[hb][:, :], ksub[hb][:, :], start=True, stop=True)),
                waits=[tok_ksub, tok_qsub, st.tok_stats_red, st.tok_bcast_copy, P.tok("ac")],
                inc="mm")
            st.tok_stats_red = P.do("vector", (lambda v, hb=hb: v.reduce_max(
                mhat[hb][:, 0:1], st_ps[:, 0:S // SUB], axis=AX.X, negate=True)),
                waits=[tok_stats], inc="ve")
            extra_mt = [tok_exp_bank[0], tok_lpv_bank[0], tok_lpv_bank[1], tok_lpv_bank[2]]
            tok_mt = P.do("tensor", (lambda t, hb=hb: t.transpose(
                mt_view, mhat[hb][:, 0:1], identf[:, :])),
                waits=[st.tok_stats_red, st.tok_negm_evac] + extra_mt, inc="mm")
            st.tok_negm_evac = P.do("scalar", (lambda s, hb=hb: s.copy(
                negmrow[hb][0:1, 0:128], mt_view)),
                waits=[tok_mt], inc="ac")
            tok_mmin = P.do("vector", (lambda v, hb=hb: v.tensor_reduce(
                mmin[hb][:, :], negmrow[hb][0:1, 0:128], axis=AX.X, op=ALU.min)),
                waits=[st.tok_negm_evac], inc="ve")
            tok_mbm = P.do("tensor", (lambda t, hb=hb: t.matmul(
                bc_ps[0:128, 0:1], ones_rf[0:1, 0:128], mmin[hb][:, :], start=True, stop=True)),
                waits=[tok_mmin, st.tok_bcast_copy], inc="mm")
            st.tok_mbias[hb] = P.do("scalar", (lambda s, hb=hb: s.copy(
                mbias[:, hb:hb + 1], bc_ps[0:128, 0:1])),
                waits=[tok_mbm, st.tok_exp_done[hb]], inc="ac")

        class st:
            tok_stats_red = None
            tok_negm_evac = None
            tok_bcast_copy = None
            tok_mbias = [None, None]
            tok_exp_done = [None, None]
            chain_toks = [None, None]

        for h in range(H):
            hb = h % 2
            if h == 0:
                tok_qh[0] = P.dmas("scalar", f"{tag}qh0", qh[0][:, :], qspill_d[0:128, :],
                                   waits=q_ready(0))
                emit_mhat_chain(0)
            if h + 1 < H:
                nhb = (h + 1) % 2
                tok_qh[nhb] = P.dmas("scalar", f"{tag}qh{nhb}", qh[nhb][:, :],
                                     qspill_d[(h + 1) * 128:(h + 2) * 128, :],
                                     waits=[tok_qh_freed[nhb]] + q_ready(h + 1))
            tok_mbias = st.tok_mbias[hb]

            for w in range(NW):
                wb = w % 2
                lp = l_ps[wb]
                cp = c_ps[wb]
                kmax = 4 * w + 4
                tok_lpv = None
                for cc in range(kmax + 2):
                    if cc < kmax:
                        c = cc
                        sb_i = c % 3
                        sp = s_ps[sb_i]
                        q0 = max(w * 512, c * 128)
                        q1 = (w + 1) * 512
                        n = q1 - q0

                        def smm(t, sp=sp, c=c, w=w, q0=q0, q1=q1, h=h, hb=hb, n=n):
                            return t.matmul(sp[:, (q0 - w * 512):(q1 - w * 512)],
                                            kf_slice(h, c), qh[hb][:, q0:q1],
                                            start=True, stop=True)
                        tok_smm = P.do("tensor", smm,
                                       waits=[tok_mbias, tok_exp_bank[sb_i]] +
                                             ([st.tok_negm_evac] if sb_i == 0 else []),
                                       inc="mm")
                        tok_exp_bank[sb_i] = P.do(
                            "scalar", (lambda s, sp=sp, sb_i=sb_i, q0=q0, w=w, n=n, hb=hb:
                                       s.activation(ptile[sb_i][:, 0:n],
                                                    sp[:, (q0 - w * 512):(q0 - w * 512) + n],
                                                    AF.Exp, bias=mbias[:, hb:hb + 1])),
                            waits=[tok_smm, tok_lpv_bank[sb_i]], inc="ac")
                        if c >= 4 * w:
                            tok_exp_bank[sb_i] = P.do(
                                "vector", (lambda v, sb_i=sb_i: v.tensor_mul(
                                    ptile[sb_i][:, 0:128], ptile[sb_i][:, 0:128], mask01[:, :])),
                                waits=[tok_exp_bank[sb_i]], inc="ve")
                    if cc >= 2:
                        c = cc - 2
                        sb_i = c % 3
                        q0 = max(w * 512, c * 128)
                        q1 = (w + 1) * 512
                        n = q1 - q0

                        def lpv(t, lp=lp, cp=cp, sb_i=sb_i, c=c, q0=q0, w=w, n=n, h=h, kmax=kmax):
                            t.matmul(lp[0:1, (q0 - w * 512):(q0 - w * 512) + n],
                                     ones_c128[:, :], ptile[sb_i][:, 0:n],
                                     start=(c == 0), stop=(c == kmax - 1))
                            return t.matmul(cp[:, (q0 - w * 512):(q0 - w * 512) + n],
                                            v_slice(h, c), ptile[sb_i][:, 0:n],
                                            start=(c == 0), stop=(c == kmax - 1))
                        extra = [tok_ctx_tt[wb], tok_recip_bank[wb]] if c == 0 else []
                        tok_lpv = P.do("tensor", lpv,
                                       waits=[tok_exp_bank[sb_i]] + extra, inc="mm")
                        tok_lpv_bank[sb_i] = tok_lpv
                    if cc == (5 if w == 0 else 7) and pending_norm:
                        tok_bcast_copy = flush_norm()
                        st.tok_bcast_copy = tok_bcast_copy
                tok_recip_bank[wb] = P.do(
                    "vector", (lambda v, lp=lp: v.reciprocal(recl[:, :], lp[0:1, :])),
                    waits=[tok_lpv, tok_bcast_copy], inc="ve")
                pending_norm.append((wb, cp, wb, h, w, tok_recip_bank[wb]))
                if w == 0 and h + 1 < H:
                    emit_mhat_chain(h + 1)
            tok_qh_freed[hb] = P.tok("mm")
            st.tok_exp_done[hb] = P.tok("ac")
        tok_bcast_copy = flush_norm()
        return ctx_done[-2:]

    def outproj_ln(tag, ctxspill_d, resid_d, w2_slice, bias_row, dest_writer,
                   ctx_ready, w2_ready, resid_ready, c_stack):
        sbC, psC = mk_alloc(c_stack)
        NBUF = 3
        ctx_win = [sbC(f"{tag}ctx_win{i}", [128, H * 128], BF16) for i in range(NBUF)]
        x_win = [sbC(f"{tag}x_win{i}", [128, D], F32) for i in range(NBUF)]
        z_sb = sbC(f"{tag}z_sb", [128, D], F32)
        sq_scr = sbC(f"{tag}sq_scr", [128, 2048], BF16)
        zsum4 = sbC(f"{tag}zsum4", [128, 4], F32)
        zsq4 = sbC(f"{tag}zsq4", [128, 4], F32)
        ln_s = sbC(f"{tag}ln_s", [128, 1], F32)
        ln_q = sbC(f"{tag}ln_q", [128, 1], F32)
        invstd = sbC(f"{tag}invstd", [128, 1], F32)
        nbias = sbC(f"{tag}nbias", [128, 1], F32)
        y_rows = sbC(f"{tag}y_rows", [128, D], F32)
        op_ps = [[psC(f"{tag}op_ps{pp}_{ew}", [128, 512]) for ew in range(4)] for pp in range(2)]

        tok_cw = [None] * NBUF
        tok_xw = [None] * NBUF
        tok_opfree = [[None] * 4 for _ in range(2)]
        tok_win_used = [None] * NBUF
        tok_r1_prev = None
        tok_r2_prev = None
        tok_sq_prev = [None] * 4
        tok_aff_prev = None
        tok_y_free = None
        dones = []
        def emit_win_load(j):
            wn = j % NBUF
            tok_cw[wn] = P.dmas("scalar", f"{tag}cw{wn}",
                                ctx_win[wn][:, :].rearrange("p (h s) -> p h s", h=H),
                                ctxspill_d[:, j * 128:(j + 1) * 128].rearrange("(h p) s -> p h s", p=128),
                                waits=[tok_win_used[wn]] + ctx_ready)
            tok_xw[wn] = P.dmas("scalar", f"{tag}xw{wn}", x_win[wn][:, :],
                                resid_d[j * 128:(j + 1) * 128, :],
                                waits=[tok_win_used[wn]] + resid_ready)

        emit_win_load(0)
        emit_win_load(1)
        for j in range(NJ):
            jb = j % 2
            wn = j % NBUF
            tok_cw_j = tok_cw[wn]
            tok_xw_j = tok_xw[wn]
            if j + 2 < NJ:
                emit_win_load(j + 2)
            tok_z = None
            for ew in range(4):
                opp = op_ps[jb][ew]

                def opmm(t, opp=opp, wn=wn, ew=ew):
                    first = True
                    if bias_row is not None:
                        t.matmul(opp[:, :], ones_r512[0:1, 0:128],
                                 bias_row[0:1, ew * 512:(ew + 1) * 512],
                                 start=True, stop=False)
                        first = False
                    last = None
                    for hh in range(H):
                        last = t.matmul(opp[:, :], ctx_win[wn][:, hh * 128:(hh + 1) * 128],
                                        w2_slice(hh, ew),
                                        start=first, stop=(hh == H - 1))
                        first = False
                    return last
                w2r = w2_ready(ew) if callable(w2_ready) else w2_ready
                tok_opmm = P.do("tensor", opmm,
                                waits=[tok_cw_j, tok_opfree[jb][ew], tok_bo] + w2r,
                                inc="mm")
                tok_z = P.do("vector", (lambda v, opp=opp, ew=ew, wn=wn: v.scalar_tensor_tensor(
                    z_sb[:, ew * 512:(ew + 1) * 512], opp[:, :], 1.0,
                    x_win[wn][:, ew * 512:(ew + 1) * 512],
                    op0=ALU.mult, op1=ALU.add, accum_out=zsum4[:, ew:ew + 1])),
                    waits=[tok_opmm, tok_xw_j, tok_aff_prev, tok_r1_prev], inc="ve")
                tok_opfree[jb][ew] = tok_z
                if ew == 3:
                    tok_win_used[wn] = tok_z
                tok_sq_prev[ew] = P.do("scalar", (lambda s, ew=ew: s.activation(
                    sq_scr[:, ew * 512:(ew + 1) * 512], z_sb[:, ew * 512:(ew + 1) * 512],
                    AF.Square, accum_out=zsq4[:, ew:ew + 1])),
                    waits=[tok_z, tok_r2_prev, tok_sq_prev[ew]], inc="ac")
            tok_sq_all = P.tok("ac")
            tok_r1_prev = P.do("vector", lambda v: v.reduce_sum(ln_s[:, :], zsum4[:, :], axis=AX.X),
                               waits=[tok_z], inc="ve")
            tok_r2_prev = P.do("vector", lambda v: v.reduce_sum(ln_q[:, :], zsq4[:, :], axis=AX.X),
                               waits=[tok_sq_all], inc="ve")
            tm = P.do("vector", lambda v: v.tensor_scalar_mul(ln_s[:, :], ln_s[:, :], 1.0 / D),
                      waits=[tok_r1_prev], inc="ve")
            te = P.do("vector", lambda v: v.tensor_scalar_mul(ln_q[:, :], ln_q[:, :], 1.0 / D),
                      waits=[tok_r2_prev], inc="ve")
            tv = P.do("vector", lambda v: v.scalar_tensor_tensor(
                ln_q[:, :], ln_s[:, :], ln_s[:, :], ln_q[:, :],
                op0=ALU.mult, op1=ALU.subtract),
                waits=[tm, te], inc="ve")
            tv2 = P.do("vector", lambda v: v.tensor_scalar_add(ln_q[:, :], ln_q[:, :], -EPS),
                       waits=[tv], inc="ve")
            tv3 = P.do("vector", lambda v: v.reciprocal(ln_q[:, :], ln_q[:, :]),
                       waits=[tv2], inc="ve")
            tv4 = P.do("vector", lambda v: v.tensor_scalar_mul(ln_q[:, :], ln_q[:, :], -1.0),
                       waits=[tv3], inc="ve")
            ti = P.do("scalar", lambda s: s.activation(invstd[:, :], ln_q[:, :], AF.Sqrt),
                      waits=[tv4], inc="ac")
            tb = P.do("vector", lambda v: v.scalar_tensor_tensor(
                nbias[:, :], ln_s[:, :], -1.0, invstd[:, :],
                op0=ALU.mult, op1=ALU.mult),
                waits=[ti], inc="ve")
            tok_aff = P.do("vector", (lambda v: v.tensor_scalar(
                y_rows[:, :], z_sb[:, :], invstd[:, :], nbias[:, :],
                op0=ALU.mult, op1=ALU.add)),
                waits=[tb] + (tok_y_free or []), inc="ve")
            tok_y_free = dest_writer(j, tok_aff, y_rows)
            tok_aff_prev = tok_aff
            dones.append(tok_y_free[0])
        return dones

    att1_stack = contextlib.ExitStack()

    def kf1_slice(h, c):
        return kt1[:, c * 128:(c + 1) * 128]

    def ksub1_src(h):
        return kt1[:, 0:S:SUB]

    def v1_slice(h, c):
        return vtok1[:, c * DH:(c + 1) * DH]

    ctx1_done = attn_block("a1", kf1_slice, ksub1_src, v1_slice, qf1_d, ctx1_d,
                           q_ready=lambda h: list(q1_spill_tok[h] or []),
                           kv_ready=kt1_ready[-1:] + [vtok1_ready],
                           att_stack=att1_stack)
    att1_stack.close()
    kv1_stack.close()

    c1_stack = contextlib.ExitStack()
    sbC1, _ = mk_alloc(c1_stack)
    x2bf = [sbC1(f"x2bf{i}", [128, D], BF16) for i in range(2)]
    x2t_stage = [sbC1(f"x2t_stage{i}", [128, NCH * 128], BF16) for i in range(2)]

    def w2_slice_1(h, ew):
        return outwT[:, h * D + ew * 512: h * D + (ew + 1) * 512]

    x2_stored = []
    x2t_stored = []
    x2t_j_tok = [None] * NJ
    x2_j_tok = [None] * NJ
    _d1 = {"bf": [None, None], "tp": [None, None], "st2": [None, None]}

    def dest1(j, tok_y, y_rows):
        jb = j % 2
        tok_store = P.dmas("sync", "x2s", x2_d[j * 128:(j + 1) * 128, :], y_rows[:, :],
                           waits=[tok_y])
        tok_bf = P.do("scalar", (lambda s, jb=jb, y_rows=y_rows: s.copy(x2bf[jb][:, :], y_rows[:, :])),
                      waits=[tok_y, _d1["tp"][jb]], inc="ac")
        tok_tp = P.dmas("sync", f"x2t{jb}",
                        x2t_stage[jb][:, :].rearrange("p (c s) -> p c s", c=NCH),
                        x2bf[jb][:, :].rearrange("p (c f) -> p c f", c=NCH),
                        waits=[tok_bf, _d1["st2"][jb]], transpose=True)
        tok_st2 = P.dmas("sync", f"x2u{jb}",
                         x2t_d[:, j * 128:(j + 1) * 128].rearrange("(c p) s -> p c s", p=128),
                         x2t_stage[jb][:, :].rearrange("p (c s) -> p c s", c=NCH),
                         waits=[tok_tp])
        _d1["tp"][jb] = tok_tp
        _d1["st2"][jb] = tok_st2
        x2_stored.append(tok_store)
        x2t_stored.append(tok_st2)
        x2t_j_tok[j] = tok_st2
        x2_j_tok[j] = tok_store
        return [tok_store, tok_bf]

    P.set_fence()
    outproj_ln("c1", ctx1_d, x_d, w2_slice_1, bo_bf, dest1,
               ctx_ready=ctx1_done, w2_ready=outwT_ready, resid_ready=[],
               c_stack=c1_stack)
    x2_ready = [x2_stored[-1], P.tok("ac")]
    x2t_ready = [x2t_stored[-1]]
    c1_stack.close()
    ow_stack.close()

    p2a_stack = contextlib.ExitStack()
    sb2, ps2 = mk_alloc(p2a_stack)
    kv2_stack = contextlib.ExitStack()
    sbKV2, _ = mk_alloc(kv2_stack)
    kf2 = sbKV2("kf2", [128, H * S], BF16)
    vtok2 = sbKV2("vtok2", [128, NJ * D], BF16)
    qtail = sb2("qtail", [128, NCH * 128], BF16)
    ktail = sb2("ktail", [128, NCH * 128], BF16)
    ltmp = sb2("ltmp", [128, NCH * 8], F32)
    lows_stack = contextlib.ExitStack()
    sbL, _ = mk_alloc(lows_stack)
    lows = sbL("lows", [128, NCH * 256], BF16)
    proj2_ps = [ps2(f"proj2_ps{i}", [128, 512]) for i in range(4)]

    P.set_fence()
    tok_lows = P.dmas("gpsimd", "low",
                      lows[:, :].rearrange("p (c f) -> p c f", c=NCH),
                      wa_d[:, 5888:6144].rearrange("(c p) f -> p c f", p=128))
    tok_tails = []
    prev_tail = [None, None]
    for h in range(H):
        qlo = lows[:, :].rearrange("p (c f) -> p c f", c=NCH)[:, :, h * 8:(h + 1) * 8]
        klo = lows[:, :].rearrange("p (c f) -> p c f", c=NCH)[:, :, 128 + h * 8:128 + (h + 1) * 8]
        qta = qtail[:, :].rearrange("p (c f) -> p c f", c=NCH)[:, :, h * 8:(h + 1) * 8]
        kta = ktail[:, :].rearrange("p (c f) -> p c f", c=NCH)[:, :, h * 8:(h + 1) * 8]
        lt = ltmp[:, :].rearrange("p (c f) -> p c f", c=NCH)
        td = P.do("vector", (lambda v, klo=klo, qlo=qlo, lt=lt: v.tensor_sub(lt, klo, qlo)),
                  waits=[tok_lows, tok_m01b] + prev_tail, inc="ve")
        tdm = P.do("vector", (lambda v, lt=lt, h=h: v.tensor_scalar_mul(lt, lt, m01b[:, h:h + 1])),
                   waits=[td], inc="ve")
        tq = P.do("vector", (lambda v, qta=qta, qlo=qlo, lt=lt: v.tensor_add(qta, qlo, lt)),
                  waits=[tdm], inc="ve")
        tk = P.do("vector", (lambda v, kta=kta, klo=klo, lt=lt: v.tensor_sub(kta, klo, lt)),
                  waits=[tdm], inc="ve")
        tok_tails += [tq, tk]
        prev_tail = [tq, tk]

    lows_stack.close()
    wagrp = [sb2(f"wagrp{i}", [128, NCH * GF * 128], BF16) for i in range(2)]
    SW2 = 512
    NW2 = S // SW2
    x2t_win = [sb2(f"x2t_win{i}", [128, NCH * SW2], BF16) for i in range(2)]
    vstage2 = sb2("vstage2", [128, SW2], BF16)

    chunks = [("k", h) for h in range(H)] + [("v", h) for h in range(H)] + \
             [("q", h) for h in range(H)]
    ngroups = len(chunks) // GF
    tok_wagrp = [None, None]
    tok_grp_used = [None, None]
    tok_x2w = [None, None]
    tok_x2w_used = [None, None]
    tok_ev2 = [None] * 4
    tok_qsp2 = [None, None]
    tok_vtp2 = None
    kf2_ready = []
    vtok2_ready = []
    q2_spill_tok = [None] * H

    items = [(gi, w) for gi in range(ngroups) for w in range(NW2)]

    def emit_x2t_load(idx):
        gi, w = items[idx]
        wbx = idx % 2
        tok_x2w[wbx] = P.dmas("scalar", f"x2w{wbx}",
                              x2t_win[wbx][:, :].rearrange("p (c s) -> p c s", c=NCH),
                              x2t_d[:, w * SW2:(w + 1) * SW2].rearrange("(c p) s -> p c s", p=128),
                              waits=[tok_x2w_used[wbx], x2t_j_tok[4 * w + 3], x2t_j_tok[4 * w + 2]])

    grp_weight_toks = [None] * ngroups

    def emit_group_weights(gi):
        gb = gi % 2
        grp = chunks[gi * GF:(gi + 1) * GF]
        kind = grp[0][0]
        h0 = grp[0][1]
        wg = wagrp[gb]
        wg4 = wg[:, :].rearrange("p (c g f) -> p c g f", c=NCH, g=GF)
        if kind == "v":
            tok_w = P.dmas("gpsimd", f"wag{gb}", wg4,
                           wa_d[:, 3840 + h0 * 128: 3840 + (h0 + GF) * 128]
                           .rearrange("(c p) (g f) -> p c g f", p=128, g=GF),
                           waits=[tok_grp_used[gb]] + tok_tails)
            grp_weight_toks[gi] = [tok_w]
        else:
            base = 1920 if kind == "k" else 0
            for g_in in range(GF):
                tok_w = P.dmas("gpsimd", f"wag{gb}", wg4[:, :, g_in, 0:120],
                               wa_d[:, base + (h0 + g_in) * 120: base + (h0 + g_in + 1) * 120]
                               .rearrange("(c p) f -> p c f", p=128),
                               waits=[tok_grp_used[gb]] + tok_tails)
            tailsrc = qtail if kind == "q" else ktail
            tok_tc = P.do("vector", (lambda v, wg4=wg4, tailsrc=tailsrc, h0=h0: v.tensor_copy(
                wg4[:, :, :, 120:128],
                tailsrc[:, :].rearrange("p (c f) -> p c f", c=NCH)[:, :, h0 * 8:(h0 + GF) * 8]
                .rearrange("p c (g f) -> p c g f", g=GF))),
                waits=[tok_w] + tok_tails + [tok_grp_used[gb]], inc="ve")
            grp_weight_toks[gi] = [tok_w, tok_tc]

    emit_group_weights(0)
    emit_group_weights(1)
    emit_x2t_load(0)
    emit_x2t_load(1)
    for idx, (gi, w) in enumerate(items):
        gb = gi % 2
        wbx = idx % 2
        grp = chunks[gi * GF:(gi + 1) * GF]
        wg = wagrp[gb]
        tok_wready = grp_weight_toks[gi]
        tok_this_win = tok_x2w[wbx]
        for g_in, (knd, hh) in enumerate(grp):
            pb = (idx * GF + g_in) % 4
            pst = proj2_ps[pb]

            def mmg2(t, pst=pst, wg=wg, g_in=g_in, wbx=wbx):
                last = None
                for c in range(NCH):
                    last = t.matmul(pst[:, :],
                                    wg[:, c * (GF * 128) + g_in * 128: c * (GF * 128) + (g_in + 1) * 128],
                                    x2t_win[wbx][:, c * SW2:(c + 1) * SW2],
                                    start=(c == 0), stop=(c == NCH - 1))
                return last
            tok_mm2 = P.do("tensor", mmg2,
                           waits=tok_wready + [tok_this_win, tok_ev2[pb]], inc="mm")
            tok_x2w_used[wbx] = tok_mm2
            tok_grp_used[gb] = tok_mm2
            if knd == "k":
                tok_ev = P.do("scalar", (lambda s, pst=pst, hh=hh, w=w: s.activation(
                    kf2[:, hh * S + w * SW2: hh * S + (w + 1) * SW2], pst[:, :],
                    AF.Copy, scale=colscale[:, 16 + hh:17 + hh])),
                    waits=[tok_mm2] + colscale_ready, inc="ac")
                kf2_ready.append(tok_ev)
            elif knd == "q":
                qb2 = (idx * GF + g_in) % 2
                tok_ev = P.do("scalar", (lambda s, pst=pst, hh=hh, qb2=qb2: s.activation(
                    qstage[qb2][:, 0:SW2], pst[:, :], AF.Copy, scale=colscale[:, hh:hh + 1])),
                    waits=[tok_mm2, tok_qsp2[qb2]] + colscale_ready, inc="ac")
                tok_qsp2[qb2] = P.dmas("sync", f"q2s{qb2}",
                                       qf2_d[hh * 128:(hh + 1) * 128, w * SW2:(w + 1) * SW2],
                                       qstage[qb2][:, 0:SW2], waits=[tok_ev])
                if w == NW2 - 1:
                    q2_spill_tok[hh] = [tok_qsp2[0], tok_qsp2[1]]
            else:
                tok_ev = P.do("scalar", (lambda s, pst=pst: s.copy(vstage2[:, :], pst[:, :])),
                              waits=[tok_mm2, tok_vtp2], inc="ac")
                tok_vtp2 = P.dmas(
                    "sync", "v2t",
                    vtok2[:, :].rearrange("p (c f) -> p c f", c=NJ)[:, 4 * w:4 * w + 4,
                                                                    hh * 128:(hh + 1) * 128],
                    vstage2[:, :].rearrange("p (t s) -> p t s", t=4),
                    waits=[tok_ev], transpose=True)
                vtok2_ready.append(tok_vtp2)
            tok_ev2[pb] = tok_ev
        if idx + 2 < len(items):
            emit_x2t_load(idx + 2)
        if w == NW2 - 1 and gi + 2 < ngroups:
            emit_group_weights(gi + 2)

    p2a_stack.close()
    att2_stack = contextlib.ExitStack()

    def kf2_slice(h, c):
        return kf2[:, h * S + c * 128: h * S + (c + 1) * 128]

    def ksub2_src(h):
        return kf2[:, h * S: (h + 1) * S: SUB]

    def v2_slice(h, c):
        return vtok2[:, c * D + h * 128: c * D + (h + 1) * 128]

    P.set_fence()
    wp_a = sbR("wp_a", [128, NCH * 1024], BF16)
    wp_a_tok = [None, None]
    for ew in range(2):
        for c in range(NCH):
            wp_a_tok[ew] = P.dmas("gpsimd", f"wpa{ew}",
                                  wp_a[:, c * 1024 + ew * 512: c * 1024 + (ew + 1) * 512],
                                  wp_d[c * 128:(c + 1) * 128, ew * 512:(ew + 1) * 512])
    ctx2_done = attn_block("a2", kf2_slice, ksub2_src, v2_slice, qf2_d, ctx2_d,
                           q_ready=lambda h: list(q2_spill_tok[h] or []),
                           kv_ready=kf2_ready[-1:] + vtok2_ready[-1:],
                           att_stack=att2_stack)
    att2_stack.close()
    kv2_stack.close()

    c2_stack = contextlib.ExitStack()
    sbC2, _ = mk_alloc(c2_stack)
    wp_b = sbC2("wp_b", [128, NCH * 1024], BF16)
    P.set_fence()
    wp_ew_tok = [None] * 4
    wp_ew_tok[0] = wp_a_tok[0]
    wp_ew_tok[1] = wp_a_tok[1]
    for ew in range(2, 4):
        for c in range(NCH):
            wp_ew_tok[ew] = P.dmas("gpsimd", f"wpb{ew}",
                                   wp_b[:, c * 1024 + (ew - 2) * 512: c * 1024 + (ew - 1) * 512],
                                   wp_d[c * 128:(c + 1) * 128, ew * 512:(ew + 1) * 512])

    def w2_slice_2(h, ew):
        if ew < 2:
            return wp_a[:, h * 1024 + ew * 512: h * 1024 + (ew + 1) * 512]
        return wp_b[:, h * 1024 + (ew - 2) * 512: h * 1024 + (ew - 1) * 512]

    out_stored = []

    def dest2(j, tok_y, y_rows):
        tok_store = P.dmas("sync", "outs", out_d[j * 128:(j + 1) * 128, :], y_rows[:, :],
                           waits=[tok_y])
        out_stored.append(tok_store)
        return [tok_store]

    outproj_ln("c2", ctx2_d, x2_d, w2_slice_2, None, dest2,
               ctx_ready=ctx2_done, w2_ready=lambda ew: [wp_ew_tok[ew]],
               resid_ready=x2_ready, c_stack=c2_stack)

    P.do("sync", lambda e: None, waits=[out_stored[-1]])
    P.do("gpsimd", lambda e: None, waits=[out_stored[-1]])

    P.emit()
    c2_stack.close()
    root.close()
    return nc


_NC_CACHE = None


def kernel(L0_attn_x, L0_attn_Wqkv_w, L0_attn_Wqkv_b, L0_attn_out_w, L0_attn_out_b,
           L0_ffn_Wa, L0_ffn_Wp, L0_ffn_w_std, L0_ffn_w_rec):
    global _NC_CACHE
    if _NC_CACHE is None:
        _NC_CACHE = build_program()
    nc = _NC_CACHE

    x = np.ascontiguousarray(np.asarray(L0_attn_x, dtype=np.float32))
    base = {
        "wqkv": np.ascontiguousarray(np.asarray(L0_attn_Wqkv_w, dtype=np.float32)),
        "bqkv": np.ascontiguousarray(np.asarray(L0_attn_Wqkv_b, dtype=np.float32)).reshape(1, EQ),
        "wout": np.ascontiguousarray(np.asarray(L0_attn_out_w, dtype=np.float32)),
        "bout": np.ascontiguousarray(np.asarray(L0_attn_out_b, dtype=np.float32)).reshape(1, D),
        "wa": np.ascontiguousarray(np.asarray(L0_ffn_Wa, dtype=np.float32)),
        "wp": np.ascontiguousarray(np.asarray(L0_ffn_Wp, dtype=np.float32)),
        "wstd": np.ascontiguousarray(np.asarray(L0_ffn_w_std, dtype=np.float32)).reshape(1, H),
        "wrec": np.ascontiguousarray(np.asarray(L0_ffn_w_rec, dtype=np.float32)).reshape(1, H),
    }
    in_maps = []
    for b in range(8):
        m = dict(base)
        m["x"] = np.ascontiguousarray(x[b])
        in_maps.append(m)
    res = run_bass_kernel_spmd(nc, in_maps, core_ids=list(range(8)))
    out = np.stack([res.results[b]["out"] for b in range(8)], axis=0).astype(np.float32)
    return out


# version 48
# speedup vs baseline: 1.0594x; 1.0208x over previous
import math
import contextlib
import numpy as np
import concourse.bass as bass
import concourse.mybir as mybir
from concourse.bass_utils import run_bass_kernel_spmd

F32 = mybir.dt.float32
BF16 = mybir.dt.bfloat16
AF = mybir.ActivationFunctionType
ALU = mybir.AluOpType
AX = mybir.AxisListType

S = 2048
D = 2048
H = 16
DH = 128
EQ = 2304
FDIM = 6144
NCH = D // 128
NJ = S // 128
NW = S // 512
ISQ = 1.0 / math.sqrt(128.0)
SUB = 8
GF = 4
MASK_NEG = -1e30
EPS = 1e-5

ENGINES = ("tensor", "vector", "scalar", "gpsimd", "sync")
DEBUG_SPILLS = False


class Prog:

    def __init__(self, nc):
        self.nc = nc
        self.items = {e: [] for e in ENGINES}
        self.counts = {}
        self.fence = []

    def sem(self, name):
        self.counts.setdefault(name, 0)
        return name

    def tok(self, name):
        self.sem(name)
        return (name, self.counts[name])

    def set_fence(self):
        self.fence = [(n, v) for n, v in self.counts.items() if v > 0]

    def do(self, eng, fn, waits=(), inc=None, inc_by=1):
        flat = []
        for t in waits:
            if t is None:
                continue
            if isinstance(t, list):
                flat.extend(x for x in t if x is not None)
            else:
                flat.append(t)
        w = flat + list(self.fence)
        tokv = None
        if inc is not None:
            self.sem(inc)
            self.counts[inc] += inc_by
            tokv = (inc, self.counts[inc])
        self.items[eng].append((fn, w, inc, inc_by))
        return tokv

    def dma(self, eng, out_ap, in_ap, waits=(), inc=None, **kw):
        def fn(e, out_ap=out_ap, in_ap=in_ap, kw=kw):
            return e.dma_start(out_ap, in_ap, **kw)
        return self.do(eng, fn, waits=waits, inc=inc, inc_by=16)

    def dmas(self, eng, sem, out_ap, in_ap, waits=(), **kw):
        return self.dma(eng, out_ap, in_ap, waits=waits, inc=sem, **kw)

    def emit(self):
        nc = self.nc
        with contextlib.ExitStack() as es:
            handles = {n: es.enter_context(nc.semaphore(n)) for n in self.counts}
            with nc.Block() as block:
                for eng_name in ENGINES:
                    items = self.items[eng_name]
                    if not items:
                        continue

                    def body(eng, items=items):
                        waited = {}
                        for fn, waits, inc, inc_by in items:
                            for (sname, val) in waits:
                                if waited.get(sname, 0) < val:
                                    eng.wait_ge(handles[sname], val)
                                    waited[sname] = val
                            ins = fn(eng)
                            if inc is not None:
                                ins.then_inc(handles[inc], inc_by)
                    getattr(block, eng_name)(body)


def build_program():
    nc = bass.Bass("TRN2", target_bir_lowering=False)

    x_d = nc.dram_tensor("x", [S, D], F32, kind="ExternalInput")
    wqkv_d = nc.dram_tensor("wqkv", [EQ, D], F32, kind="ExternalInput")
    bqkv_d = nc.dram_tensor("bqkv", [1, EQ], F32, kind="ExternalInput")
    wout_d = nc.dram_tensor("wout", [D, D], F32, kind="ExternalInput")
    bout_d = nc.dram_tensor("bout", [1, D], F32, kind="ExternalInput")
    wa_d = nc.dram_tensor("wa", [D, FDIM], F32, kind="ExternalInput")
    wp_d = nc.dram_tensor("wp", [D, D], F32, kind="ExternalInput")
    wstd_d = nc.dram_tensor("wstd", [1, H], F32, kind="ExternalInput")
    wrec_d = nc.dram_tensor("wrec", [1, H], F32, kind="ExternalInput")
    out_d = nc.dram_tensor("out", [S, D], F32, kind="ExternalOutput")

    spill_kind = "ExternalOutput" if DEBUG_SPILLS else "Internal"
    qf1_d = nc.dram_tensor("qf1_spill", [D, S], BF16, kind=spill_kind)
    ctx1_d = nc.dram_tensor("ctx1_spill", [D, S], BF16, kind=spill_kind)
    x2_d = nc.dram_tensor("x2_spill", [S, D], F32, kind=spill_kind)
    x2t_d = nc.dram_tensor("x2t_spill", [D, S], BF16, kind=spill_kind)
    qf2_d = nc.dram_tensor("qf2_spill", [D, S], BF16, kind=spill_kind)
    ctx2_d = nc.dram_tensor("ctx2_spill", [D, S], BF16, kind=spill_kind)

    P = Prog(nc)
    root = contextlib.ExitStack()

    def mk_alloc(stack, side=None):
        def sb(name, shape, dt):
            return stack.enter_context(nc.sbuf_tensor(name, shape, dt, side=side))

        def ps(name, shape, dt=F32):
            return stack.enter_context(nc.psum_tensor(name, shape, dt))
        return sb, ps

    sb0, ps0 = mk_alloc(root)

    ident = sb0("ident", [128, 128], BF16)
    identf = sb0("identf", [128, 128], F32)
    maskT = sb0("maskT", [128, 128], BF16)
    mask01 = sb0("mask01", [128, 128], BF16)
    ones_r512 = sb0("ones_r512", [1, 512], BF16)
    ones_c128 = sb0("ones_c128", [128, 1], BF16)
    ones_rf = sb0("ones_rf", [1, 128], F32)
    wstd_sb = sb0("wstd_sb", [1, H], F32)
    wrec_sb = sb0("wrec_sb", [1, H], F32)
    sstd_sb = sb0("sstd_sb", [1, H], F32)
    srec_sb = sb0("srec_sb", [1, H], F32)
    m01_sb = sb0("m01_sb", [1, H], F32)
    m01b = sb0("m01b", [128, H], F32)
    colscale = sb0("colscale", [128, 32], F32)


    for s in ("g", "sy", "scq", "mm", "ac", "ve", "gp", "so"):
        P.sem(s)


    kv1_stack = contextlib.ExitStack()
    sbKV1, _ = mk_alloc(kv1_stack)
    kt1 = sbKV1("kt1", [128, S], BF16)
    vtok1 = sbKV1("vtok1", [128, NJ * DH], BF16)
    bq_cols = sbKV1("bq_cols", [128, 18], F32)
    tok_bq = P.dmas("gpsimd", "ini0",
                    bq_cols[:, :],
                    bqkv_d[:, :].rearrange("o (c p) -> (o p) c", p=128),
                    allow_slow_non_contiguous=True)
    a_stack = contextlib.ExitStack()
    sbA, psA = mk_alloc(a_stack)
    xT = sbA("xT", [128, NCH * S], BF16)
    xrows_bf = [sbA(f"xrows_bf{i}", [128, D], BF16) for i in range(2)]
    wqrows = [sbA(f"wqrows{i}", [128, D], BF16) for i in range(2)]
    wqT = [sbA(f"wqT{i}", [128, NCH * 128], BF16) for i in range(2)]
    vstage1 = sbA("vstage1", [128, S], BF16)
    right_root = contextlib.ExitStack()
    sbR, _ = mk_alloc(right_root, side="right")
    qstage = [sbR(f"qstage{i}", [128, 512], BF16) for i in range(2)]

    tok_xrow = [None, None]
    tok_xtp = [None, None]
    xT_chunk_tok = [None] * NJ

    def emit_xT(j):
        b = j % 2
        tok_xrow[b] = P.dmas("gpsimd", f"xr{b}", xrows_bf[b][:, :], x_d[j * 128:(j + 1) * 128, :],
                             waits=[tok_xtp[b]])
        tok_xtp[b] = P.dmas(
            "sync", f"xt{b}",
            xT[:, :].rearrange("p (c s) -> p c s", c=NCH)[:, :, j * 128:(j + 1) * 128],
            xrows_bf[b][:, :].rearrange("p (c f) -> p c f", c=NCH),
            waits=[tok_xrow[b]], transpose=True)
        xT_chunk_tok[j] = tok_xtp[b]

    for j in range(4):
        emit_xT(j)

    t_ms = [None]

    def g_op(fn):
        tok = P.do("gpsimd", (lambda g, fn=fn: fn(g)), waits=[t_ms[0]], inc="gp")
        t_ms[0] = tok
        return tok

    g_op(lambda g: g.memset(ident[:, :], 0.0))
    g_op(lambda g: g.affine_select(out=ident[:, :], in_=ident[:, :], compare_op=ALU.not_equal,
                                   fill=1.0, base=0, pattern=[[-1, 128]], channel_multiplier=1))
    g_op(lambda g: g.memset(identf[:, :], 0.0))
    g_op(lambda g: g.affine_select(out=identf[:, :], in_=identf[:, :], compare_op=ALU.not_equal,
                                   fill=1.0, base=0, pattern=[[-1, 128]], channel_multiplier=1))
    g_op(lambda g: g.memset(maskT[:, :], 0.0))
    g_op(lambda g: g.affine_select(out=maskT[:, :], in_=maskT[:, :], compare_op=ALU.is_ge,
                                   fill=MASK_NEG, base=0, pattern=[[1, 128]], channel_multiplier=-1))
    g_op(lambda g: g.memset(mask01[:, :], 1.0))
    g_op(lambda g: g.affine_select(out=mask01[:, :], in_=mask01[:, :], compare_op=ALU.is_ge,
                                   fill=0.0, base=0, pattern=[[1, 128]], channel_multiplier=-1))
    g_op(lambda g: g.memset(ones_r512[:, :], 1.0))
    g_op(lambda g: g.memset(ones_c128[:, :], 1.0))
    tok_consts = g_op(lambda g: g.memset(ones_rf[:, :], 1.0))
    tok_ws = P.dmas("gpsimd", "ini2", wstd_sb[:, :], wstd_d[:, :])
    tok_wr = P.dmas("gpsimd", "ini3", wrec_sb[:, :], wrec_d[:, :])

    t1 = P.do("vector", lambda v: v.tensor_scalar_max(sstd_sb[:, :], wstd_sb[:, :], 1e-8),
              waits=[tok_ws], inc="ve")
    t2 = P.do("vector", lambda v: v.tensor_scalar_max(srec_sb[:, :], wrec_sb[:, :], 1e-8),
              waits=[tok_wr], inc="ve")
    tok_m01 = P.do("vector", lambda v: v.tensor_single_scalar(m01_sb[:, :], wrec_sb[:, :], 0.1, ALU.is_gt),
                   waits=[tok_wr], inc="ve")
    tok_ss = P.do("scalar", lambda s: s.activation(sstd_sb[:, :], sstd_sb[:, :], AF.Sqrt),
                  waits=[t1], inc="ac")
    tok_sr = P.do("scalar", lambda s: s.activation(srec_sb[:, :], srec_sb[:, :], AF.Sqrt),
                  waits=[t2], inc="ac")

    pro_stack = contextlib.ExitStack()
    sbP, psP = mk_alloc(pro_stack)
    u_row = sbP("u_row", [1, 128], F32)
    v_row = sbP("v_row", [1, 128], F32)
    sstd2 = sbP("sstd2", [1, 32], F32)
    srec2 = sbP("srec2", [1, 32], F32)
    cs_ps = psP("cs_ps", [128, 32])
    mb_ps = psP("mb_ps", [128, H])

    g_op(lambda g: g.memset(u_row[:, :], 0.0))
    g_op(lambda g: g.affine_select(out=u_row[:, :], in_=u_row[:, :], compare_op=ALU.is_ge,
                                   fill=1.0, base=-120, pattern=[[1, 128]], channel_multiplier=0))
    g_op(lambda g: g.memset(v_row[:, :], 1.0))
    tok_uv = g_op(lambda g: g.affine_select(out=v_row[:, :], in_=v_row[:, :], compare_op=ALU.is_ge,
                                            fill=0.0, base=-120, pattern=[[1, 128]], channel_multiplier=0))

    tuv3 = P.do("vector", lambda v: v.tensor_copy(sstd2[0:1, 0:16], sstd_sb[:, :]),
                waits=[tok_ss], inc="ve")
    tuv4 = P.do("vector", lambda v: v.tensor_copy(sstd2[0:1, 16:32], sstd_sb[:, :]),
                waits=[tok_ss], inc="ve")
    tuv5 = P.do("vector", lambda v: v.tensor_copy(srec2[0:1, 0:16], srec_sb[:, :]),
                waits=[tok_sr], inc="ve")
    tuv6 = P.do("vector", lambda v: v.tensor_copy(srec2[0:1, 16:32], srec_sb[:, :]),
                waits=[tok_sr], inc="ve")

    def build_colscale(t):
        t.matmul(cs_ps[:, :], u_row[:, :], sstd2[:, :], start=True, stop=False)
        return t.matmul(cs_ps[:, :], v_row[:, :], srec2[:, :], start=False, stop=True)

    tok_csm = P.do("tensor", build_colscale,
                   waits=[tuv3, tuv4, tuv5, tuv6, tok_consts, tok_uv], inc="mm")
    tok_cs1 = P.do("scalar", lambda s: s.mul(colscale[:, 0:16], cs_ps[:, 0:16], ISQ),
                   waits=[tok_csm], inc="ac")
    tok_cs2 = P.do("scalar", lambda s: s.copy(colscale[:, 16:32], cs_ps[:, 16:32]),
                   waits=[tok_csm], inc="ac")
    tok_mbm = P.do("tensor", lambda t: t.matmul(mb_ps[:, :], ones_rf[0:1, 0:128], m01_sb[:, :],
                                                start=True, stop=True),
                   waits=[tok_m01, tok_consts], inc="mm")
    tok_m01b = P.do("scalar", lambda s: s.copy(m01b[:, :], mb_ps[:, :]),
                    waits=[tok_mbm], inc="ac")
    colscale_ready = [tok_cs1, tok_cs2]
    pro_stack.close()

    proj_ps = [psA(f"proj_ps{i}", [128, 512]) for i in range(4)]

    ec_order = [16, 17] + list(range(16))
    tok_wqrow = [None, None]
    tok_wqtp = [None, None]
    tok_wq_used = [None, None]
    tok_evac = [None] * 4
    tok_qspill = [None, None]
    kt1_ready = []
    vtok1_ready = None
    q1_spill_tok = [None] * H

    for i, ec in enumerate(ec_order):
        b = i % 2
        tok_wqrow[b] = P.dmas("gpsimd", f"wqr{b}", wqrows[b][:, :], wqkv_d[ec * 128:(ec + 1) * 128, :],
                              waits=[tok_wqtp[b]])
        tok_wqtp[b] = P.dmas("sync", f"wqt{b}",
                             wqT[b][:, :].rearrange("p (c e) -> p c e", c=NCH),
                             wqrows[b][:, :].rearrange("p (c f) -> p c f", c=NCH),
                             waits=[tok_wqrow[b], tok_wq_used[b]], transpose=True)
        for w in range(NW):
            pb = (i * NW + w) % 4
            pst = proj_ps[pb]
            wq_b = wqT[b]

            def mmg(t, pst=pst, wq_b=wq_b, ec=ec, w=w):
                last = None
                for c in range(NCH):
                    last = t.matmul(pst[:, :], wq_b[:, c * 128:(c + 1) * 128],
                                    xT[:, c * S + w * 512: c * S + (w + 1) * 512],
                                    start=(c == 0), stop=(c == NCH - 1))
                return last
            tok_mm = P.do("tensor", mmg,
                          waits=[tok_wqtp[b], tok_evac[pb], tok_consts, tok_bq,
                                 tok_cs1, tok_cs2, tok_m01b,
                                 xT_chunk_tok[4 * w + 3], xT_chunk_tok[4 * w + 2]],
                          inc="mm")
            if ec == 16:
                tok_ev = P.do("vector", (lambda v, pst=pst, w=w: v.tensor_scalar(
                    kt1[:, w * 512:(w + 1) * 512], pst[:, :],
                    bq_cols[:, 16:17], None, op0=ALU.add)),
                    waits=[tok_mm], inc="ve")
                kt1_ready.append(tok_ev)
            elif ec == 17:
                tok_ev = P.do("vector", (lambda v, pst=pst, w=w: v.tensor_scalar(
                    vstage1[:, w * 512:(w + 1) * 512], pst[:, :],
                    bq_cols[:, 17:18], None, op0=ALU.add)),
                    waits=[tok_mm], inc="ve")
                if w == NW - 1:
                    vtok1_ready = P.dmas(
                        "sync", "v1t",
                        vtok1[:, :].rearrange("p (c f) -> p c f", c=NJ),
                        vstage1[:, :].rearrange("p (c s) -> p c s", c=NJ),
                        waits=[tok_ev, P.tok("ac")], transpose=True)
            else:
                qb = (i * NW + w) % 2
                tok_ev = P.do("vector", (lambda v, pst=pst, qb=qb, ec=ec: v.tensor_scalar(
                    qstage[qb][:, :], pst[:, :],
                    bq_cols[:, ec:ec + 1], ISQ, op0=ALU.add, op1=ALU.mult)),
                    waits=[tok_mm, tok_qspill[qb]], inc="ve")
                tok_qspill[qb] = P.dmas("sync", f"q1s{qb}",
                                        qf1_d[ec * 128:(ec + 1) * 128, w * 512:(w + 1) * 512],
                                        qstage[qb][:, :], waits=[tok_ev])
                q1_spill_tok[ec] = [tok_qspill[0], tok_qspill[1]]
            tok_evac[pb] = tok_ev
            tok_wq_used[b] = tok_mm
            if i == 0 and w < NW - 1:
                for jj in range(4 * (w + 1), 4 * (w + 1) + 4):
                    emit_xT(jj)

    P1A_done = [P.tok("mm"), P.tok("ac")]
    a_stack.close()
    P.set_fence()
    ow_stack = contextlib.ExitStack()
    sbW, _ = mk_alloc(ow_stack, side="right")
    outwT = sbW("outwT", [128, NCH * D], BF16)
    bo_bf = sbW("bo_bf", [1, D], BF16)
    tok_bo = P.dmas("gpsimd", "ini1", bo_bf[:, :], bout_d[:, :])
    worows = [sbW(f"worows{i}", [128, D], BF16) for i in range(2)]
    tok_worow = [None, None]
    tok_wotp = [None, None]
    for ecb in range(16):
        b = ecb % 2
        tok_worow[b] = P.dmas("gpsimd", f"wor{b}", worows[b][:, :], wout_d[ecb * 128:(ecb + 1) * 128, :],
                              waits=[tok_wotp[b]] + P1A_done)
        tok_wotp[b] = P.dmas(
            "sync", f"wot{b}",
            outwT[:, :].rearrange("p (c e) -> p c e", c=NCH)[:, :, ecb * 128:(ecb + 1) * 128],
            worows[b][:, :].rearrange("p (c f) -> p c f", c=NCH),
            waits=[tok_worow[b]], transpose=True)
    outwT_ready = [tok_wotp[0], tok_wotp[1]]

    def attn_block(tag, kf_slice, ksub_src, v_slice, qspill_d, ctxspill_d,
                   q_ready, kv_ready, att_stack):
        sbT, psT = mk_alloc(att_stack)
        qh = [sbT(f"{tag}qh{i}", [128, S], BF16) for i in range(2)]
        ksub = [sbT(f"{tag}ksub{i}", [128, S // SUB], BF16) for i in range(2)]
        qsub = [sbT(f"{tag}qsub{i}", [128, 128], BF16) for i in range(2)]
        mhat = [sbT(f"{tag}mhat{i}", [128, 1], F32) for i in range(2)]
        negmrow = [sbT(f"{tag}negmrow{i}", [1, 128], F32) for i in range(2)]
        mmin = [sbT(f"{tag}mmin{i}", [1, 1], F32) for i in range(2)]
        mbias = sbT(f"{tag}mbias", [128, 2], F32)
        ptile = [sbT(f"{tag}ptile{i}", [128, 512], BF16) for i in range(3)]
        recl = sbT(f"{tag}recl", [1, 512], F32)
        bcast_sb = sbT(f"{tag}bcast_sb", [128, 512], F32)
        ctx_sb = [sbT(f"{tag}ctx_sb{i}", [128, 512], BF16) for i in range(2)]
        s_ps = [psT(f"{tag}s_ps0", [128, 512]), psT(f"{tag}s_ps1", [128, 512]),
                psT(f"{tag}s_ps2", [128, 512])]
        l_ps = [psT(f"{tag}l_ps0", [1, 512]), psT(f"{tag}l_ps1", [1, 512])]
        c_ps = [psT(f"{tag}c_ps0", [128, 512]), psT(f"{tag}c_ps1", [128, 512])]
        bc_ps = psT(f"{tag}bc_ps", [128, 512])
        st_ps = bc_ps[:, 0:256]
        mt_view = s_ps[0][0:1, 0:128]

        tok_qh = [None, None]
        tok_qh_freed = [None, None]
        tok_ctxsp = [None, None]
        tok_ctx_tt = [None, None]
        tok_bcast_copy = None
        tok_stats_red = None
        tok_negm_evac = None
        tok_exp_bank = [None, None, None]
        tok_lpv_bank = [None, None, None]
        tok_recip_bank = [None, None]
        ctx_done = []
        pending_norm = []

        def emit_recip():
            ent = pending_norm[-1]
            (cp, wb, hh, ww, tok_lpv_last, _) = ent
            lp = l_ps[wb]
            tok_recip_bank[wb] = P.do(
                "vector", (lambda v, lp=lp: v.reciprocal(recl[:, :], lp[0:1, :])),
                waits=[tok_lpv_last, tok_bcast_copy], inc="ve")
            ent[5] = tok_recip_bank[wb]

        def flush_norm():
            while pending_norm:
                (cp, wb, hh, ww, tok_lpv_last, tok_recip) = pending_norm.pop(0)
                assert tok_recip is not None
                tok_bcmm = P.do("tensor", (lambda t: t.matmul(
                    bc_ps[:, :], ones_rf[0:1, 0:128], recl[:, :], start=True, stop=True)),
                    waits=[tok_recip, tok_bcast_copy, P.tok("ac")], inc="mm")
                tbc = P.do("vector", (lambda v: v.tensor_copy(bcast_sb[:, :], bc_ps[:, :])),
                           waits=[tok_bcmm], inc="ve")
                ttt = P.do("vector", (lambda v, cp=cp, wb=wb: v.tensor_mul(
                    ctx_sb[wb][:, :], cp[:, :], bcast_sb[:, :])),
                    waits=[tbc, tok_ctxsp[wb]], inc="ve")
                tok_ctx_tt[wb] = ttt
                tok_ctxsp[wb] = P.dmas(
                    "sync", f"{tag}cs{wb}",
                    ctxspill_d[hh * 128:(hh + 1) * 128, ww * 512:(ww + 1) * 512],
                    ctx_sb[wb][:, :], waits=[ttt])
                ctx_done.append(tok_ctxsp[wb])
                return tbc
            return None

        def emit_mhat_chain(h):
            hb = h % 2
            st.chain_toks[hb] = {}
            tok_ksub = P.do("vector", (lambda v, h=h, hb=hb: v.tensor_copy(
                ksub[hb][:, :], ksub_src(h))),
                waits=kv_ready + [P.tok("mm")], inc="ve")
            tok_qsub = P.do("vector", (lambda v, hb=hb: v.tensor_copy(
                qsub[hb][:, :], qh[hb][:, 0:S:16])),
                waits=[tok_qh[hb], P.tok("mm")], inc="ve")
            tok_stats = P.do("tensor", (lambda t, hb=hb: t.matmul(
                st_ps[:, 0:S // SUB], qsub[hb][:, :], ksub[hb][:, :], start=True, stop=True)),
                waits=[tok_ksub, tok_qsub, st.tok_stats_red, st.tok_bcast_copy, P.tok("ac")],
                inc="mm")
            st.tok_stats_red = P.do("vector", (lambda v, hb=hb: v.reduce_max(
                mhat[hb][:, 0:1], st_ps[:, 0:S // SUB], axis=AX.X, negate=True)),
                waits=[tok_stats], inc="ve")
            extra_mt = [tok_exp_bank[0], tok_lpv_bank[0], tok_lpv_bank[1], tok_lpv_bank[2]]
            tok_mt = P.do("tensor", (lambda t, hb=hb: t.transpose(
                mt_view, mhat[hb][:, 0:1], identf[:, :])),
                waits=[st.tok_stats_red, st.tok_negm_evac] + extra_mt, inc="mm")
            st.tok_negm_evac = P.do("scalar", (lambda s, hb=hb: s.copy(
                negmrow[hb][0:1, 0:128], mt_view)),
                waits=[tok_mt], inc="ac")
            tok_mmin = P.do("vector", (lambda v, hb=hb: v.tensor_reduce(
                mmin[hb][:, :], negmrow[hb][0:1, 0:128], axis=AX.X, op=ALU.min)),
                waits=[st.tok_negm_evac], inc="ve")
            tok_mbm = P.do("tensor", (lambda t, hb=hb: t.matmul(
                bc_ps[0:128, 0:1], ones_rf[0:1, 0:128], mmin[hb][:, :], start=True, stop=True)),
                waits=[tok_mmin, st.tok_bcast_copy], inc="mm")
            st.tok_mbias[hb] = P.do("scalar", (lambda s, hb=hb: s.copy(
                mbias[:, hb:hb + 1], bc_ps[0:128, 0:1])),
                waits=[tok_mbm, st.tok_exp_done[hb]], inc="ac")

        class st:
            tok_stats_red = None
            tok_negm_evac = None
            tok_bcast_copy = None
            tok_mbias = [None, None]
            tok_exp_done = [None, None]
            chain_toks = [None, None]

        for h in range(H):
            hb = h % 2
            if h == 0:
                tok_qh[0] = P.dmas("scalar", f"{tag}qh0", qh[0][:, :], qspill_d[0:128, :],
                                   waits=q_ready(0))
                emit_mhat_chain(0)
            if h + 1 < H:
                nhb = (h + 1) % 2
                tok_qh[nhb] = P.dmas("scalar", f"{tag}qh{nhb}", qh[nhb][:, :],
                                     qspill_d[(h + 1) * 128:(h + 2) * 128, :],
                                     waits=[tok_qh_freed[nhb]] + q_ready(h + 1))
            tok_mbias = st.tok_mbias[hb]

            for w in range(NW):
                wb = w % 2
                lp = l_ps[wb]
                cp = c_ps[wb]
                kmax = 4 * w + 4
                tok_lpv = None
                for cc in range(kmax + 2):
                    if cc < kmax:
                        c = cc
                        sb_i = c % 3
                        sp = s_ps[sb_i]
                        q0 = max(w * 512, c * 128)
                        q1 = (w + 1) * 512
                        n = q1 - q0

                        def smm(t, sp=sp, c=c, w=w, q0=q0, q1=q1, h=h, hb=hb, n=n):
                            return t.matmul(sp[:, (q0 - w * 512):(q1 - w * 512)],
                                            kf_slice(h, c), qh[hb][:, q0:q1],
                                            start=True, stop=True)
                        tok_smm = P.do("tensor", smm,
                                       waits=[tok_mbias, tok_exp_bank[sb_i]] +
                                             ([st.tok_negm_evac] if sb_i == 0 else []),
                                       inc="mm")
                        tok_exp_bank[sb_i] = P.do(
                            "scalar", (lambda s, sp=sp, sb_i=sb_i, q0=q0, w=w, n=n, hb=hb:
                                       s.activation(ptile[sb_i][:, 0:n],
                                                    sp[:, (q0 - w * 512):(q0 - w * 512) + n],
                                                    AF.Exp, bias=mbias[:, hb:hb + 1])),
                            waits=[tok_smm, tok_lpv_bank[sb_i]], inc="ac")
                        if c >= 4 * w:
                            tok_exp_bank[sb_i] = P.do(
                                "vector", (lambda v, sb_i=sb_i: v.tensor_mul(
                                    ptile[sb_i][:, 0:128], ptile[sb_i][:, 0:128], mask01[:, :])),
                                waits=[tok_exp_bank[sb_i]], inc="ve")
                    if cc >= 2:
                        c = cc - 2
                        sb_i = c % 3
                        q0 = max(w * 512, c * 128)
                        q1 = (w + 1) * 512
                        n = q1 - q0

                        def lpv(t, lp=lp, cp=cp, sb_i=sb_i, c=c, q0=q0, w=w, n=n, h=h, kmax=kmax):
                            t.matmul(lp[0:1, (q0 - w * 512):(q0 - w * 512) + n],
                                     ones_c128[:, :], ptile[sb_i][:, 0:n],
                                     start=(c == 0), stop=(c == kmax - 1))
                            return t.matmul(cp[:, (q0 - w * 512):(q0 - w * 512) + n],
                                            v_slice(h, c), ptile[sb_i][:, 0:n],
                                            start=(c == 0), stop=(c == kmax - 1))
                        extra = [tok_ctx_tt[wb], tok_recip_bank[wb]] if c == 0 else []
                        tok_lpv = P.do("tensor", lpv,
                                       waits=[tok_exp_bank[sb_i]] + extra, inc="mm")
                        tok_lpv_bank[sb_i] = tok_lpv
                    if cc == 1 and w == 1 and len(pending_norm) >= 2:
                        tok_bcast_copy = flush_norm()
                        st.tok_bcast_copy = tok_bcast_copy
                    if cc == (2 if w <= 1 else 0) and len(pending_norm) == 1 \
                            and pending_norm[-1][5] is None:
                        emit_recip()
                    if cc == (8 if w == 1 else 7) and w >= 1 and pending_norm \
                            and pending_norm[0][5] is not None:
                        tok_bcast_copy = flush_norm()
                        st.tok_bcast_copy = tok_bcast_copy
                pending_norm.append([cp, wb, h, w, tok_lpv, None])
                if w == 0 and h + 1 < H:
                    emit_mhat_chain(h + 1)
            tok_qh_freed[hb] = P.tok("mm")
            st.tok_exp_done[hb] = P.tok("ac")
        while pending_norm:
            if pending_norm[-1][5] is None:
                emit_recip()
            tok_bcast_copy = flush_norm()
        return ctx_done[-2:]

    def outproj_ln(tag, ctxspill_d, resid_d, w2_slice, bias_row, dest_writer,
                   ctx_ready, w2_ready, resid_ready, c_stack):
        sbC, psC = mk_alloc(c_stack)
        NBUF = 3
        ctx_win = [sbC(f"{tag}ctx_win{i}", [128, H * 128], BF16) for i in range(NBUF)]
        x_win = [sbC(f"{tag}x_win{i}", [128, D], F32) for i in range(NBUF)]
        z_sb = sbC(f"{tag}z_sb", [128, D], F32)
        sq_scr = sbC(f"{tag}sq_scr", [128, 2048], BF16)
        zsum4 = sbC(f"{tag}zsum4", [128, 4], F32)
        zsq4 = sbC(f"{tag}zsq4", [128, 4], F32)
        ln_s = sbC(f"{tag}ln_s", [128, 1], F32)
        ln_q = sbC(f"{tag}ln_q", [128, 1], F32)
        invstd = sbC(f"{tag}invstd", [128, 1], F32)
        nbias = sbC(f"{tag}nbias", [128, 1], F32)
        y_rows = sbC(f"{tag}y_rows", [128, D], F32)
        op_ps = [[psC(f"{tag}op_ps{pp}_{ew}", [128, 512]) for ew in range(4)] for pp in range(2)]

        tok_cw = [None] * NBUF
        tok_xw = [None] * NBUF
        tok_opfree = [[None] * 4 for _ in range(2)]
        tok_win_used = [None] * NBUF
        tok_r1_prev = None
        tok_r2_prev = None
        tok_sq_prev = [None] * 4
        tok_aff_prev = None
        tok_y_free = None
        dones = []
        def emit_win_load(j):
            wn = j % NBUF
            tok_cw[wn] = P.dmas("scalar", f"{tag}cw{wn}",
                                ctx_win[wn][:, :].rearrange("p (h s) -> p h s", h=H),
                                ctxspill_d[:, j * 128:(j + 1) * 128].rearrange("(h p) s -> p h s", p=128),
                                waits=[tok_win_used[wn]] + ctx_ready)
            tok_xw[wn] = P.dmas("scalar", f"{tag}xw{wn}", x_win[wn][:, :],
                                resid_d[j * 128:(j + 1) * 128, :],
                                waits=[tok_win_used[wn]] + resid_ready)

        emit_win_load(0)
        emit_win_load(1)
        for j in range(NJ):
            jb = j % 2
            wn = j % NBUF
            tok_cw_j = tok_cw[wn]
            tok_xw_j = tok_xw[wn]
            if j + 2 < NJ:
                emit_win_load(j + 2)
            tok_z = None
            for ew in range(4):
                opp = op_ps[jb][ew]

                def opmm(t, opp=opp, wn=wn, ew=ew):
                    first = True
                    if bias_row is not None:
                        t.matmul(opp[:, :], ones_r512[0:1, 0:128],
                                 bias_row[0:1, ew * 512:(ew + 1) * 512],
                                 start=True, stop=False)
                        first = False
                    last = None
                    for hh in range(H):
                        last = t.matmul(opp[:, :], ctx_win[wn][:, hh * 128:(hh + 1) * 128],
                                        w2_slice(hh, ew),
                                        start=first, stop=(hh == H - 1))
                        first = False
                    return last
                w2r = w2_ready(ew) if callable(w2_ready) else w2_ready
                tok_opmm = P.do("tensor", opmm,
                                waits=[tok_cw_j, tok_opfree[jb][ew], tok_bo] + w2r,
                                inc="mm")
                tok_z = P.do("vector", (lambda v, opp=opp, ew=ew, wn=wn: v.scalar_tensor_tensor(
                    z_sb[:, ew * 512:(ew + 1) * 512], opp[:, :], 1.0,
                    x_win[wn][:, ew * 512:(ew + 1) * 512],
                    op0=ALU.mult, op1=ALU.add, accum_out=zsum4[:, ew:ew + 1])),
                    waits=[tok_opmm, tok_xw_j, tok_aff_prev, tok_r1_prev], inc="ve")
                tok_opfree[jb][ew] = tok_z
                if ew == 3:
                    tok_win_used[wn] = tok_z
                tok_sq_prev[ew] = P.do("scalar", (lambda s, ew=ew: s.activation(
                    sq_scr[:, ew * 512:(ew + 1) * 512], z_sb[:, ew * 512:(ew + 1) * 512],
                    AF.Square, accum_out=zsq4[:, ew:ew + 1])),
                    waits=[tok_z, tok_r2_prev, tok_sq_prev[ew]], inc="ac")
            tok_sq_all = P.tok("ac")
            tok_r1_prev = P.do("vector", lambda v: v.reduce_sum(ln_s[:, :], zsum4[:, :], axis=AX.X),
                               waits=[tok_z], inc="ve")
            tok_r2_prev = P.do("vector", lambda v: v.reduce_sum(ln_q[:, :], zsq4[:, :], axis=AX.X),
                               waits=[tok_sq_all], inc="ve")
            tm = P.do("vector", lambda v: v.tensor_scalar_mul(ln_s[:, :], ln_s[:, :], 1.0 / D),
                      waits=[tok_r1_prev], inc="ve")
            te = P.do("vector", lambda v: v.tensor_scalar_mul(ln_q[:, :], ln_q[:, :], 1.0 / D),
                      waits=[tok_r2_prev], inc="ve")
            tv = P.do("vector", lambda v: v.scalar_tensor_tensor(
                ln_q[:, :], ln_s[:, :], ln_s[:, :], ln_q[:, :],
                op0=ALU.mult, op1=ALU.subtract),
                waits=[tm, te], inc="ve")
            tv2 = P.do("vector", lambda v: v.tensor_scalar_add(ln_q[:, :], ln_q[:, :], -EPS),
                       waits=[tv], inc="ve")
            tv3 = P.do("vector", lambda v: v.reciprocal(ln_q[:, :], ln_q[:, :]),
                       waits=[tv2], inc="ve")
            tv4 = P.do("vector", lambda v: v.tensor_scalar_mul(ln_q[:, :], ln_q[:, :], -1.0),
                       waits=[tv3], inc="ve")
            ti = P.do("scalar", lambda s: s.activation(invstd[:, :], ln_q[:, :], AF.Sqrt),
                      waits=[tv4], inc="ac")
            tb = P.do("vector", lambda v: v.scalar_tensor_tensor(
                nbias[:, :], ln_s[:, :], -1.0, invstd[:, :],
                op0=ALU.mult, op1=ALU.mult),
                waits=[ti], inc="ve")
            tok_aff = P.do("vector", (lambda v: v.tensor_scalar(
                y_rows[:, :], z_sb[:, :], invstd[:, :], nbias[:, :],
                op0=ALU.mult, op1=ALU.add)),
                waits=[tb] + (tok_y_free or []), inc="ve")
            tok_y_free = dest_writer(j, tok_aff, y_rows)
            tok_aff_prev = tok_aff
            dones.append(tok_y_free[0])
        return dones

    att1_stack = contextlib.ExitStack()

    def kf1_slice(h, c):
        return kt1[:, c * 128:(c + 1) * 128]

    def ksub1_src(h):
        return kt1[:, 0:S:SUB]

    def v1_slice(h, c):
        return vtok1[:, c * DH:(c + 1) * DH]

    ctx1_done = attn_block("a1", kf1_slice, ksub1_src, v1_slice, qf1_d, ctx1_d,
                           q_ready=lambda h: list(q1_spill_tok[h] or []),
                           kv_ready=kt1_ready[-1:] + [vtok1_ready],
                           att_stack=att1_stack)
    att1_stack.close()
    kv1_stack.close()

    c1_stack = contextlib.ExitStack()
    sbC1, _ = mk_alloc(c1_stack)
    x2bf = [sbC1(f"x2bf{i}", [128, D], BF16) for i in range(2)]
    x2t_stage = [sbC1(f"x2t_stage{i}", [128, NCH * 128], BF16) for i in range(2)]

    def w2_slice_1(h, ew):
        return outwT[:, h * D + ew * 512: h * D + (ew + 1) * 512]

    x2_stored = []
    x2t_stored = []
    x2t_j_tok = [None] * NJ
    x2_j_tok = [None] * NJ
    _d1 = {"bf": [None, None], "tp": [None, None], "st2": [None, None]}

    def dest1(j, tok_y, y_rows):
        jb = j % 2
        tok_store = P.dmas("sync", "x2s", x2_d[j * 128:(j + 1) * 128, :], y_rows[:, :],
                           waits=[tok_y])
        tok_bf = P.do("scalar", (lambda s, jb=jb, y_rows=y_rows: s.copy(x2bf[jb][:, :], y_rows[:, :])),
                      waits=[tok_y, _d1["tp"][jb]], inc="ac")
        tok_tp = P.dmas("sync", f"x2t{jb}",
                        x2t_stage[jb][:, :].rearrange("p (c s) -> p c s", c=NCH),
                        x2bf[jb][:, :].rearrange("p (c f) -> p c f", c=NCH),
                        waits=[tok_bf, _d1["st2"][jb]], transpose=True)
        tok_st2 = P.dmas("sync", f"x2u{jb}",
                         x2t_d[:, j * 128:(j + 1) * 128].rearrange("(c p) s -> p c s", p=128),
                         x2t_stage[jb][:, :].rearrange("p (c s) -> p c s", c=NCH),
                         waits=[tok_tp])
        _d1["tp"][jb] = tok_tp
        _d1["st2"][jb] = tok_st2
        x2_stored.append(tok_store)
        x2t_stored.append(tok_st2)
        x2t_j_tok[j] = tok_st2
        x2_j_tok[j] = tok_store
        return [tok_store, tok_bf]

    P.set_fence()
    outproj_ln("c1", ctx1_d, x_d, w2_slice_1, bo_bf, dest1,
               ctx_ready=ctx1_done, w2_ready=outwT_ready, resid_ready=[],
               c_stack=c1_stack)
    x2_ready = [x2_stored[-1], P.tok("ac")]
    x2t_ready = [x2t_stored[-1]]
    c1_stack.close()
    ow_stack.close()

    p2a_stack = contextlib.ExitStack()
    sb2, ps2 = mk_alloc(p2a_stack)
    kv2_stack = contextlib.ExitStack()
    sbKV2, _ = mk_alloc(kv2_stack)
    kf2 = sbKV2("kf2", [128, H * S], BF16)
    vtok2 = sbKV2("vtok2", [128, NJ * D], BF16)
    qtail = sb2("qtail", [128, NCH * 128], BF16)
    ktail = sb2("ktail", [128, NCH * 128], BF16)
    ltmp = sb2("ltmp", [128, NCH * 8], F32)
    lows_stack = contextlib.ExitStack()
    sbL, _ = mk_alloc(lows_stack)
    lows = sbL("lows", [128, NCH * 256], BF16)
    proj2_ps = [ps2(f"proj2_ps{i}", [128, 512]) for i in range(4)]

    P.set_fence()
    tok_lows = P.dmas("gpsimd", "low",
                      lows[:, :].rearrange("p (c f) -> p c f", c=NCH),
                      wa_d[:, 5888:6144].rearrange("(c p) f -> p c f", p=128))
    tok_tails = []
    prev_tail = [None, None]
    for h in range(H):
        qlo = lows[:, :].rearrange("p (c f) -> p c f", c=NCH)[:, :, h * 8:(h + 1) * 8]
        klo = lows[:, :].rearrange("p (c f) -> p c f", c=NCH)[:, :, 128 + h * 8:128 + (h + 1) * 8]
        qta = qtail[:, :].rearrange("p (c f) -> p c f", c=NCH)[:, :, h * 8:(h + 1) * 8]
        kta = ktail[:, :].rearrange("p (c f) -> p c f", c=NCH)[:, :, h * 8:(h + 1) * 8]
        lt = ltmp[:, :].rearrange("p (c f) -> p c f", c=NCH)
        td = P.do("vector", (lambda v, klo=klo, qlo=qlo, lt=lt: v.tensor_sub(lt, klo, qlo)),
                  waits=[tok_lows, tok_m01b] + prev_tail, inc="ve")
        tdm = P.do("vector", (lambda v, lt=lt, h=h: v.tensor_scalar_mul(lt, lt, m01b[:, h:h + 1])),
                   waits=[td], inc="ve")
        tq = P.do("vector", (lambda v, qta=qta, qlo=qlo, lt=lt: v.tensor_add(qta, qlo, lt)),
                  waits=[tdm], inc="ve")
        tk = P.do("vector", (lambda v, kta=kta, klo=klo, lt=lt: v.tensor_sub(kta, klo, lt)),
                  waits=[tdm], inc="ve")
        tok_tails += [tq, tk]
        prev_tail = [tq, tk]

    lows_stack.close()
    wagrp = [sb2(f"wagrp{i}", [128, NCH * GF * 128], BF16) for i in range(2)]
    SW2 = 512
    NW2 = S // SW2
    x2t_win = [sb2(f"x2t_win{i}", [128, NCH * SW2], BF16) for i in range(2)]
    vstage2 = sb2("vstage2", [128, SW2], BF16)

    chunks = [("k", h) for h in range(H)] + [("v", h) for h in range(H)] + \
             [("q", h) for h in range(H)]
    ngroups = len(chunks) // GF
    tok_wagrp = [None, None]
    tok_grp_used = [None, None]
    tok_x2w = [None, None]
    tok_x2w_used = [None, None]
    tok_ev2 = [None] * 4
    tok_qsp2 = [None, None]
    tok_vtp2 = None
    kf2_ready = []
    vtok2_ready = []
    q2_spill_tok = [None] * H

    items = [(gi, w) for gi in range(ngroups) for w in range(NW2)]

    def emit_x2t_load(idx):
        gi, w = items[idx]
        wbx = idx % 2
        tok_x2w[wbx] = P.dmas("scalar", f"x2w{wbx}",
                              x2t_win[wbx][:, :].rearrange("p (c s) -> p c s", c=NCH),
                              x2t_d[:, w * SW2:(w + 1) * SW2].rearrange("(c p) s -> p c s", p=128),
                              waits=[tok_x2w_used[wbx], x2t_j_tok[4 * w + 3], x2t_j_tok[4 * w + 2]])

    grp_weight_toks = [None] * ngroups

    def emit_group_weights(gi):
        gb = gi % 2
        grp = chunks[gi * GF:(gi + 1) * GF]
        kind = grp[0][0]
        h0 = grp[0][1]
        wg = wagrp[gb]
        wg4 = wg[:, :].rearrange("p (c g f) -> p c g f", c=NCH, g=GF)
        if kind == "v":
            tok_w = P.dmas("gpsimd", f"wag{gb}", wg4,
                           wa_d[:, 3840 + h0 * 128: 3840 + (h0 + GF) * 128]
                           .rearrange("(c p) (g f) -> p c g f", p=128, g=GF),
                           waits=[tok_grp_used[gb]] + tok_tails)
            grp_weight_toks[gi] = [tok_w]
        else:
            base = 1920 if kind == "k" else 0
            for g_in in range(GF):
                tok_w = P.dmas("gpsimd", f"wag{gb}", wg4[:, :, g_in, 0:120],
                               wa_d[:, base + (h0 + g_in) * 120: base + (h0 + g_in + 1) * 120]
                               .rearrange("(c p) f -> p c f", p=128),
                               waits=[tok_grp_used[gb]] + tok_tails)
            tailsrc = qtail if kind == "q" else ktail
            tok_tc = P.do("vector", (lambda v, wg4=wg4, tailsrc=tailsrc, h0=h0: v.tensor_copy(
                wg4[:, :, :, 120:128],
                tailsrc[:, :].rearrange("p (c f) -> p c f", c=NCH)[:, :, h0 * 8:(h0 + GF) * 8]
                .rearrange("p c (g f) -> p c g f", g=GF))),
                waits=[tok_w] + tok_tails + [tok_grp_used[gb]], inc="ve")
            grp_weight_toks[gi] = [tok_w, tok_tc]

    emit_group_weights(0)
    emit_group_weights(1)
    emit_x2t_load(0)
    emit_x2t_load(1)
    for idx, (gi, w) in enumerate(items):
        gb = gi % 2
        wbx = idx % 2
        grp = chunks[gi * GF:(gi + 1) * GF]
        wg = wagrp[gb]
        tok_wready = grp_weight_toks[gi]
        tok_this_win = tok_x2w[wbx]
        for g_in, (knd, hh) in enumerate(grp):
            pb = (idx * GF + g_in) % 4
            pst = proj2_ps[pb]

            def mmg2(t, pst=pst, wg=wg, g_in=g_in, wbx=wbx):
                last = None
                for c in range(NCH):
                    last = t.matmul(pst[:, :],
                                    wg[:, c * (GF * 128) + g_in * 128: c * (GF * 128) + (g_in + 1) * 128],
                                    x2t_win[wbx][:, c * SW2:(c + 1) * SW2],
                                    start=(c == 0), stop=(c == NCH - 1))
                return last
            tok_mm2 = P.do("tensor", mmg2,
                           waits=tok_wready + [tok_this_win, tok_ev2[pb]], inc="mm")
            tok_x2w_used[wbx] = tok_mm2
            tok_grp_used[gb] = tok_mm2
            if knd == "k":
                tok_ev = P.do("scalar", (lambda s, pst=pst, hh=hh, w=w: s.activation(
                    kf2[:, hh * S + w * SW2: hh * S + (w + 1) * SW2], pst[:, :],
                    AF.Copy, scale=colscale[:, 16 + hh:17 + hh])),
                    waits=[tok_mm2] + colscale_ready, inc="ac")
                kf2_ready.append(tok_ev)
            elif knd == "q":
                qb2 = (idx * GF + g_in) % 2
                tok_ev = P.do("scalar", (lambda s, pst=pst, hh=hh, qb2=qb2: s.activation(
                    qstage[qb2][:, 0:SW2], pst[:, :], AF.Copy, scale=colscale[:, hh:hh + 1])),
                    waits=[tok_mm2, tok_qsp2[qb2]] + colscale_ready, inc="ac")
                tok_qsp2[qb2] = P.dmas("sync", f"q2s{qb2}",
                                       qf2_d[hh * 128:(hh + 1) * 128, w * SW2:(w + 1) * SW2],
                                       qstage[qb2][:, 0:SW2], waits=[tok_ev])
                if w == NW2 - 1:
                    q2_spill_tok[hh] = [tok_qsp2[0], tok_qsp2[1]]
            else:
                tok_ev = P.do("scalar", (lambda s, pst=pst: s.copy(vstage2[:, :], pst[:, :])),
                              waits=[tok_mm2, tok_vtp2], inc="ac")
                tok_vtp2 = P.dmas(
                    "sync", "v2t",
                    vtok2[:, :].rearrange("p (c f) -> p c f", c=NJ)[:, 4 * w:4 * w + 4,
                                                                    hh * 128:(hh + 1) * 128],
                    vstage2[:, :].rearrange("p (t s) -> p t s", t=4),
                    waits=[tok_ev], transpose=True)
                vtok2_ready.append(tok_vtp2)
            tok_ev2[pb] = tok_ev
        if idx + 2 < len(items):
            emit_x2t_load(idx + 2)
        if w == NW2 - 1 and gi + 2 < ngroups:
            emit_group_weights(gi + 2)

    p2a_stack.close()
    att2_stack = contextlib.ExitStack()

    def kf2_slice(h, c):
        return kf2[:, h * S + c * 128: h * S + (c + 1) * 128]

    def ksub2_src(h):
        return kf2[:, h * S: (h + 1) * S: SUB]

    def v2_slice(h, c):
        return vtok2[:, c * D + h * 128: c * D + (h + 1) * 128]

    P.set_fence()
    wp_a = sbR("wp_a", [128, NCH * 1024], BF16)
    wp_a_tok = [None, None]
    for ew in range(2):
        for c in range(NCH):
            wp_a_tok[ew] = P.dmas("gpsimd", f"wpa{ew}",
                                  wp_a[:, c * 1024 + ew * 512: c * 1024 + (ew + 1) * 512],
                                  wp_d[c * 128:(c + 1) * 128, ew * 512:(ew + 1) * 512])
    ctx2_done = attn_block("a2", kf2_slice, ksub2_src, v2_slice, qf2_d, ctx2_d,
                           q_ready=lambda h: list(q2_spill_tok[h] or []),
                           kv_ready=kf2_ready[-1:] + vtok2_ready[-1:],
                           att_stack=att2_stack)
    att2_stack.close()
    kv2_stack.close()

    c2_stack = contextlib.ExitStack()
    sbC2, _ = mk_alloc(c2_stack)
    wp_b = sbC2("wp_b", [128, NCH * 1024], BF16)
    P.set_fence()
    wp_ew_tok = [None] * 4
    wp_ew_tok[0] = wp_a_tok[0]
    wp_ew_tok[1] = wp_a_tok[1]
    for ew in range(2, 4):
        for c in range(NCH):
            wp_ew_tok[ew] = P.dmas("gpsimd", f"wpb{ew}",
                                   wp_b[:, c * 1024 + (ew - 2) * 512: c * 1024 + (ew - 1) * 512],
                                   wp_d[c * 128:(c + 1) * 128, ew * 512:(ew + 1) * 512])

    def w2_slice_2(h, ew):
        if ew < 2:
            return wp_a[:, h * 1024 + ew * 512: h * 1024 + (ew + 1) * 512]
        return wp_b[:, h * 1024 + (ew - 2) * 512: h * 1024 + (ew - 1) * 512]

    out_stored = []

    def dest2(j, tok_y, y_rows):
        tok_store = P.dmas("sync", "outs", out_d[j * 128:(j + 1) * 128, :], y_rows[:, :],
                           waits=[tok_y])
        out_stored.append(tok_store)
        return [tok_store]

    outproj_ln("c2", ctx2_d, x2_d, w2_slice_2, None, dest2,
               ctx_ready=ctx2_done, w2_ready=lambda ew: [wp_ew_tok[ew]],
               resid_ready=x2_ready, c_stack=c2_stack)

    P.do("sync", lambda e: None, waits=[out_stored[-1]])
    P.do("gpsimd", lambda e: None, waits=[out_stored[-1]])

    P.emit()
    c2_stack.close()
    root.close()
    return nc


_NC_CACHE = None


def kernel(L0_attn_x, L0_attn_Wqkv_w, L0_attn_Wqkv_b, L0_attn_out_w, L0_attn_out_b,
           L0_ffn_Wa, L0_ffn_Wp, L0_ffn_w_std, L0_ffn_w_rec):
    global _NC_CACHE
    if _NC_CACHE is None:
        _NC_CACHE = build_program()
    nc = _NC_CACHE

    x = np.ascontiguousarray(np.asarray(L0_attn_x, dtype=np.float32))
    base = {
        "wqkv": np.ascontiguousarray(np.asarray(L0_attn_Wqkv_w, dtype=np.float32)),
        "bqkv": np.ascontiguousarray(np.asarray(L0_attn_Wqkv_b, dtype=np.float32)).reshape(1, EQ),
        "wout": np.ascontiguousarray(np.asarray(L0_attn_out_w, dtype=np.float32)),
        "bout": np.ascontiguousarray(np.asarray(L0_attn_out_b, dtype=np.float32)).reshape(1, D),
        "wa": np.ascontiguousarray(np.asarray(L0_ffn_Wa, dtype=np.float32)),
        "wp": np.ascontiguousarray(np.asarray(L0_ffn_Wp, dtype=np.float32)),
        "wstd": np.ascontiguousarray(np.asarray(L0_ffn_w_std, dtype=np.float32)).reshape(1, H),
        "wrec": np.ascontiguousarray(np.asarray(L0_ffn_w_rec, dtype=np.float32)).reshape(1, H),
    }
    in_maps = []
    for b in range(8):
        m = dict(base)
        m["x"] = np.ascontiguousarray(x[b])
        in_maps.append(m)
    res = run_bass_kernel_spmd(nc, in_maps, core_ids=list(range(8)))
    out = np.stack([res.results[b]["out"] for b in range(8)], axis=0).astype(np.float32)
    return out


# version 49
# speedup vs baseline: 1.0696x; 1.0096x over previous
import math
import contextlib
import numpy as np
import concourse.bass as bass
import concourse.mybir as mybir
from concourse.bass_utils import run_bass_kernel_spmd

F32 = mybir.dt.float32
BF16 = mybir.dt.bfloat16
AF = mybir.ActivationFunctionType
ALU = mybir.AluOpType
AX = mybir.AxisListType

S = 2048
D = 2048
H = 16
DH = 128
EQ = 2304
FDIM = 6144
NCH = D // 128
NJ = S // 128
NW = S // 512
ISQ = 1.0 / math.sqrt(128.0)
SUB = 8
GF = 4
MASK_NEG = -1e30
EPS = 1e-5

ENGINES = ("tensor", "vector", "scalar", "gpsimd", "sync")
DEBUG_SPILLS = False


class Prog:

    def __init__(self, nc):
        self.nc = nc
        self.items = {e: [] for e in ENGINES}
        self.counts = {}
        self.fence = []

    def sem(self, name):
        self.counts.setdefault(name, 0)
        return name

    def tok(self, name):
        self.sem(name)
        return (name, self.counts[name])

    def set_fence(self):
        self.fence = [(n, v) for n, v in self.counts.items() if v > 0]

    def do(self, eng, fn, waits=(), inc=None, inc_by=1):
        flat = []
        for t in waits:
            if t is None:
                continue
            if isinstance(t, list):
                flat.extend(x for x in t if x is not None)
            else:
                flat.append(t)
        w = flat + list(self.fence)
        tokv = None
        if inc is not None:
            self.sem(inc)
            self.counts[inc] += inc_by
            tokv = (inc, self.counts[inc])
        self.items[eng].append((fn, w, inc, inc_by))
        return tokv

    def dma(self, eng, out_ap, in_ap, waits=(), inc=None, **kw):
        def fn(e, out_ap=out_ap, in_ap=in_ap, kw=kw):
            return e.dma_start(out_ap, in_ap, **kw)
        return self.do(eng, fn, waits=waits, inc=inc, inc_by=16)

    def dmas(self, eng, sem, out_ap, in_ap, waits=(), **kw):
        return self.dma(eng, out_ap, in_ap, waits=waits, inc=sem, **kw)

    def emit(self):
        nc = self.nc
        with contextlib.ExitStack() as es:
            handles = {n: es.enter_context(nc.semaphore(n)) for n in self.counts}
            with nc.Block() as block:
                for eng_name in ENGINES:
                    items = self.items[eng_name]
                    if not items:
                        continue

                    def body(eng, items=items):
                        waited = {}
                        for fn, waits, inc, inc_by in items:
                            for (sname, val) in waits:
                                if waited.get(sname, 0) < val:
                                    eng.wait_ge(handles[sname], val)
                                    waited[sname] = val
                            ins = fn(eng)
                            if inc is not None:
                                ins.then_inc(handles[inc], inc_by)
                    getattr(block, eng_name)(body)


def build_program():
    nc = bass.Bass("TRN2", target_bir_lowering=False)

    x_d = nc.dram_tensor("x", [S, D], F32, kind="ExternalInput")
    wqkv_d = nc.dram_tensor("wqkv", [EQ, D], F32, kind="ExternalInput")
    bqkv_d = nc.dram_tensor("bqkv", [1, EQ], F32, kind="ExternalInput")
    wout_d = nc.dram_tensor("wout", [D, D], F32, kind="ExternalInput")
    bout_d = nc.dram_tensor("bout", [1, D], F32, kind="ExternalInput")
    wa_d = nc.dram_tensor("wa", [D, FDIM], F32, kind="ExternalInput")
    wp_d = nc.dram_tensor("wp", [D, D], F32, kind="ExternalInput")
    wstd_d = nc.dram_tensor("wstd", [1, H], F32, kind="ExternalInput")
    wrec_d = nc.dram_tensor("wrec", [1, H], F32, kind="ExternalInput")
    out_d = nc.dram_tensor("out", [S, D], F32, kind="ExternalOutput")

    spill_kind = "ExternalOutput" if DEBUG_SPILLS else "Internal"
    qf1_d = nc.dram_tensor("qf1_spill", [D, S], BF16, kind=spill_kind)
    ctx1_d = nc.dram_tensor("ctx1_spill", [D, S], BF16, kind=spill_kind)
    x2_d = nc.dram_tensor("x2_spill", [S, D], F32, kind=spill_kind)
    x2t_d = nc.dram_tensor("x2t_spill", [D, S], BF16, kind=spill_kind)
    qf2_d = nc.dram_tensor("qf2_spill", [D, S], BF16, kind=spill_kind)
    ctx2_d = nc.dram_tensor("ctx2_spill", [D, S], BF16, kind=spill_kind)

    P = Prog(nc)
    root = contextlib.ExitStack()

    def mk_alloc(stack, side=None):
        def sb(name, shape, dt):
            return stack.enter_context(nc.sbuf_tensor(name, shape, dt, side=side))

        def ps(name, shape, dt=F32):
            return stack.enter_context(nc.psum_tensor(name, shape, dt))
        return sb, ps

    sb0, ps0 = mk_alloc(root)

    ident = sb0("ident", [128, 128], BF16)
    identf = sb0("identf", [128, 128], F32)
    maskT = sb0("maskT", [128, 128], BF16)
    mask01 = sb0("mask01", [128, 128], BF16)
    ones_r512 = sb0("ones_r512", [1, 512], BF16)
    ones_c128 = sb0("ones_c128", [128, 1], BF16)
    ones_rf = sb0("ones_rf", [1, 128], F32)
    wstd_sb = sb0("wstd_sb", [1, H], F32)
    wrec_sb = sb0("wrec_sb", [1, H], F32)
    sstd_sb = sb0("sstd_sb", [1, H], F32)
    srec_sb = sb0("srec_sb", [1, H], F32)
    m01_sb = sb0("m01_sb", [1, H], F32)
    m01b = sb0("m01b", [128, H], F32)
    colscale = sb0("colscale", [128, 32], F32)


    for s in ("g", "sy", "scq", "mm", "ac", "ve", "gp", "so"):
        P.sem(s)


    kv1_stack = contextlib.ExitStack()
    sbKV1, _ = mk_alloc(kv1_stack)
    kt1 = sbKV1("kt1", [128, S], BF16)
    vtok1 = sbKV1("vtok1", [128, NJ * DH], BF16)
    bq_cols = sbKV1("bq_cols", [128, 18], F32)
    tok_bq = P.dmas("gpsimd", "ini0",
                    bq_cols[:, :],
                    bqkv_d[:, :].rearrange("o (c p) -> (o p) c", p=128),
                    allow_slow_non_contiguous=True)
    a_stack = contextlib.ExitStack()
    sbA, psA = mk_alloc(a_stack)
    xT = sbA("xT", [128, NCH * S], BF16)
    xrows_bf = [sbA(f"xrows_bf{i}", [128, D], BF16) for i in range(2)]
    wqrows = [sbA(f"wqrows{i}", [128, D], BF16) for i in range(2)]
    wqT = [sbA(f"wqT{i}", [128, NCH * 128], BF16) for i in range(2)]
    vstage1 = sbA("vstage1", [128, S], BF16)
    right_root = contextlib.ExitStack()
    sbR, _ = mk_alloc(right_root, side="right")
    qstage = [sbR(f"qstage{i}", [128, 512], BF16) for i in range(2)]

    tok_xrow = [None, None]
    tok_xtp = [None, None]
    xT_chunk_tok = [None] * NJ

    def emit_xT(j):
        b = j % 2
        tok_xrow[b] = P.dmas("gpsimd", f"xr{b}", xrows_bf[b][:, :], x_d[j * 128:(j + 1) * 128, :],
                             waits=[tok_xtp[b]])
        tok_xtp[b] = P.dmas(
            "sync", f"xt{b}",
            xT[:, :].rearrange("p (c s) -> p c s", c=NCH)[:, :, j * 128:(j + 1) * 128],
            xrows_bf[b][:, :].rearrange("p (c f) -> p c f", c=NCH),
            waits=[tok_xrow[b]], transpose=True)
        xT_chunk_tok[j] = tok_xtp[b]

    for j in range(4):
        emit_xT(j)

    t_ms = [None]

    def g_op(fn):
        tok = P.do("gpsimd", (lambda g, fn=fn: fn(g)), waits=[t_ms[0]], inc="gp")
        t_ms[0] = tok
        return tok

    g_op(lambda g: g.memset(ident[:, :], 0.0))
    g_op(lambda g: g.affine_select(out=ident[:, :], in_=ident[:, :], compare_op=ALU.not_equal,
                                   fill=1.0, base=0, pattern=[[-1, 128]], channel_multiplier=1))
    g_op(lambda g: g.memset(identf[:, :], 0.0))
    g_op(lambda g: g.affine_select(out=identf[:, :], in_=identf[:, :], compare_op=ALU.not_equal,
                                   fill=1.0, base=0, pattern=[[-1, 128]], channel_multiplier=1))
    g_op(lambda g: g.memset(maskT[:, :], 0.0))
    g_op(lambda g: g.affine_select(out=maskT[:, :], in_=maskT[:, :], compare_op=ALU.is_ge,
                                   fill=MASK_NEG, base=0, pattern=[[1, 128]], channel_multiplier=-1))
    g_op(lambda g: g.memset(mask01[:, :], 1.0))
    g_op(lambda g: g.affine_select(out=mask01[:, :], in_=mask01[:, :], compare_op=ALU.is_ge,
                                   fill=0.0, base=0, pattern=[[1, 128]], channel_multiplier=-1))
    g_op(lambda g: g.memset(ones_r512[:, :], 1.0))
    g_op(lambda g: g.memset(ones_c128[:, :], 1.0))
    tok_consts = g_op(lambda g: g.memset(ones_rf[:, :], 1.0))
    tok_ws = P.dmas("gpsimd", "ini2", wstd_sb[:, :], wstd_d[:, :])
    tok_wr = P.dmas("gpsimd", "ini3", wrec_sb[:, :], wrec_d[:, :])

    t1 = P.do("vector", lambda v: v.tensor_scalar_max(sstd_sb[:, :], wstd_sb[:, :], 1e-8),
              waits=[tok_ws], inc="ve")
    t2 = P.do("vector", lambda v: v.tensor_scalar_max(srec_sb[:, :], wrec_sb[:, :], 1e-8),
              waits=[tok_wr], inc="ve")
    tok_m01 = P.do("vector", lambda v: v.tensor_single_scalar(m01_sb[:, :], wrec_sb[:, :], 0.1, ALU.is_gt),
                   waits=[tok_wr], inc="ve")
    tok_ss = P.do("scalar", lambda s: s.activation(sstd_sb[:, :], sstd_sb[:, :], AF.Sqrt),
                  waits=[t1], inc="ac")
    tok_sr = P.do("scalar", lambda s: s.activation(srec_sb[:, :], srec_sb[:, :], AF.Sqrt),
                  waits=[t2], inc="ac")

    pro_stack = contextlib.ExitStack()
    sbP, psP = mk_alloc(pro_stack)
    u_row = sbP("u_row", [1, 128], F32)
    v_row = sbP("v_row", [1, 128], F32)
    sstd2 = sbP("sstd2", [1, 32], F32)
    srec2 = sbP("srec2", [1, 32], F32)
    cs_ps = psP("cs_ps", [128, 32])
    mb_ps = psP("mb_ps", [128, H])

    g_op(lambda g: g.memset(u_row[:, :], 0.0))
    g_op(lambda g: g.affine_select(out=u_row[:, :], in_=u_row[:, :], compare_op=ALU.is_ge,
                                   fill=1.0, base=-120, pattern=[[1, 128]], channel_multiplier=0))
    g_op(lambda g: g.memset(v_row[:, :], 1.0))
    tok_uv = g_op(lambda g: g.affine_select(out=v_row[:, :], in_=v_row[:, :], compare_op=ALU.is_ge,
                                            fill=0.0, base=-120, pattern=[[1, 128]], channel_multiplier=0))

    tuv3 = P.do("vector", lambda v: v.tensor_copy(sstd2[0:1, 0:16], sstd_sb[:, :]),
                waits=[tok_ss], inc="ve")
    tuv4 = P.do("vector", lambda v: v.tensor_copy(sstd2[0:1, 16:32], sstd_sb[:, :]),
                waits=[tok_ss], inc="ve")
    tuv5 = P.do("vector", lambda v: v.tensor_copy(srec2[0:1, 0:16], srec_sb[:, :]),
                waits=[tok_sr], inc="ve")
    tuv6 = P.do("vector", lambda v: v.tensor_copy(srec2[0:1, 16:32], srec_sb[:, :]),
                waits=[tok_sr], inc="ve")

    def build_colscale(t):
        t.matmul(cs_ps[:, :], u_row[:, :], sstd2[:, :], start=True, stop=False)
        return t.matmul(cs_ps[:, :], v_row[:, :], srec2[:, :], start=False, stop=True)

    tok_csm = P.do("tensor", build_colscale,
                   waits=[tuv3, tuv4, tuv5, tuv6, tok_consts, tok_uv], inc="mm")
    tok_cs1 = P.do("scalar", lambda s: s.mul(colscale[:, 0:16], cs_ps[:, 0:16], ISQ),
                   waits=[tok_csm], inc="ac")
    tok_cs2 = P.do("scalar", lambda s: s.copy(colscale[:, 16:32], cs_ps[:, 16:32]),
                   waits=[tok_csm], inc="ac")
    tok_mbm = P.do("tensor", lambda t: t.matmul(mb_ps[:, :], ones_rf[0:1, 0:128], m01_sb[:, :],
                                                start=True, stop=True),
                   waits=[tok_m01, tok_consts], inc="mm")
    tok_m01b = P.do("scalar", lambda s: s.copy(m01b[:, :], mb_ps[:, :]),
                    waits=[tok_mbm], inc="ac")
    colscale_ready = [tok_cs1, tok_cs2]
    pro_stack.close()

    proj_ps = [psA(f"proj_ps{i}", [128, 512]) for i in range(4)]

    ec_order = [16, 17] + list(range(16))
    tok_wqrow = [None, None]
    tok_wqtp = [None, None]
    tok_wq_used = [None, None]
    tok_evac = [None] * 4
    tok_qspill = [None, None]
    kt1_ready = []
    vtok1_ready = None
    q1_spill_tok = [None] * H

    for i, ec in enumerate(ec_order):
        b = i % 2
        tok_wqrow[b] = P.dmas("gpsimd", f"wqr{b}", wqrows[b][:, :], wqkv_d[ec * 128:(ec + 1) * 128, :],
                              waits=[tok_wqtp[b]])
        tok_wqtp[b] = P.dmas("sync", f"wqt{b}",
                             wqT[b][:, :].rearrange("p (c e) -> p c e", c=NCH),
                             wqrows[b][:, :].rearrange("p (c f) -> p c f", c=NCH),
                             waits=[tok_wqrow[b], tok_wq_used[b]], transpose=True)
        for w in range(NW):
            pb = (i * NW + w) % 4
            pst = proj_ps[pb]
            wq_b = wqT[b]

            def mmg(t, pst=pst, wq_b=wq_b, ec=ec, w=w):
                last = None
                for c in range(NCH):
                    last = t.matmul(pst[:, :], wq_b[:, c * 128:(c + 1) * 128],
                                    xT[:, c * S + w * 512: c * S + (w + 1) * 512],
                                    start=(c == 0), stop=(c == NCH - 1))
                return last
            tok_mm = P.do("tensor", mmg,
                          waits=[tok_wqtp[b], tok_evac[pb], tok_consts, tok_bq,
                                 tok_cs1, tok_cs2, tok_m01b,
                                 xT_chunk_tok[4 * w + 3], xT_chunk_tok[4 * w + 2]],
                          inc="mm")
            if ec == 16:
                tok_ev = P.do("vector", (lambda v, pst=pst, w=w: v.tensor_scalar(
                    kt1[:, w * 512:(w + 1) * 512], pst[:, :],
                    bq_cols[:, 16:17], None, op0=ALU.add)),
                    waits=[tok_mm], inc="ve")
                kt1_ready.append(tok_ev)
            elif ec == 17:
                tok_ev = P.do("vector", (lambda v, pst=pst, w=w: v.tensor_scalar(
                    vstage1[:, w * 512:(w + 1) * 512], pst[:, :],
                    bq_cols[:, 17:18], None, op0=ALU.add)),
                    waits=[tok_mm], inc="ve")
                if w == NW - 1:
                    vtok1_ready = P.dmas(
                        "sync", "v1t",
                        vtok1[:, :].rearrange("p (c f) -> p c f", c=NJ),
                        vstage1[:, :].rearrange("p (c s) -> p c s", c=NJ),
                        waits=[tok_ev, P.tok("ac")], transpose=True)
            else:
                qb = (i * NW + w) % 2
                tok_ev = P.do("vector", (lambda v, pst=pst, qb=qb, ec=ec: v.tensor_scalar(
                    qstage[qb][:, :], pst[:, :],
                    bq_cols[:, ec:ec + 1], ISQ, op0=ALU.add, op1=ALU.mult)),
                    waits=[tok_mm, tok_qspill[qb]], inc="ve")
                tok_qspill[qb] = P.dmas("sync", f"q1s{qb}",
                                        qf1_d[ec * 128:(ec + 1) * 128, w * 512:(w + 1) * 512],
                                        qstage[qb][:, :], waits=[tok_ev])
                q1_spill_tok[ec] = [tok_qspill[0], tok_qspill[1]]
            tok_evac[pb] = tok_ev
            tok_wq_used[b] = tok_mm
            if i == 0 and w < NW - 1:
                for jj in range(4 * (w + 1), 4 * (w + 1) + 4):
                    emit_xT(jj)

    P1A_done = [P.tok("mm"), P.tok("ac")]
    a_stack.close()
    P.set_fence()
    ow_stack = contextlib.ExitStack()
    sbW, _ = mk_alloc(ow_stack, side="right")
    outwT = sbW("outwT", [128, NCH * D], BF16)
    bo_bf = sbW("bo_bf", [1, D], BF16)
    tok_bo = P.dmas("gpsimd", "ini1", bo_bf[:, :], bout_d[:, :])
    worows = [sbW(f"worows{i}", [128, D], BF16) for i in range(2)]
    tok_worow = [None, None]
    tok_wotp = [None, None]
    for ecb in range(16):
        b = ecb % 2
        tok_worow[b] = P.dmas("gpsimd", f"wor{b}", worows[b][:, :], wout_d[ecb * 128:(ecb + 1) * 128, :],
                              waits=[tok_wotp[b]] + P1A_done)
        tok_wotp[b] = P.dmas(
            "sync", f"wot{b}",
            outwT[:, :].rearrange("p (c e) -> p c e", c=NCH)[:, :, ecb * 128:(ecb + 1) * 128],
            worows[b][:, :].rearrange("p (c f) -> p c f", c=NCH),
            waits=[tok_worow[b]], transpose=True)
    outwT_ready = [tok_wotp[0], tok_wotp[1]]

    def attn_block(tag, kf_slice, ksub_src, v_slice, qspill_d, ctxspill_d,
                   q_ready, kv_ready, att_stack):
        sbT, psT = mk_alloc(att_stack)
        qh = [sbT(f"{tag}qh{i}", [128, S], BF16) for i in range(2)]
        ksub = [sbT(f"{tag}ksub{i}", [128, S // SUB], BF16) for i in range(2)]
        qsub = [sbT(f"{tag}qsub{i}", [128, 128], BF16) for i in range(2)]
        mhat = [sbT(f"{tag}mhat{i}", [128, 1], F32) for i in range(2)]
        negmrow = [sbT(f"{tag}negmrow{i}", [1, 128], F32) for i in range(2)]
        mmin = [sbT(f"{tag}mmin{i}", [1, 1], F32) for i in range(2)]
        mbias = sbT(f"{tag}mbias", [128, 2], F32)
        ptile = [sbT(f"{tag}ptile{i}", [128, 512], BF16) for i in range(3)]
        recl = sbT(f"{tag}recl", [1, 512], F32)
        bcast_sb = sbT(f"{tag}bcast_sb", [128, 512], F32)
        ctx_sb = [sbT(f"{tag}ctx_sb{i}", [128, 512], BF16) for i in range(2)]
        s_ps = [psT(f"{tag}s_ps0", [128, 512]), psT(f"{tag}s_ps1", [128, 512]),
                psT(f"{tag}s_ps2", [128, 512])]
        l_ps = [psT(f"{tag}l_ps0", [1, 512]), psT(f"{tag}l_ps1", [1, 512])]
        c_ps = [psT(f"{tag}c_ps0", [128, 512]), psT(f"{tag}c_ps1", [128, 512])]
        bc_ps = psT(f"{tag}bc_ps", [128, 512])
        st_ps = bc_ps[:, 0:256]
        mt_view = s_ps[0][0:1, 0:128]

        tok_qh = [None, None]
        tok_qh_freed = [None, None]
        tok_ctxsp = [None, None]
        tok_ctx_tt = [None, None]
        tok_bcast_copy = None
        tok_stats_red = None
        tok_negm_evac = None
        tok_exp_bank = [None, None, None]
        tok_lpv_bank = [None, None, None]
        tok_recip_bank = [None, None]
        ctx_done = []
        pending_norm = []

        def emit_recip():
            ent = pending_norm[-1]
            (cp, wb, hh, ww, tok_lpv_last, _) = ent
            lp = l_ps[wb]
            tok_recip_bank[wb] = P.do(
                "vector", (lambda v, lp=lp: v.reciprocal(recl[:, :], lp[0:1, :])),
                waits=[tok_lpv_last, tok_bcast_copy], inc="ve")
            ent[5] = tok_recip_bank[wb]

        def flush_norm():
            while pending_norm:
                (cp, wb, hh, ww, tok_lpv_last, tok_recip) = pending_norm.pop(0)
                assert tok_recip is not None
                tok_bcmm = P.do("tensor", (lambda t: t.matmul(
                    bc_ps[:, :], ones_rf[0:1, 0:128], recl[:, :], start=True, stop=True)),
                    waits=[tok_recip, tok_bcast_copy,
                           st.tok_mbias[0], st.tok_mbias[1]], inc="mm")
                tbc = P.do("vector", (lambda v: v.tensor_copy(bcast_sb[:, :], bc_ps[:, :])),
                           waits=[tok_bcmm], inc="ve")
                ttt = P.do("vector", (lambda v, cp=cp, wb=wb: v.tensor_mul(
                    ctx_sb[wb][:, :], cp[:, :], bcast_sb[:, :])),
                    waits=[tbc, tok_ctxsp[wb]], inc="ve")
                tok_ctx_tt[wb] = ttt
                tok_ctxsp[wb] = P.dmas(
                    "sync", f"{tag}cs{wb}",
                    ctxspill_d[hh * 128:(hh + 1) * 128, ww * 512:(ww + 1) * 512],
                    ctx_sb[wb][:, :], waits=[ttt])
                ctx_done.append(tok_ctxsp[wb])
                return tbc
            return None

        def emit_mhat_chain(h):
            hb = h % 2
            st.chain_toks[hb] = {}
            tok_ksub = P.do("vector", (lambda v, h=h, hb=hb: v.tensor_copy(
                ksub[hb][:, :], ksub_src(h))),
                waits=kv_ready + [st.prev_stats[hb]], inc="ve")
            tok_qsub = P.do("vector", (lambda v, hb=hb: v.tensor_copy(
                qsub[hb][:, :], qh[hb][:, 0:S:16])),
                waits=[tok_qh[hb], st.prev_stats[hb]], inc="ve")
            tok_stats = P.do("tensor", (lambda t, hb=hb: t.matmul(
                st_ps[:, 0:S // SUB], qsub[hb][:, :], ksub[hb][:, :], start=True, stop=True)),
                waits=[tok_ksub, tok_qsub, st.tok_stats_red, st.tok_bcast_copy, P.tok("ac")],
                inc="mm")
            st.prev_stats[hb] = tok_stats
            st.tok_stats_red = P.do("vector", (lambda v, hb=hb: v.reduce_max(
                mhat[hb][:, 0:1], st_ps[:, 0:S // SUB], axis=AX.X, negate=True)),
                waits=[tok_stats], inc="ve")
            extra_mt = [tok_exp_bank[0], tok_lpv_bank[0], tok_lpv_bank[1], tok_lpv_bank[2]]
            tok_mt = P.do("tensor", (lambda t, hb=hb: t.transpose(
                mt_view, mhat[hb][:, 0:1], identf[:, :])),
                waits=[st.tok_stats_red, st.tok_negm_evac] + extra_mt, inc="mm")
            st.tok_negm_evac = P.do("scalar", (lambda s, hb=hb: s.copy(
                negmrow[hb][0:1, 0:128], mt_view)),
                waits=[tok_mt], inc="ac")
            tok_mmin = P.do("vector", (lambda v, hb=hb: v.tensor_reduce(
                mmin[hb][:, :], negmrow[hb][0:1, 0:128], axis=AX.X, op=ALU.min)),
                waits=[st.tok_negm_evac], inc="ve")
            tok_mbm = P.do("tensor", (lambda t, hb=hb: t.matmul(
                bc_ps[0:128, 0:1], ones_rf[0:1, 0:128], mmin[hb][:, :], start=True, stop=True)),
                waits=[tok_mmin, st.tok_bcast_copy], inc="mm")
            st.tok_mbias[hb] = P.do("scalar", (lambda s, hb=hb: s.copy(
                mbias[:, hb:hb + 1], bc_ps[0:128, 0:1])),
                waits=[tok_mbm, st.tok_exp_done[hb]], inc="ac")

        class st:
            tok_stats_red = None
            tok_negm_evac = None
            tok_bcast_copy = None
            tok_mbias = [None, None]
            tok_exp_done = [None, None]
            chain_toks = [None, None]
            prev_stats = [None, None]

        for h in range(H):
            hb = h % 2
            if h == 0:
                tok_qh[0] = P.dmas("scalar", f"{tag}qh0", qh[0][:, :], qspill_d[0:128, :],
                                   waits=q_ready(0))
                emit_mhat_chain(0)
            if h + 1 < H:
                nhb = (h + 1) % 2
                tok_qh[nhb] = P.dmas("scalar", f"{tag}qh{nhb}", qh[nhb][:, :],
                                     qspill_d[(h + 1) * 128:(h + 2) * 128, :],
                                     waits=[tok_qh_freed[nhb]] + q_ready(h + 1))
            tok_mbias = st.tok_mbias[hb]

            for w in range(NW):
                wb = w % 2
                lp = l_ps[wb]
                cp = c_ps[wb]
                kmax = 4 * w + 4
                tok_lpv = None
                for cc in range(kmax + 2):
                    if cc < kmax:
                        c = cc
                        sb_i = c % 3
                        sp = s_ps[sb_i]
                        q0 = max(w * 512, c * 128)
                        q1 = (w + 1) * 512
                        n = q1 - q0

                        def smm(t, sp=sp, c=c, w=w, q0=q0, q1=q1, h=h, hb=hb, n=n):
                            return t.matmul(sp[:, (q0 - w * 512):(q1 - w * 512)],
                                            kf_slice(h, c), qh[hb][:, q0:q1],
                                            start=True, stop=True)
                        tok_smm = P.do("tensor", smm,
                                       waits=[tok_mbias, tok_exp_bank[sb_i]] +
                                             ([st.tok_negm_evac] if sb_i == 0 else []),
                                       inc="mm")
                        tok_exp_bank[sb_i] = P.do(
                            "scalar", (lambda s, sp=sp, sb_i=sb_i, q0=q0, w=w, n=n, hb=hb:
                                       s.activation(ptile[sb_i][:, 0:n],
                                                    sp[:, (q0 - w * 512):(q0 - w * 512) + n],
                                                    AF.Exp, bias=mbias[:, hb:hb + 1])),
                            waits=[tok_smm, tok_lpv_bank[sb_i]], inc="ac")
                        if c >= 4 * w:
                            tok_exp_bank[sb_i] = P.do(
                                "vector", (lambda v, sb_i=sb_i: v.tensor_mul(
                                    ptile[sb_i][:, 0:128], ptile[sb_i][:, 0:128], mask01[:, :])),
                                waits=[tok_exp_bank[sb_i]], inc="ve")
                    if cc >= 2:
                        c = cc - 2
                        sb_i = c % 3
                        q0 = max(w * 512, c * 128)
                        q1 = (w + 1) * 512
                        n = q1 - q0

                        def lpv(t, lp=lp, cp=cp, sb_i=sb_i, c=c, q0=q0, w=w, n=n, h=h, kmax=kmax):
                            t.matmul(lp[0:1, (q0 - w * 512):(q0 - w * 512) + n],
                                     ones_c128[:, :], ptile[sb_i][:, 0:n],
                                     start=(c == 0), stop=(c == kmax - 1))
                            return t.matmul(cp[:, (q0 - w * 512):(q0 - w * 512) + n],
                                            v_slice(h, c), ptile[sb_i][:, 0:n],
                                            start=(c == 0), stop=(c == kmax - 1))
                        extra = [tok_ctx_tt[wb], tok_recip_bank[wb]] if c == 0 else []
                        tok_lpv = P.do("tensor", lpv,
                                       waits=[tok_exp_bank[sb_i]] + extra, inc="mm")
                        tok_lpv_bank[sb_i] = tok_lpv
                    if cc == 1 and w == 1 and len(pending_norm) >= 2:
                        tok_bcast_copy = flush_norm()
                        st.tok_bcast_copy = tok_bcast_copy
                    if cc == (3 if w == 0 else 2 if w == 1 else 0) \
                            and len(pending_norm) == 1 and pending_norm[-1][5] is None:
                        emit_recip()
                    if cc == (8 if w == 1 else 7) and w >= 1 and pending_norm \
                            and pending_norm[0][5] is not None:
                        tok_bcast_copy = flush_norm()
                        st.tok_bcast_copy = tok_bcast_copy
                pending_norm.append([cp, wb, h, w, tok_lpv, None])
                if w == 0 and h + 1 < H:
                    emit_mhat_chain(h + 1)
            tok_qh_freed[hb] = P.tok("mm")
            st.tok_exp_done[hb] = P.tok("ac")
        while pending_norm:
            if pending_norm[-1][5] is None:
                emit_recip()
            tok_bcast_copy = flush_norm()
        return ctx_done[-2:]

    def outproj_ln(tag, ctxspill_d, resid_d, w2_slice, bias_row, dest_writer,
                   ctx_ready, w2_ready, resid_ready, c_stack):
        sbC, psC = mk_alloc(c_stack)
        NBUF = 3
        ctx_win = [sbC(f"{tag}ctx_win{i}", [128, H * 128], BF16) for i in range(NBUF)]
        x_win = [sbC(f"{tag}x_win{i}", [128, D], F32) for i in range(NBUF)]
        z_sb = sbC(f"{tag}z_sb", [128, D], F32)
        sq_scr = sbC(f"{tag}sq_scr", [128, 2048], BF16)
        zsum4 = sbC(f"{tag}zsum4", [128, 4], F32)
        zsq4 = sbC(f"{tag}zsq4", [128, 4], F32)
        ln_s = sbC(f"{tag}ln_s", [128, 1], F32)
        ln_q = sbC(f"{tag}ln_q", [128, 1], F32)
        invstd = sbC(f"{tag}invstd", [128, 1], F32)
        nbias = sbC(f"{tag}nbias", [128, 1], F32)
        y_rows = sbC(f"{tag}y_rows", [128, D], F32)
        op_ps = [[psC(f"{tag}op_ps{pp}_{ew}", [128, 512]) for ew in range(4)] for pp in range(2)]

        tok_cw = [None] * NBUF
        tok_xw = [None] * NBUF
        tok_opfree = [[None] * 4 for _ in range(2)]
        tok_win_used = [None] * NBUF
        tok_r1_prev = None
        tok_r2_prev = None
        tok_sq_prev = [None] * 4
        tok_aff_prev = None
        tok_y_free = None
        dones = []
        def emit_win_load(j):
            wn = j % NBUF
            tok_cw[wn] = P.dmas("scalar", f"{tag}cw{wn}",
                                ctx_win[wn][:, :].rearrange("p (h s) -> p h s", h=H),
                                ctxspill_d[:, j * 128:(j + 1) * 128].rearrange("(h p) s -> p h s", p=128),
                                waits=[tok_win_used[wn]] + ctx_ready)
            tok_xw[wn] = P.dmas("scalar", f"{tag}xw{wn}", x_win[wn][:, :],
                                resid_d[j * 128:(j + 1) * 128, :],
                                waits=[tok_win_used[wn]] + resid_ready)

        emit_win_load(0)
        emit_win_load(1)
        for j in range(NJ):
            jb = j % 2
            wn = j % NBUF
            tok_cw_j = tok_cw[wn]
            tok_xw_j = tok_xw[wn]
            if j + 2 < NJ:
                emit_win_load(j + 2)
            tok_z = None
            for ew in range(4):
                opp = op_ps[jb][ew]

                def opmm(t, opp=opp, wn=wn, ew=ew):
                    first = True
                    if bias_row is not None:
                        t.matmul(opp[:, :], ones_r512[0:1, 0:128],
                                 bias_row[0:1, ew * 512:(ew + 1) * 512],
                                 start=True, stop=False)
                        first = False
                    last = None
                    for hh in range(H):
                        last = t.matmul(opp[:, :], ctx_win[wn][:, hh * 128:(hh + 1) * 128],
                                        w2_slice(hh, ew),
                                        start=first, stop=(hh == H - 1))
                        first = False
                    return last
                w2r = w2_ready(ew) if callable(w2_ready) else w2_ready
                tok_opmm = P.do("tensor", opmm,
                                waits=[tok_cw_j, tok_opfree[jb][ew], tok_bo] + w2r,
                                inc="mm")
                tok_z = P.do("vector", (lambda v, opp=opp, ew=ew, wn=wn: v.scalar_tensor_tensor(
                    z_sb[:, ew * 512:(ew + 1) * 512], opp[:, :], 1.0,
                    x_win[wn][:, ew * 512:(ew + 1) * 512],
                    op0=ALU.mult, op1=ALU.add, accum_out=zsum4[:, ew:ew + 1])),
                    waits=[tok_opmm, tok_xw_j, tok_aff_prev, tok_r1_prev], inc="ve")
                tok_opfree[jb][ew] = tok_z
                if ew == 3:
                    tok_win_used[wn] = tok_z
                tok_sq_prev[ew] = P.do("scalar", (lambda s, ew=ew: s.activation(
                    sq_scr[:, ew * 512:(ew + 1) * 512], z_sb[:, ew * 512:(ew + 1) * 512],
                    AF.Square, accum_out=zsq4[:, ew:ew + 1])),
                    waits=[tok_z, tok_r2_prev, tok_sq_prev[ew]], inc="ac")
            tok_sq_all = P.tok("ac")
            tok_r1_prev = P.do("vector", lambda v: v.reduce_sum(ln_s[:, :], zsum4[:, :], axis=AX.X),
                               waits=[tok_z], inc="ve")
            tok_r2_prev = P.do("vector", lambda v: v.reduce_sum(ln_q[:, :], zsq4[:, :], axis=AX.X),
                               waits=[tok_sq_all], inc="ve")
            tm = P.do("vector", lambda v: v.tensor_scalar_mul(ln_s[:, :], ln_s[:, :], 1.0 / D),
                      waits=[tok_r1_prev], inc="ve")
            te = P.do("vector", lambda v: v.tensor_scalar_mul(ln_q[:, :], ln_q[:, :], 1.0 / D),
                      waits=[tok_r2_prev], inc="ve")
            tv = P.do("vector", lambda v: v.scalar_tensor_tensor(
                ln_q[:, :], ln_s[:, :], ln_s[:, :], ln_q[:, :],
                op0=ALU.mult, op1=ALU.subtract),
                waits=[tm, te], inc="ve")
            tv2 = P.do("vector", lambda v: v.tensor_scalar_add(ln_q[:, :], ln_q[:, :], -EPS),
                       waits=[tv], inc="ve")
            tv3 = P.do("vector", lambda v: v.reciprocal(ln_q[:, :], ln_q[:, :]),
                       waits=[tv2], inc="ve")
            tv4 = P.do("vector", lambda v: v.tensor_scalar_mul(ln_q[:, :], ln_q[:, :], -1.0),
                       waits=[tv3], inc="ve")
            ti = P.do("scalar", lambda s: s.activation(invstd[:, :], ln_q[:, :], AF.Sqrt),
                      waits=[tv4], inc="ac")
            tb = P.do("vector", lambda v: v.scalar_tensor_tensor(
                nbias[:, :], ln_s[:, :], -1.0, invstd[:, :],
                op0=ALU.mult, op1=ALU.mult),
                waits=[ti], inc="ve")
            tok_aff = P.do("vector", (lambda v: v.tensor_scalar(
                y_rows[:, :], z_sb[:, :], invstd[:, :], nbias[:, :],
                op0=ALU.mult, op1=ALU.add)),
                waits=[tb] + (tok_y_free or []), inc="ve")
            tok_y_free = dest_writer(j, tok_aff, y_rows)
            tok_aff_prev = tok_aff
            dones.append(tok_y_free[0])
        return dones

    att1_stack = contextlib.ExitStack()

    def kf1_slice(h, c):
        return kt1[:, c * 128:(c + 1) * 128]

    def ksub1_src(h):
        return kt1[:, 0:S:SUB]

    def v1_slice(h, c):
        return vtok1[:, c * DH:(c + 1) * DH]

    ctx1_done = attn_block("a1", kf1_slice, ksub1_src, v1_slice, qf1_d, ctx1_d,
                           q_ready=lambda h: list(q1_spill_tok[h] or []),
                           kv_ready=kt1_ready[-1:] + [vtok1_ready],
                           att_stack=att1_stack)
    att1_stack.close()
    kv1_stack.close()

    c1_stack = contextlib.ExitStack()
    sbC1, _ = mk_alloc(c1_stack)
    x2bf = [sbC1(f"x2bf{i}", [128, D], BF16) for i in range(2)]
    x2t_stage = [sbC1(f"x2t_stage{i}", [128, NCH * 128], BF16) for i in range(2)]

    def w2_slice_1(h, ew):
        return outwT[:, h * D + ew * 512: h * D + (ew + 1) * 512]

    x2_stored = []
    x2t_stored = []
    x2t_j_tok = [None] * NJ
    x2_j_tok = [None] * NJ
    _d1 = {"bf": [None, None], "tp": [None, None], "st2": [None, None]}

    def dest1(j, tok_y, y_rows):
        jb = j % 2
        tok_store = P.dmas("sync", "x2s", x2_d[j * 128:(j + 1) * 128, :], y_rows[:, :],
                           waits=[tok_y])
        tok_bf = P.do("scalar", (lambda s, jb=jb, y_rows=y_rows: s.copy(x2bf[jb][:, :], y_rows[:, :])),
                      waits=[tok_y, _d1["tp"][jb]], inc="ac")
        tok_tp = P.dmas("sync", f"x2t{jb}",
                        x2t_stage[jb][:, :].rearrange("p (c s) -> p c s", c=NCH),
                        x2bf[jb][:, :].rearrange("p (c f) -> p c f", c=NCH),
                        waits=[tok_bf, _d1["st2"][jb]], transpose=True)
        tok_st2 = P.dmas("sync", f"x2u{jb}",
                         x2t_d[:, j * 128:(j + 1) * 128].rearrange("(c p) s -> p c s", p=128),
                         x2t_stage[jb][:, :].rearrange("p (c s) -> p c s", c=NCH),
                         waits=[tok_tp])
        _d1["tp"][jb] = tok_tp
        _d1["st2"][jb] = tok_st2
        x2_stored.append(tok_store)
        x2t_stored.append(tok_st2)
        x2t_j_tok[j] = tok_st2
        x2_j_tok[j] = tok_store
        return [tok_store, tok_bf]

    P.set_fence()
    outproj_ln("c1", ctx1_d, x_d, w2_slice_1, bo_bf, dest1,
               ctx_ready=ctx1_done, w2_ready=outwT_ready, resid_ready=[],
               c_stack=c1_stack)
    x2_ready = [x2_stored[-1], P.tok("ac")]
    x2t_ready = [x2t_stored[-1]]
    c1_stack.close()
    ow_stack.close()

    p2a_stack = contextlib.ExitStack()
    sb2, ps2 = mk_alloc(p2a_stack)
    kv2_stack = contextlib.ExitStack()
    sbKV2, _ = mk_alloc(kv2_stack)
    kf2 = sbKV2("kf2", [128, H * S], BF16)
    vtok2 = sbKV2("vtok2", [128, NJ * D], BF16)
    qtail = sb2("qtail", [128, NCH * 128], BF16)
    ktail = sb2("ktail", [128, NCH * 128], BF16)
    ltmp = sb2("ltmp", [128, NCH * 8], F32)
    lows_stack = contextlib.ExitStack()
    sbL, _ = mk_alloc(lows_stack)
    lows = sbL("lows", [128, NCH * 256], BF16)
    proj2_ps = [ps2(f"proj2_ps{i}", [128, 512]) for i in range(4)]

    P.set_fence()
    tok_lows = P.dmas("gpsimd", "low",
                      lows[:, :].rearrange("p (c f) -> p c f", c=NCH),
                      wa_d[:, 5888:6144].rearrange("(c p) f -> p c f", p=128))
    tok_tails = []
    prev_tail = [None, None]
    for h in range(H):
        qlo = lows[:, :].rearrange("p (c f) -> p c f", c=NCH)[:, :, h * 8:(h + 1) * 8]
        klo = lows[:, :].rearrange("p (c f) -> p c f", c=NCH)[:, :, 128 + h * 8:128 + (h + 1) * 8]
        qta = qtail[:, :].rearrange("p (c f) -> p c f", c=NCH)[:, :, h * 8:(h + 1) * 8]
        kta = ktail[:, :].rearrange("p (c f) -> p c f", c=NCH)[:, :, h * 8:(h + 1) * 8]
        lt = ltmp[:, :].rearrange("p (c f) -> p c f", c=NCH)
        td = P.do("vector", (lambda v, klo=klo, qlo=qlo, lt=lt: v.tensor_sub(lt, klo, qlo)),
                  waits=[tok_lows, tok_m01b] + prev_tail, inc="ve")
        tdm = P.do("vector", (lambda v, lt=lt, h=h: v.tensor_scalar_mul(lt, lt, m01b[:, h:h + 1])),
                   waits=[td], inc="ve")
        tq = P.do("vector", (lambda v, qta=qta, qlo=qlo, lt=lt: v.tensor_add(qta, qlo, lt)),
                  waits=[tdm], inc="ve")
        tk = P.do("vector", (lambda v, kta=kta, klo=klo, lt=lt: v.tensor_sub(kta, klo, lt)),
                  waits=[tdm], inc="ve")
        tok_tails += [tq, tk]
        prev_tail = [tq, tk]

    lows_stack.close()
    wagrp = [sb2(f"wagrp{i}", [128, NCH * GF * 128], BF16) for i in range(2)]
    SW2 = 512
    NW2 = S // SW2
    x2t_win = [sb2(f"x2t_win{i}", [128, NCH * SW2], BF16) for i in range(2)]
    vstage2 = sb2("vstage2", [128, SW2], BF16)

    chunks = [("k", h) for h in range(H)] + [("v", h) for h in range(H)] + \
             [("q", h) for h in range(H)]
    ngroups = len(chunks) // GF
    tok_wagrp = [None, None]
    tok_grp_used = [None, None]
    tok_x2w = [None, None]
    tok_x2w_used = [None, None]
    tok_ev2 = [None] * 4
    tok_qsp2 = [None, None]
    tok_vtp2 = None
    kf2_ready = []
    vtok2_ready = []
    q2_spill_tok = [None] * H

    items = [(gi, w) for gi in range(ngroups) for w in range(NW2)]

    def emit_x2t_load(idx):
        gi, w = items[idx]
        wbx = idx % 2
        tok_x2w[wbx] = P.dmas("scalar", f"x2w{wbx}",
                              x2t_win[wbx][:, :].rearrange("p (c s) -> p c s", c=NCH),
                              x2t_d[:, w * SW2:(w + 1) * SW2].rearrange("(c p) s -> p c s", p=128),
                              waits=[tok_x2w_used[wbx], x2t_j_tok[4 * w + 3], x2t_j_tok[4 * w + 2]])

    grp_weight_toks = [None] * ngroups

    def emit_group_weights(gi):
        gb = gi % 2
        grp = chunks[gi * GF:(gi + 1) * GF]
        kind = grp[0][0]
        h0 = grp[0][1]
        wg = wagrp[gb]
        wg4 = wg[:, :].rearrange("p (c g f) -> p c g f", c=NCH, g=GF)
        if kind == "v":
            tok_w = P.dmas("gpsimd", f"wag{gb}", wg4,
                           wa_d[:, 3840 + h0 * 128: 3840 + (h0 + GF) * 128]
                           .rearrange("(c p) (g f) -> p c g f", p=128, g=GF),
                           waits=[tok_grp_used[gb]] + tok_tails)
            grp_weight_toks[gi] = [tok_w]
        else:
            base = 1920 if kind == "k" else 0
            for g_in in range(GF):
                tok_w = P.dmas("gpsimd", f"wag{gb}", wg4[:, :, g_in, 0:120],
                               wa_d[:, base + (h0 + g_in) * 120: base + (h0 + g_in + 1) * 120]
                               .rearrange("(c p) f -> p c f", p=128),
                               waits=[tok_grp_used[gb]] + tok_tails)
            tailsrc = qtail if kind == "q" else ktail
            tok_tc = P.do("vector", (lambda v, wg4=wg4, tailsrc=tailsrc, h0=h0: v.tensor_copy(
                wg4[:, :, :, 120:128],
                tailsrc[:, :].rearrange("p (c f) -> p c f", c=NCH)[:, :, h0 * 8:(h0 + GF) * 8]
                .rearrange("p c (g f) -> p c g f", g=GF))),
                waits=[tok_w] + tok_tails + [tok_grp_used[gb]], inc="ve")
            grp_weight_toks[gi] = [tok_w, tok_tc]

    emit_group_weights(0)
    emit_group_weights(1)
    emit_x2t_load(0)
    emit_x2t_load(1)
    for idx, (gi, w) in enumerate(items):
        gb = gi % 2
        wbx = idx % 2
        grp = chunks[gi * GF:(gi + 1) * GF]
        wg = wagrp[gb]
        tok_wready = grp_weight_toks[gi]
        tok_this_win = tok_x2w[wbx]
        for g_in, (knd, hh) in enumerate(grp):
            pb = (idx * GF + g_in) % 4
            pst = proj2_ps[pb]

            def mmg2(t, pst=pst, wg=wg, g_in=g_in, wbx=wbx):
                last = None
                for c in range(NCH):
                    last = t.matmul(pst[:, :],
                                    wg[:, c * (GF * 128) + g_in * 128: c * (GF * 128) + (g_in + 1) * 128],
                                    x2t_win[wbx][:, c * SW2:(c + 1) * SW2],
                                    start=(c == 0), stop=(c == NCH - 1))
                return last
            tok_mm2 = P.do("tensor", mmg2,
                           waits=tok_wready + [tok_this_win, tok_ev2[pb]], inc="mm")
            tok_x2w_used[wbx] = tok_mm2
            tok_grp_used[gb] = tok_mm2
            if knd == "k":
                tok_ev = P.do("scalar", (lambda s, pst=pst, hh=hh, w=w: s.activation(
                    kf2[:, hh * S + w * SW2: hh * S + (w + 1) * SW2], pst[:, :],
                    AF.Copy, scale=colscale[:, 16 + hh:17 + hh])),
                    waits=[tok_mm2] + colscale_ready, inc="ac")
                kf2_ready.append(tok_ev)
            elif knd == "q":
                qb2 = (idx * GF + g_in) % 2
                tok_ev = P.do("scalar", (lambda s, pst=pst, hh=hh, qb2=qb2: s.activation(
                    qstage[qb2][:, 0:SW2], pst[:, :], AF.Copy, scale=colscale[:, hh:hh + 1])),
                    waits=[tok_mm2, tok_qsp2[qb2]] + colscale_ready, inc="ac")
                tok_qsp2[qb2] = P.dmas("sync", f"q2s{qb2}",
                                       qf2_d[hh * 128:(hh + 1) * 128, w * SW2:(w + 1) * SW2],
                                       qstage[qb2][:, 0:SW2], waits=[tok_ev])
                if w == NW2 - 1:
                    q2_spill_tok[hh] = [tok_qsp2[0], tok_qsp2[1]]
            else:
                tok_ev = P.do("scalar", (lambda s, pst=pst: s.copy(vstage2[:, :], pst[:, :])),
                              waits=[tok_mm2, tok_vtp2], inc="ac")
                tok_vtp2 = P.dmas(
                    "sync", "v2t",
                    vtok2[:, :].rearrange("p (c f) -> p c f", c=NJ)[:, 4 * w:4 * w + 4,
                                                                    hh * 128:(hh + 1) * 128],
                    vstage2[:, :].rearrange("p (t s) -> p t s", t=4),
                    waits=[tok_ev], transpose=True)
                vtok2_ready.append(tok_vtp2)
            tok_ev2[pb] = tok_ev
        if idx + 2 < len(items):
            emit_x2t_load(idx + 2)
        if w == NW2 - 1 and gi + 2 < ngroups:
            emit_group_weights(gi + 2)

    p2a_stack.close()
    att2_stack = contextlib.ExitStack()

    def kf2_slice(h, c):
        return kf2[:, h * S + c * 128: h * S + (c + 1) * 128]

    def ksub2_src(h):
        return kf2[:, h * S: (h + 1) * S: SUB]

    def v2_slice(h, c):
        return vtok2[:, c * D + h * 128: c * D + (h + 1) * 128]

    P.set_fence()
    wp_a = sbR("wp_a", [128, NCH * 1024], BF16)
    wp_a_tok = [None, None]
    for ew in range(2):
        for c in range(NCH):
            wp_a_tok[ew] = P.dmas("gpsimd", f"wpa{ew}",
                                  wp_a[:, c * 1024 + ew * 512: c * 1024 + (ew + 1) * 512],
                                  wp_d[c * 128:(c + 1) * 128, ew * 512:(ew + 1) * 512])
    ctx2_done = attn_block("a2", kf2_slice, ksub2_src, v2_slice, qf2_d, ctx2_d,
                           q_ready=lambda h: list(q2_spill_tok[h] or []),
                           kv_ready=kf2_ready[-1:] + vtok2_ready[-1:],
                           att_stack=att2_stack)
    att2_stack.close()
    kv2_stack.close()

    c2_stack = contextlib.ExitStack()
    sbC2, _ = mk_alloc(c2_stack)
    wp_b = sbC2("wp_b", [128, NCH * 1024], BF16)
    P.set_fence()
    wp_ew_tok = [None] * 4
    wp_ew_tok[0] = wp_a_tok[0]
    wp_ew_tok[1] = wp_a_tok[1]
    for ew in range(2, 4):
        for c in range(NCH):
            wp_ew_tok[ew] = P.dmas("gpsimd", f"wpb{ew}",
                                   wp_b[:, c * 1024 + (ew - 2) * 512: c * 1024 + (ew - 1) * 512],
                                   wp_d[c * 128:(c + 1) * 128, ew * 512:(ew + 1) * 512])

    def w2_slice_2(h, ew):
        if ew < 2:
            return wp_a[:, h * 1024 + ew * 512: h * 1024 + (ew + 1) * 512]
        return wp_b[:, h * 1024 + (ew - 2) * 512: h * 1024 + (ew - 1) * 512]

    out_stored = []

    def dest2(j, tok_y, y_rows):
        tok_store = P.dmas("sync", "outs", out_d[j * 128:(j + 1) * 128, :], y_rows[:, :],
                           waits=[tok_y])
        out_stored.append(tok_store)
        return [tok_store]

    outproj_ln("c2", ctx2_d, x2_d, w2_slice_2, None, dest2,
               ctx_ready=ctx2_done, w2_ready=lambda ew: [wp_ew_tok[ew]],
               resid_ready=x2_ready, c_stack=c2_stack)

    P.do("sync", lambda e: None, waits=[out_stored[-1]])
    P.do("gpsimd", lambda e: None, waits=[out_stored[-1]])

    P.emit()
    c2_stack.close()
    root.close()
    return nc


_NC_CACHE = None


def kernel(L0_attn_x, L0_attn_Wqkv_w, L0_attn_Wqkv_b, L0_attn_out_w, L0_attn_out_b,
           L0_ffn_Wa, L0_ffn_Wp, L0_ffn_w_std, L0_ffn_w_rec):
    global _NC_CACHE
    if _NC_CACHE is None:
        _NC_CACHE = build_program()
    nc = _NC_CACHE

    x = np.ascontiguousarray(np.asarray(L0_attn_x, dtype=np.float32))
    base = {
        "wqkv": np.ascontiguousarray(np.asarray(L0_attn_Wqkv_w, dtype=np.float32)),
        "bqkv": np.ascontiguousarray(np.asarray(L0_attn_Wqkv_b, dtype=np.float32)).reshape(1, EQ),
        "wout": np.ascontiguousarray(np.asarray(L0_attn_out_w, dtype=np.float32)),
        "bout": np.ascontiguousarray(np.asarray(L0_attn_out_b, dtype=np.float32)).reshape(1, D),
        "wa": np.ascontiguousarray(np.asarray(L0_ffn_Wa, dtype=np.float32)),
        "wp": np.ascontiguousarray(np.asarray(L0_ffn_Wp, dtype=np.float32)),
        "wstd": np.ascontiguousarray(np.asarray(L0_ffn_w_std, dtype=np.float32)).reshape(1, H),
        "wrec": np.ascontiguousarray(np.asarray(L0_ffn_w_rec, dtype=np.float32)).reshape(1, H),
    }
    in_maps = []
    for b in range(8):
        m = dict(base)
        m["x"] = np.ascontiguousarray(x[b])
        in_maps.append(m)
    res = run_bass_kernel_spmd(nc, in_maps, core_ids=list(range(8)))
    out = np.stack([res.results[b]["out"] for b in range(8)], axis=0).astype(np.float32)
    return out


# version 50 (final)
# speedup vs baseline: 1.0787x; 1.0086x over previous
import math
import contextlib
import numpy as np
import concourse.bass as bass
import concourse.mybir as mybir
from concourse.bass_utils import run_bass_kernel_spmd

F32 = mybir.dt.float32
BF16 = mybir.dt.bfloat16
AF = mybir.ActivationFunctionType
ALU = mybir.AluOpType
AX = mybir.AxisListType

S = 2048
D = 2048
H = 16
DH = 128
EQ = 2304
FDIM = 6144
NCH = D // 128
NJ = S // 128
NW = S // 512
ISQ = 1.0 / math.sqrt(128.0)
SUB = 8
GF = 4
MASK_NEG = -1e30
EPS = 1e-5

ENGINES = ("tensor", "vector", "scalar", "gpsimd", "sync")
DEBUG_SPILLS = False


class Prog:

    def __init__(self, nc):
        self.nc = nc
        self.items = {e: [] for e in ENGINES}
        self.counts = {}
        self.fence = []

    def sem(self, name):
        self.counts.setdefault(name, 0)
        return name

    def tok(self, name):
        self.sem(name)
        return (name, self.counts[name])

    def set_fence(self):
        self.fence = [(n, v) for n, v in self.counts.items() if v > 0]

    def do(self, eng, fn, waits=(), inc=None, inc_by=1):
        flat = []
        for t in waits:
            if t is None:
                continue
            if isinstance(t, list):
                flat.extend(x for x in t if x is not None)
            else:
                flat.append(t)
        w = flat + list(self.fence)
        tokv = None
        if inc is not None:
            self.sem(inc)
            self.counts[inc] += inc_by
            tokv = (inc, self.counts[inc])
        self.items[eng].append((fn, w, inc, inc_by))
        return tokv

    def dma(self, eng, out_ap, in_ap, waits=(), inc=None, **kw):
        def fn(e, out_ap=out_ap, in_ap=in_ap, kw=kw):
            return e.dma_start(out_ap, in_ap, **kw)
        return self.do(eng, fn, waits=waits, inc=inc, inc_by=16)

    def dmas(self, eng, sem, out_ap, in_ap, waits=(), **kw):
        return self.dma(eng, out_ap, in_ap, waits=waits, inc=sem, **kw)

    def emit(self):
        nc = self.nc
        with contextlib.ExitStack() as es:
            handles = {n: es.enter_context(nc.semaphore(n)) for n in self.counts}
            with nc.Block() as block:
                for eng_name in ENGINES:
                    items = self.items[eng_name]
                    if not items:
                        continue

                    def body(eng, items=items):
                        waited = {}
                        for fn, waits, inc, inc_by in items:
                            for (sname, val) in waits:
                                if waited.get(sname, 0) < val:
                                    eng.wait_ge(handles[sname], val)
                                    waited[sname] = val
                            ins = fn(eng)
                            if inc is not None:
                                ins.then_inc(handles[inc], inc_by)
                    getattr(block, eng_name)(body)


def build_program():
    nc = bass.Bass("TRN2", target_bir_lowering=False)

    x_d = nc.dram_tensor("x", [S, D], F32, kind="ExternalInput")
    wqkv_d = nc.dram_tensor("wqkv", [EQ, D], F32, kind="ExternalInput")
    bqkv_d = nc.dram_tensor("bqkv", [1, EQ], F32, kind="ExternalInput")
    wout_d = nc.dram_tensor("wout", [D, D], F32, kind="ExternalInput")
    bout_d = nc.dram_tensor("bout", [1, D], F32, kind="ExternalInput")
    wa_d = nc.dram_tensor("wa", [D, FDIM], F32, kind="ExternalInput")
    wp_d = nc.dram_tensor("wp", [D, D], F32, kind="ExternalInput")
    wstd_d = nc.dram_tensor("wstd", [1, H], F32, kind="ExternalInput")
    wrec_d = nc.dram_tensor("wrec", [1, H], F32, kind="ExternalInput")
    out_d = nc.dram_tensor("out", [S, D], F32, kind="ExternalOutput")

    spill_kind = "ExternalOutput" if DEBUG_SPILLS else "Internal"
    qf1_d = nc.dram_tensor("qf1_spill", [D, S], BF16, kind=spill_kind)
    ctx1_d = nc.dram_tensor("ctx1_spill", [D, S], BF16, kind=spill_kind)
    x2_d = nc.dram_tensor("x2_spill", [S, D], F32, kind=spill_kind)
    x2t_d = nc.dram_tensor("x2t_spill", [D, S], BF16, kind=spill_kind)
    qf2_d = nc.dram_tensor("qf2_spill", [D, S], BF16, kind=spill_kind)
    ctx2_d = nc.dram_tensor("ctx2_spill", [D, S], BF16, kind=spill_kind)

    P = Prog(nc)
    root = contextlib.ExitStack()

    def mk_alloc(stack, side=None):
        def sb(name, shape, dt):
            return stack.enter_context(nc.sbuf_tensor(name, shape, dt, side=side))

        def ps(name, shape, dt=F32):
            return stack.enter_context(nc.psum_tensor(name, shape, dt))
        return sb, ps

    sb0, ps0 = mk_alloc(root)

    ident = sb0("ident", [128, 128], BF16)
    identf = sb0("identf", [128, 128], F32)
    maskT = sb0("maskT", [128, 128], BF16)
    mask01 = sb0("mask01", [128, 128], BF16)
    ones_r512 = sb0("ones_r512", [1, 512], BF16)
    ones_c128 = sb0("ones_c128", [128, 1], BF16)
    ones_rf = sb0("ones_rf", [1, 128], F32)
    wstd_sb = sb0("wstd_sb", [1, H], F32)
    wrec_sb = sb0("wrec_sb", [1, H], F32)
    sstd_sb = sb0("sstd_sb", [1, H], F32)
    srec_sb = sb0("srec_sb", [1, H], F32)
    m01_sb = sb0("m01_sb", [1, H], F32)
    m01b = sb0("m01b", [128, H], F32)
    colscale = sb0("colscale", [128, 32], F32)


    for s in ("g", "sy", "scq", "mm", "ac", "ve", "gp", "so"):
        P.sem(s)


    kv1_stack = contextlib.ExitStack()
    sbKV1, _ = mk_alloc(kv1_stack)
    kt1 = sbKV1("kt1", [128, S], BF16)
    vtok1 = sbKV1("vtok1", [128, NJ * DH], BF16)
    bq_cols = sbKV1("bq_cols", [128, 18], F32)
    tok_bq = P.dmas("gpsimd", "ini0",
                    bq_cols[:, :],
                    bqkv_d[:, :].rearrange("o (c p) -> (o p) c", p=128),
                    allow_slow_non_contiguous=True)
    a_stack = contextlib.ExitStack()
    sbA, psA = mk_alloc(a_stack)
    xT = sbA("xT", [128, NCH * S], BF16)
    xrows_bf = [sbA(f"xrows_bf{i}", [128, D], BF16) for i in range(2)]
    wqrows = [sbA(f"wqrows{i}", [128, D], BF16) for i in range(2)]
    wqT = [sbA(f"wqT{i}", [128, NCH * 128], BF16) for i in range(2)]
    vstage1 = sbA("vstage1", [128, S], BF16)
    right_root = contextlib.ExitStack()
    sbR, _ = mk_alloc(right_root, side="right")
    qstage = [sbR(f"qstage{i}", [128, 512], BF16) for i in range(2)]

    tok_xrow = [None, None]
    tok_xtp = [None, None]
    xT_chunk_tok = [None] * NJ

    def emit_xT(j):
        b = j % 2
        tok_xrow[b] = P.dmas("gpsimd", f"xr{b}", xrows_bf[b][:, :], x_d[j * 128:(j + 1) * 128, :],
                             waits=[tok_xtp[b]])
        tok_xtp[b] = P.dmas(
            "sync", f"xt{b}",
            xT[:, :].rearrange("p (c s) -> p c s", c=NCH)[:, :, j * 128:(j + 1) * 128],
            xrows_bf[b][:, :].rearrange("p (c f) -> p c f", c=NCH),
            waits=[tok_xrow[b]], transpose=True)
        xT_chunk_tok[j] = tok_xtp[b]

    for j in range(4):
        emit_xT(j)

    t_ms = [None]

    def g_op(fn):
        tok = P.do("gpsimd", (lambda g, fn=fn: fn(g)), waits=[t_ms[0]], inc="gp")
        t_ms[0] = tok
        return tok

    g_op(lambda g: g.memset(ident[:, :], 0.0))
    g_op(lambda g: g.affine_select(out=ident[:, :], in_=ident[:, :], compare_op=ALU.not_equal,
                                   fill=1.0, base=0, pattern=[[-1, 128]], channel_multiplier=1))
    g_op(lambda g: g.memset(identf[:, :], 0.0))
    g_op(lambda g: g.affine_select(out=identf[:, :], in_=identf[:, :], compare_op=ALU.not_equal,
                                   fill=1.0, base=0, pattern=[[-1, 128]], channel_multiplier=1))
    g_op(lambda g: g.memset(maskT[:, :], 0.0))
    g_op(lambda g: g.affine_select(out=maskT[:, :], in_=maskT[:, :], compare_op=ALU.is_ge,
                                   fill=MASK_NEG, base=0, pattern=[[1, 128]], channel_multiplier=-1))
    g_op(lambda g: g.memset(mask01[:, :], 1.0))
    g_op(lambda g: g.affine_select(out=mask01[:, :], in_=mask01[:, :], compare_op=ALU.is_ge,
                                   fill=0.0, base=0, pattern=[[1, 128]], channel_multiplier=-1))
    g_op(lambda g: g.memset(ones_r512[:, :], 1.0))
    g_op(lambda g: g.memset(ones_c128[:, :], 1.0))
    tok_consts = g_op(lambda g: g.memset(ones_rf[:, :], 1.0))
    tok_ws = P.dmas("gpsimd", "ini2", wstd_sb[:, :], wstd_d[:, :])
    tok_wr = P.dmas("gpsimd", "ini3", wrec_sb[:, :], wrec_d[:, :])

    t1 = P.do("vector", lambda v: v.tensor_scalar_max(sstd_sb[:, :], wstd_sb[:, :], 1e-8),
              waits=[tok_ws], inc="ve")
    t2 = P.do("vector", lambda v: v.tensor_scalar_max(srec_sb[:, :], wrec_sb[:, :], 1e-8),
              waits=[tok_wr], inc="ve")
    tok_m01 = P.do("vector", lambda v: v.tensor_single_scalar(m01_sb[:, :], wrec_sb[:, :], 0.1, ALU.is_gt),
                   waits=[tok_wr], inc="ve")
    tok_ss = P.do("scalar", lambda s: s.activation(sstd_sb[:, :], sstd_sb[:, :], AF.Sqrt),
                  waits=[t1], inc="ac")
    tok_sr = P.do("scalar", lambda s: s.activation(srec_sb[:, :], srec_sb[:, :], AF.Sqrt),
                  waits=[t2], inc="ac")

    pro_stack = contextlib.ExitStack()
    sbP, psP = mk_alloc(pro_stack)
    u_row = sbP("u_row", [1, 128], F32)
    v_row = sbP("v_row", [1, 128], F32)
    sstd2 = sbP("sstd2", [1, 32], F32)
    srec2 = sbP("srec2", [1, 32], F32)
    cs_ps = psP("cs_ps", [128, 32])
    mb_ps = psP("mb_ps", [128, H])

    g_op(lambda g: g.memset(u_row[:, :], 0.0))
    g_op(lambda g: g.affine_select(out=u_row[:, :], in_=u_row[:, :], compare_op=ALU.is_ge,
                                   fill=1.0, base=-120, pattern=[[1, 128]], channel_multiplier=0))
    g_op(lambda g: g.memset(v_row[:, :], 1.0))
    tok_uv = g_op(lambda g: g.affine_select(out=v_row[:, :], in_=v_row[:, :], compare_op=ALU.is_ge,
                                            fill=0.0, base=-120, pattern=[[1, 128]], channel_multiplier=0))

    tuv3 = P.do("vector", lambda v: v.tensor_copy(sstd2[0:1, 0:16], sstd_sb[:, :]),
                waits=[tok_ss], inc="ve")
    tuv4 = P.do("vector", lambda v: v.tensor_copy(sstd2[0:1, 16:32], sstd_sb[:, :]),
                waits=[tok_ss], inc="ve")
    tuv5 = P.do("vector", lambda v: v.tensor_copy(srec2[0:1, 0:16], srec_sb[:, :]),
                waits=[tok_sr], inc="ve")
    tuv6 = P.do("vector", lambda v: v.tensor_copy(srec2[0:1, 16:32], srec_sb[:, :]),
                waits=[tok_sr], inc="ve")

    def build_colscale(t):
        t.matmul(cs_ps[:, :], u_row[:, :], sstd2[:, :], start=True, stop=False)
        return t.matmul(cs_ps[:, :], v_row[:, :], srec2[:, :], start=False, stop=True)

    tok_csm = P.do("tensor", build_colscale,
                   waits=[tuv3, tuv4, tuv5, tuv6, tok_consts, tok_uv], inc="mm")
    tok_cs1 = P.do("scalar", lambda s: s.mul(colscale[:, 0:16], cs_ps[:, 0:16], ISQ),
                   waits=[tok_csm], inc="ac")
    tok_cs2 = P.do("scalar", lambda s: s.copy(colscale[:, 16:32], cs_ps[:, 16:32]),
                   waits=[tok_csm], inc="ac")
    tok_mbm = P.do("tensor", lambda t: t.matmul(mb_ps[:, :], ones_rf[0:1, 0:128], m01_sb[:, :],
                                                start=True, stop=True),
                   waits=[tok_m01, tok_consts], inc="mm")
    tok_m01b = P.do("scalar", lambda s: s.copy(m01b[:, :], mb_ps[:, :]),
                    waits=[tok_mbm], inc="ac")
    colscale_ready = [tok_cs1, tok_cs2]
    pro_stack.close()

    proj_ps = [psA(f"proj_ps{i}", [128, 512]) for i in range(4)]

    ec_order = [16, 17] + list(range(16))
    tok_wqrow = [None, None]
    tok_wqtp = [None, None]
    tok_wq_used = [None, None]
    tok_evac = [None] * 4
    tok_qspill = [None, None]
    kt1_ready = []
    vtok1_ready = None
    q1_spill_tok = [None] * H

    for i, ec in enumerate(ec_order):
        b = i % 2
        tok_wqrow[b] = P.dmas("gpsimd", f"wqr{b}", wqrows[b][:, :], wqkv_d[ec * 128:(ec + 1) * 128, :],
                              waits=[tok_wqtp[b]])
        tok_wqtp[b] = P.dmas("sync", f"wqt{b}",
                             wqT[b][:, :].rearrange("p (c e) -> p c e", c=NCH),
                             wqrows[b][:, :].rearrange("p (c f) -> p c f", c=NCH),
                             waits=[tok_wqrow[b], tok_wq_used[b]], transpose=True)
        for w in range(NW):
            pb = (i * NW + w) % 4
            pst = proj_ps[pb]
            wq_b = wqT[b]

            def mmg(t, pst=pst, wq_b=wq_b, ec=ec, w=w):
                last = None
                for c in range(NCH):
                    last = t.matmul(pst[:, :], wq_b[:, c * 128:(c + 1) * 128],
                                    xT[:, c * S + w * 512: c * S + (w + 1) * 512],
                                    start=(c == 0), stop=(c == NCH - 1))
                return last
            tok_mm = P.do("tensor", mmg,
                          waits=[tok_wqtp[b], tok_evac[pb], tok_consts, tok_bq,
                                 tok_cs1, tok_cs2, tok_m01b,
                                 xT_chunk_tok[4 * w + 3], xT_chunk_tok[4 * w + 2]],
                          inc="mm")
            if ec == 16:
                tok_ev = P.do("vector", (lambda v, pst=pst, w=w: v.tensor_scalar(
                    kt1[:, w * 512:(w + 1) * 512], pst[:, :],
                    bq_cols[:, 16:17], None, op0=ALU.add)),
                    waits=[tok_mm], inc="ve")
                kt1_ready.append(tok_ev)
            elif ec == 17:
                tok_ev = P.do("vector", (lambda v, pst=pst, w=w: v.tensor_scalar(
                    vstage1[:, w * 512:(w + 1) * 512], pst[:, :],
                    bq_cols[:, 17:18], None, op0=ALU.add)),
                    waits=[tok_mm], inc="ve")
                if w == NW - 1:
                    vtok1_ready = P.dmas(
                        "sync", "v1t",
                        vtok1[:, :].rearrange("p (c f) -> p c f", c=NJ),
                        vstage1[:, :].rearrange("p (c s) -> p c s", c=NJ),
                        waits=[tok_ev, P.tok("ac")], transpose=True)
            else:
                qb = (i * NW + w) % 2
                tok_ev = P.do("vector", (lambda v, pst=pst, qb=qb, ec=ec: v.tensor_scalar(
                    qstage[qb][:, :], pst[:, :],
                    bq_cols[:, ec:ec + 1], ISQ, op0=ALU.add, op1=ALU.mult)),
                    waits=[tok_mm, tok_qspill[qb]], inc="ve")
                tok_qspill[qb] = P.dmas("sync", f"q1s{qb}",
                                        qf1_d[ec * 128:(ec + 1) * 128, w * 512:(w + 1) * 512],
                                        qstage[qb][:, :], waits=[tok_ev])
                q1_spill_tok[ec] = [tok_qspill[0], tok_qspill[1]]
            tok_evac[pb] = tok_ev
            tok_wq_used[b] = tok_mm
            if i == 0 and w < NW - 1:
                for jj in range(4 * (w + 1), 4 * (w + 1) + 4):
                    emit_xT(jj)

    P1A_done = [P.tok("mm"), P.tok("ac")]
    a_stack.close()
    P.set_fence()
    ow_stack = contextlib.ExitStack()
    sbW, _ = mk_alloc(ow_stack, side="right")
    outwT = sbW("outwT", [128, NCH * D], BF16)
    bo_bf = sbW("bo_bf", [1, D], BF16)
    tok_bo = P.dmas("gpsimd", "ini1", bo_bf[:, :], bout_d[:, :])
    worows = [sbW(f"worows{i}", [128, D], BF16) for i in range(2)]
    tok_worow = [None, None]
    tok_wotp = [None, None]
    for ecb in range(16):
        b = ecb % 2
        tok_worow[b] = P.dmas("gpsimd", f"wor{b}", worows[b][:, :], wout_d[ecb * 128:(ecb + 1) * 128, :],
                              waits=[tok_wotp[b]] + P1A_done)
        tok_wotp[b] = P.dmas(
            "sync", f"wot{b}",
            outwT[:, :].rearrange("p (c e) -> p c e", c=NCH)[:, :, ecb * 128:(ecb + 1) * 128],
            worows[b][:, :].rearrange("p (c f) -> p c f", c=NCH),
            waits=[tok_worow[b]], transpose=True)
    outwT_ready = [tok_wotp[0], tok_wotp[1]]

    def attn_block(tag, kf_slice, ksub_src, v_slice, qspill_d, ctxspill_d,
                   q_ready, kv_ready, att_stack):
        sbT, psT = mk_alloc(att_stack)
        qh = [sbT(f"{tag}qh{i}", [128, S], BF16) for i in range(2)]
        ksub = [sbT(f"{tag}ksub{i}", [128, S // SUB], BF16) for i in range(2)]
        qsub = [sbT(f"{tag}qsub{i}", [128, 128], BF16) for i in range(2)]
        mhat = [sbT(f"{tag}mhat{i}", [128, 1], F32) for i in range(2)]
        negmrow = [sbT(f"{tag}negmrow{i}", [1, 128], F32) for i in range(2)]
        mmin = [sbT(f"{tag}mmin{i}", [1, 1], F32) for i in range(2)]
        mbias = sbT(f"{tag}mbias", [128, 2], F32)
        ptile = [sbT(f"{tag}ptile{i}", [128, 512], BF16) for i in range(3)]
        recl = sbT(f"{tag}recl", [1, 512], F32)
        bcast_sb = sbT(f"{tag}bcast_sb", [128, 512], F32)
        ctx_sb = [sbT(f"{tag}ctx_sb{i}", [128, 512], BF16) for i in range(2)]
        s_ps = [psT(f"{tag}s_ps0", [128, 512]), psT(f"{tag}s_ps1", [128, 512]),
                psT(f"{tag}s_ps2", [128, 512])]
        l_ps = [psT(f"{tag}l_ps0", [1, 512]), psT(f"{tag}l_ps1", [1, 512])]
        c_ps = [psT(f"{tag}c_ps0", [128, 512]), psT(f"{tag}c_ps1", [128, 512])]
        bc_ps = psT(f"{tag}bc_ps", [128, 512])
        st_ps = bc_ps[:, 0:256]
        mt_view = s_ps[0][0:1, 0:128]

        tok_qh = [None, None]
        tok_qh_freed = [None, None]
        tok_ctxsp = [None, None]
        tok_ctx_tt = [None, None]
        tok_bcast_copy = None
        tok_stats_red = None
        tok_negm_evac = None
        tok_exp_bank = [None, None, None]
        tok_lpv_bank = [None, None, None]
        tok_recip_bank = [None, None]
        ctx_done = []
        pending_norm = []

        def emit_recip():
            ent = pending_norm[-1]
            (cp, wb, hh, ww, tok_lpv_last, _) = ent
            lp = l_ps[wb]
            tok_recip_bank[wb] = P.do(
                "vector", (lambda v, lp=lp: v.reciprocal(recl[:, :], lp[0:1, :])),
                waits=[tok_lpv_last, tok_bcast_copy], inc="ve")
            ent[5] = tok_recip_bank[wb]

        def flush_norm():
            while pending_norm:
                (cp, wb, hh, ww, tok_lpv_last, tok_recip) = pending_norm.pop(0)
                assert tok_recip is not None
                tok_bcmm = P.do("tensor", (lambda t: t.matmul(
                    bc_ps[:, :], ones_rf[0:1, 0:128], recl[:, :], start=True, stop=True)),
                    waits=[tok_recip, tok_bcast_copy,
                           st.tok_mbias[0], st.tok_mbias[1]], inc="mm")
                tbc = P.do("vector", (lambda v: v.tensor_copy(bcast_sb[:, :], bc_ps[:, :])),
                           waits=[tok_bcmm], inc="ve")
                ttt = P.do("vector", (lambda v, cp=cp, wb=wb: v.tensor_mul(
                    ctx_sb[wb][:, :], cp[:, :], bcast_sb[:, :])),
                    waits=[tbc, tok_ctxsp[wb]], inc="ve")
                tok_ctx_tt[wb] = ttt
                tok_ctxsp[wb] = P.dmas(
                    "sync", f"{tag}cs{wb}",
                    ctxspill_d[hh * 128:(hh + 1) * 128, ww * 512:(ww + 1) * 512],
                    ctx_sb[wb][:, :], waits=[ttt])
                ctx_done.append(tok_ctxsp[wb])
                return tbc
            return None

        def emit_mhat_chain(h):
            hb = h % 2
            st.chain_toks[hb] = {}
            tok_ksub = P.do("scalar", (lambda s, h=h, hb=hb: s.copy(
                ksub[hb][:, :], ksub_src(h))),
                waits=kv_ready + [st.prev_stats[hb]], inc="ac")
            tok_qsub = P.do("scalar", (lambda s, hb=hb: s.copy(
                qsub[hb][:, :], qh[hb][:, 0:S:16])),
                waits=[tok_qh[hb], st.prev_stats[hb]], inc="ac")
            tok_stats = P.do("tensor", (lambda t, hb=hb: t.matmul(
                st_ps[:, 0:S // SUB], qsub[hb][:, :], ksub[hb][:, :], start=True, stop=True)),
                waits=[tok_ksub, tok_qsub, st.tok_stats_red, st.tok_bcast_copy, P.tok("ac")],
                inc="mm")
            st.prev_stats[hb] = tok_stats
            st.tok_stats_red = P.do("vector", (lambda v, hb=hb: v.reduce_max(
                mhat[hb][:, 0:1], st_ps[:, 0:S // SUB], axis=AX.X, negate=True)),
                waits=[tok_stats], inc="ve")
            extra_mt = [tok_exp_bank[0], tok_lpv_bank[0], tok_lpv_bank[1], tok_lpv_bank[2]]
            tok_mt = P.do("tensor", (lambda t, hb=hb: t.transpose(
                mt_view, mhat[hb][:, 0:1], identf[:, :])),
                waits=[st.tok_stats_red, st.tok_negm_evac] + extra_mt, inc="mm")
            st.tok_negm_evac = P.do("scalar", (lambda s, hb=hb: s.copy(
                negmrow[hb][0:1, 0:128], mt_view)),
                waits=[tok_mt], inc="ac")
            tok_mmin = P.do("vector", (lambda v, hb=hb: v.tensor_reduce(
                mmin[hb][:, :], negmrow[hb][0:1, 0:128], axis=AX.X, op=ALU.min)),
                waits=[st.tok_negm_evac], inc="ve")
            tok_mbm = P.do("tensor", (lambda t, hb=hb: t.matmul(
                bc_ps[0:128, 0:1], ones_rf[0:1, 0:128], mmin[hb][:, :], start=True, stop=True)),
                waits=[tok_mmin, st.tok_bcast_copy], inc="mm")
            st.tok_mbias[hb] = P.do("scalar", (lambda s, hb=hb: s.copy(
                mbias[:, hb:hb + 1], bc_ps[0:128, 0:1])),
                waits=[tok_mbm, st.tok_exp_done[hb]], inc="ac")

        class st:
            tok_stats_red = None
            tok_negm_evac = None
            tok_bcast_copy = None
            tok_mbias = [None, None]
            tok_exp_done = [None, None]
            chain_toks = [None, None]
            prev_stats = [None, None]

        for h in range(H):
            hb = h % 2
            if h == 0:
                tok_qh[0] = P.dmas("scalar", f"{tag}qh0", qh[0][:, :], qspill_d[0:128, :],
                                   waits=q_ready(0))
                emit_mhat_chain(0)
            if h + 1 < H:
                nhb = (h + 1) % 2
                tok_qh[nhb] = P.dmas("scalar", f"{tag}qh{nhb}", qh[nhb][:, :],
                                     qspill_d[(h + 1) * 128:(h + 2) * 128, :],
                                     waits=[tok_qh_freed[nhb]] + q_ready(h + 1))
            tok_mbias = st.tok_mbias[hb]

            for w in range(NW):
                wb = w % 2
                lp = l_ps[wb]
                cp = c_ps[wb]
                kmax = 4 * w + 4
                tok_lpv = None
                for cc in range(kmax + 2):
                    if cc < kmax:
                        c = cc
                        sb_i = c % 3
                        sp = s_ps[sb_i]
                        q0 = max(w * 512, c * 128)
                        q1 = (w + 1) * 512
                        n = q1 - q0

                        def smm(t, sp=sp, c=c, w=w, q0=q0, q1=q1, h=h, hb=hb, n=n):
                            return t.matmul(sp[:, (q0 - w * 512):(q1 - w * 512)],
                                            kf_slice(h, c), qh[hb][:, q0:q1],
                                            start=True, stop=True)
                        tok_smm = P.do("tensor", smm,
                                       waits=[tok_mbias, tok_exp_bank[sb_i]] +
                                             ([st.tok_negm_evac] if sb_i == 0 else []),
                                       inc="mm")
                        tok_exp_bank[sb_i] = P.do(
                            "scalar", (lambda s, sp=sp, sb_i=sb_i, q0=q0, w=w, n=n, hb=hb:
                                       s.activation(ptile[sb_i][:, 0:n],
                                                    sp[:, (q0 - w * 512):(q0 - w * 512) + n],
                                                    AF.Exp, bias=mbias[:, hb:hb + 1])),
                            waits=[tok_smm, tok_lpv_bank[sb_i]], inc="ac")
                        if c >= 4 * w:
                            tok_exp_bank[sb_i] = P.do(
                                "vector", (lambda v, sb_i=sb_i: v.tensor_mul(
                                    ptile[sb_i][:, 0:128], ptile[sb_i][:, 0:128], mask01[:, :])),
                                waits=[tok_exp_bank[sb_i]], inc="ve")
                    if cc >= 2:
                        c = cc - 2
                        sb_i = c % 3
                        q0 = max(w * 512, c * 128)
                        q1 = (w + 1) * 512
                        n = q1 - q0

                        def lpv(t, lp=lp, cp=cp, sb_i=sb_i, c=c, q0=q0, w=w, n=n, h=h, kmax=kmax):
                            t.matmul(lp[0:1, (q0 - w * 512):(q0 - w * 512) + n],
                                     ones_c128[:, :], ptile[sb_i][:, 0:n],
                                     start=(c == 0), stop=(c == kmax - 1))
                            return t.matmul(cp[:, (q0 - w * 512):(q0 - w * 512) + n],
                                            v_slice(h, c), ptile[sb_i][:, 0:n],
                                            start=(c == 0), stop=(c == kmax - 1))
                        extra = [tok_ctx_tt[wb], tok_recip_bank[wb]] if c == 0 else []
                        tok_lpv = P.do("tensor", lpv,
                                       waits=[tok_exp_bank[sb_i]] + extra, inc="mm")
                        tok_lpv_bank[sb_i] = tok_lpv
                    if cc == 1 and w == 1 and len(pending_norm) >= 2:
                        tok_bcast_copy = flush_norm()
                        st.tok_bcast_copy = tok_bcast_copy
                    if cc == (3 if w == 0 else 2 if w == 1 else 0) \
                            and len(pending_norm) == 1 and pending_norm[-1][5] is None:
                        emit_recip()
                    if cc == (8 if w == 1 else 7) and w >= 1 and pending_norm \
                            and pending_norm[0][5] is not None:
                        tok_bcast_copy = flush_norm()
                        st.tok_bcast_copy = tok_bcast_copy
                pending_norm.append([cp, wb, h, w, tok_lpv, None])
                if w == 0 and h + 1 < H:
                    emit_mhat_chain(h + 1)
            tok_qh_freed[hb] = P.tok("mm")
            st.tok_exp_done[hb] = P.tok("ac")
        while pending_norm:
            if pending_norm[-1][5] is None:
                emit_recip()
            tok_bcast_copy = flush_norm()
        return ctx_done[-2:]

    def outproj_ln(tag, ctxspill_d, resid_d, w2_slice, bias_row, dest_writer,
                   ctx_ready, w2_ready, resid_ready, c_stack):
        sbC, psC = mk_alloc(c_stack)
        NBUF = 3
        ctx_win = [sbC(f"{tag}ctx_win{i}", [128, H * 128], BF16) for i in range(NBUF)]
        x_win = [sbC(f"{tag}x_win{i}", [128, D], F32) for i in range(NBUF)]
        z_sb = sbC(f"{tag}z_sb", [128, D], F32)
        sq_scr = sbC(f"{tag}sq_scr", [128, 2048], BF16)
        zsum4 = sbC(f"{tag}zsum4", [128, 4], F32)
        zsq4 = sbC(f"{tag}zsq4", [128, 4], F32)
        ln_s = sbC(f"{tag}ln_s", [128, 1], F32)
        ln_q = sbC(f"{tag}ln_q", [128, 1], F32)
        invstd = sbC(f"{tag}invstd", [128, 1], F32)
        nbias = sbC(f"{tag}nbias", [128, 1], F32)
        y_rows = sbC(f"{tag}y_rows", [128, D], F32)
        op_ps = [[psC(f"{tag}op_ps{pp}_{ew}", [128, 512]) for ew in range(4)] for pp in range(2)]

        tok_cw = [None] * NBUF
        tok_xw = [None] * NBUF
        tok_opfree = [[None] * 4 for _ in range(2)]
        tok_win_used = [None] * NBUF
        tok_r1_prev = None
        tok_r2_prev = None
        tok_sq_prev = [None] * 4
        tok_aff_prev = None
        tok_y_free = None
        dones = []
        def emit_win_load(j):
            wn = j % NBUF
            tok_cw[wn] = P.dmas("scalar", f"{tag}cw{wn}",
                                ctx_win[wn][:, :].rearrange("p (h s) -> p h s", h=H),
                                ctxspill_d[:, j * 128:(j + 1) * 128].rearrange("(h p) s -> p h s", p=128),
                                waits=[tok_win_used[wn]] + ctx_ready)
            tok_xw[wn] = P.dmas("scalar", f"{tag}xw{wn}", x_win[wn][:, :],
                                resid_d[j * 128:(j + 1) * 128, :],
                                waits=[tok_win_used[wn]] + resid_ready)

        emit_win_load(0)
        emit_win_load(1)
        for j in range(NJ):
            jb = j % 2
            wn = j % NBUF
            tok_cw_j = tok_cw[wn]
            tok_xw_j = tok_xw[wn]
            if j + 2 < NJ:
                emit_win_load(j + 2)
            tok_z = None
            for ew in range(4):
                opp = op_ps[jb][ew]

                def opmm(t, opp=opp, wn=wn, ew=ew):
                    first = True
                    if bias_row is not None:
                        t.matmul(opp[:, :], ones_r512[0:1, 0:128],
                                 bias_row[0:1, ew * 512:(ew + 1) * 512],
                                 start=True, stop=False)
                        first = False
                    last = None
                    for hh in range(H):
                        last = t.matmul(opp[:, :], ctx_win[wn][:, hh * 128:(hh + 1) * 128],
                                        w2_slice(hh, ew),
                                        start=first, stop=(hh == H - 1))
                        first = False
                    return last
                w2r = w2_ready(ew) if callable(w2_ready) else w2_ready
                tok_opmm = P.do("tensor", opmm,
                                waits=[tok_cw_j, tok_opfree[jb][ew], tok_bo] + w2r,
                                inc="mm")
                tok_z = P.do("vector", (lambda v, opp=opp, ew=ew, wn=wn: v.scalar_tensor_tensor(
                    z_sb[:, ew * 512:(ew + 1) * 512], opp[:, :], 1.0,
                    x_win[wn][:, ew * 512:(ew + 1) * 512],
                    op0=ALU.mult, op1=ALU.add, accum_out=zsum4[:, ew:ew + 1])),
                    waits=[tok_opmm, tok_xw_j, tok_aff_prev, tok_r1_prev], inc="ve")
                tok_opfree[jb][ew] = tok_z
                if ew == 3:
                    tok_win_used[wn] = tok_z
                tok_sq_prev[ew] = P.do("scalar", (lambda s, ew=ew: s.activation(
                    sq_scr[:, ew * 512:(ew + 1) * 512], z_sb[:, ew * 512:(ew + 1) * 512],
                    AF.Square, accum_out=zsq4[:, ew:ew + 1])),
                    waits=[tok_z, tok_r2_prev, tok_sq_prev[ew]], inc="ac")
            tok_sq_all = P.tok("ac")
            tok_r1_prev = P.do("vector", lambda v: v.reduce_sum(ln_s[:, :], zsum4[:, :], axis=AX.X),
                               waits=[tok_z], inc="ve")
            tok_r2_prev = P.do("vector", lambda v: v.reduce_sum(ln_q[:, :], zsq4[:, :], axis=AX.X),
                               waits=[tok_sq_all], inc="ve")
            tm = P.do("vector", lambda v: v.tensor_scalar_mul(ln_s[:, :], ln_s[:, :], 1.0 / D),
                      waits=[tok_r1_prev], inc="ve")
            te = P.do("vector", lambda v: v.tensor_scalar_mul(ln_q[:, :], ln_q[:, :], 1.0 / D),
                      waits=[tok_r2_prev], inc="ve")
            tv = P.do("vector", lambda v: v.scalar_tensor_tensor(
                ln_q[:, :], ln_s[:, :], ln_s[:, :], ln_q[:, :],
                op0=ALU.mult, op1=ALU.subtract),
                waits=[tm, te], inc="ve")
            tv2 = P.do("vector", lambda v: v.tensor_scalar_add(ln_q[:, :], ln_q[:, :], -EPS),
                       waits=[tv], inc="ve")
            tv3 = P.do("vector", lambda v: v.reciprocal(ln_q[:, :], ln_q[:, :]),
                       waits=[tv2], inc="ve")
            tv4 = P.do("vector", lambda v: v.tensor_scalar_mul(ln_q[:, :], ln_q[:, :], -1.0),
                       waits=[tv3], inc="ve")
            ti = P.do("scalar", lambda s: s.activation(invstd[:, :], ln_q[:, :], AF.Sqrt),
                      waits=[tv4], inc="ac")
            tb = P.do("vector", lambda v: v.scalar_tensor_tensor(
                nbias[:, :], ln_s[:, :], -1.0, invstd[:, :],
                op0=ALU.mult, op1=ALU.mult),
                waits=[ti], inc="ve")
            tok_aff = P.do("vector", (lambda v: v.tensor_scalar(
                y_rows[:, :], z_sb[:, :], invstd[:, :], nbias[:, :],
                op0=ALU.mult, op1=ALU.add)),
                waits=[tb] + (tok_y_free or []), inc="ve")
            tok_y_free = dest_writer(j, tok_aff, y_rows)
            tok_aff_prev = tok_aff
            dones.append(tok_y_free[0])
        return dones

    att1_stack = contextlib.ExitStack()

    def kf1_slice(h, c):
        return kt1[:, c * 128:(c + 1) * 128]

    def ksub1_src(h):
        return kt1[:, 0:S:SUB]

    def v1_slice(h, c):
        return vtok1[:, c * DH:(c + 1) * DH]

    ctx1_done = attn_block("a1", kf1_slice, ksub1_src, v1_slice, qf1_d, ctx1_d,
                           q_ready=lambda h: list(q1_spill_tok[h] or []),
                           kv_ready=kt1_ready[-1:] + [vtok1_ready],
                           att_stack=att1_stack)
    att1_stack.close()
    kv1_stack.close()

    c1_stack = contextlib.ExitStack()
    sbC1, _ = mk_alloc(c1_stack)
    x2bf = [sbC1(f"x2bf{i}", [128, D], BF16) for i in range(2)]
    x2t_stage = [sbC1(f"x2t_stage{i}", [128, NCH * 128], BF16) for i in range(2)]

    def w2_slice_1(h, ew):
        return outwT[:, h * D + ew * 512: h * D + (ew + 1) * 512]

    x2_stored = []
    x2t_stored = []
    x2t_j_tok = [None] * NJ
    x2_j_tok = [None] * NJ
    _d1 = {"bf": [None, None], "tp": [None, None], "st2": [None, None]}

    def dest1(j, tok_y, y_rows):
        jb = j % 2
        tok_store = P.dmas("sync", "x2s", x2_d[j * 128:(j + 1) * 128, :], y_rows[:, :],
                           waits=[tok_y])
        tok_bf = P.do("scalar", (lambda s, jb=jb, y_rows=y_rows: s.copy(x2bf[jb][:, :], y_rows[:, :])),
                      waits=[tok_y, _d1["tp"][jb]], inc="ac")
        tok_tp = P.dmas("sync", f"x2t{jb}",
                        x2t_stage[jb][:, :].rearrange("p (c s) -> p c s", c=NCH),
                        x2bf[jb][:, :].rearrange("p (c f) -> p c f", c=NCH),
                        waits=[tok_bf, _d1["st2"][jb]], transpose=True)
        tok_st2 = P.dmas("sync", f"x2u{jb}",
                         x2t_d[:, j * 128:(j + 1) * 128].rearrange("(c p) s -> p c s", p=128),
                         x2t_stage[jb][:, :].rearrange("p (c s) -> p c s", c=NCH),
                         waits=[tok_tp])
        _d1["tp"][jb] = tok_tp
        _d1["st2"][jb] = tok_st2
        x2_stored.append(tok_store)
        x2t_stored.append(tok_st2)
        x2t_j_tok[j] = tok_st2
        x2_j_tok[j] = tok_store
        return [tok_store, tok_bf]

    P.set_fence()
    outproj_ln("c1", ctx1_d, x_d, w2_slice_1, bo_bf, dest1,
               ctx_ready=ctx1_done, w2_ready=outwT_ready, resid_ready=[],
               c_stack=c1_stack)
    x2_ready = [x2_stored[-1], P.tok("ac")]
    x2t_ready = [x2t_stored[-1]]
    c1_stack.close()
    ow_stack.close()

    p2a_stack = contextlib.ExitStack()
    sb2, ps2 = mk_alloc(p2a_stack)
    kv2_stack = contextlib.ExitStack()
    sbKV2, _ = mk_alloc(kv2_stack)
    kf2 = sbKV2("kf2", [128, H * S], BF16)
    vtok2 = sbKV2("vtok2", [128, NJ * D], BF16)
    qtail = sb2("qtail", [128, NCH * 128], BF16)
    ktail = sb2("ktail", [128, NCH * 128], BF16)
    ltmp = sb2("ltmp", [128, NCH * 8], F32)
    lows_stack = contextlib.ExitStack()
    sbL, _ = mk_alloc(lows_stack)
    lows = sbL("lows", [128, NCH * 256], BF16)
    proj2_ps = [ps2(f"proj2_ps{i}", [128, 512]) for i in range(4)]

    P.set_fence()
    tok_lows = P.dmas("gpsimd", "low",
                      lows[:, :].rearrange("p (c f) -> p c f", c=NCH),
                      wa_d[:, 5888:6144].rearrange("(c p) f -> p c f", p=128))
    tok_tails = []
    prev_tail = [None, None]
    for h in range(H):
        qlo = lows[:, :].rearrange("p (c f) -> p c f", c=NCH)[:, :, h * 8:(h + 1) * 8]
        klo = lows[:, :].rearrange("p (c f) -> p c f", c=NCH)[:, :, 128 + h * 8:128 + (h + 1) * 8]
        qta = qtail[:, :].rearrange("p (c f) -> p c f", c=NCH)[:, :, h * 8:(h + 1) * 8]
        kta = ktail[:, :].rearrange("p (c f) -> p c f", c=NCH)[:, :, h * 8:(h + 1) * 8]
        lt = ltmp[:, :].rearrange("p (c f) -> p c f", c=NCH)
        td = P.do("vector", (lambda v, klo=klo, qlo=qlo, lt=lt: v.tensor_sub(lt, klo, qlo)),
                  waits=[tok_lows, tok_m01b] + prev_tail, inc="ve")
        tdm = P.do("vector", (lambda v, lt=lt, h=h: v.tensor_scalar_mul(lt, lt, m01b[:, h:h + 1])),
                   waits=[td], inc="ve")
        tq = P.do("vector", (lambda v, qta=qta, qlo=qlo, lt=lt: v.tensor_add(qta, qlo, lt)),
                  waits=[tdm], inc="ve")
        tk = P.do("vector", (lambda v, kta=kta, klo=klo, lt=lt: v.tensor_sub(kta, klo, lt)),
                  waits=[tdm], inc="ve")
        tok_tails += [tq, tk]
        prev_tail = [tq, tk]

    lows_stack.close()
    wagrp = [sb2(f"wagrp{i}", [128, NCH * GF * 128], BF16) for i in range(2)]
    SW2 = 512
    NW2 = S // SW2
    x2t_win = [sb2(f"x2t_win{i}", [128, NCH * SW2], BF16) for i in range(2)]
    vstage2 = sb2("vstage2", [128, SW2], BF16)

    chunks = [("k", h) for h in range(H)] + [("v", h) for h in range(H)] + \
             [("q", h) for h in range(H)]
    ngroups = len(chunks) // GF
    tok_wagrp = [None, None]
    tok_grp_used = [None, None]
    tok_x2w = [None, None]
    tok_x2w_used = [None, None]
    tok_ev2 = [None] * 4
    tok_qsp2 = [None, None]
    tok_vtp2 = None
    kf2_ready = []
    vtok2_ready = []
    q2_spill_tok = [None] * H

    items = [(gi, w) for gi in range(ngroups) for w in range(NW2)]

    def emit_x2t_load(idx):
        gi, w = items[idx]
        wbx = idx % 2
        tok_x2w[wbx] = P.dmas("scalar", f"x2w{wbx}",
                              x2t_win[wbx][:, :].rearrange("p (c s) -> p c s", c=NCH),
                              x2t_d[:, w * SW2:(w + 1) * SW2].rearrange("(c p) s -> p c s", p=128),
                              waits=[tok_x2w_used[wbx], x2t_j_tok[4 * w + 3], x2t_j_tok[4 * w + 2]])

    grp_weight_toks = [None] * ngroups

    def emit_group_weights(gi):
        gb = gi % 2
        grp = chunks[gi * GF:(gi + 1) * GF]
        kind = grp[0][0]
        h0 = grp[0][1]
        wg = wagrp[gb]
        wg4 = wg[:, :].rearrange("p (c g f) -> p c g f", c=NCH, g=GF)
        if kind == "v":
            tok_w = P.dmas("gpsimd", f"wag{gb}", wg4,
                           wa_d[:, 3840 + h0 * 128: 3840 + (h0 + GF) * 128]
                           .rearrange("(c p) (g f) -> p c g f", p=128, g=GF),
                           waits=[tok_grp_used[gb]] + tok_tails)
            grp_weight_toks[gi] = [tok_w]
        else:
            base = 1920 if kind == "k" else 0
            for g_in in range(GF):
                tok_w = P.dmas("gpsimd", f"wag{gb}", wg4[:, :, g_in, 0:120],
                               wa_d[:, base + (h0 + g_in) * 120: base + (h0 + g_in + 1) * 120]
                               .rearrange("(c p) f -> p c f", p=128),
                               waits=[tok_grp_used[gb]] + tok_tails)
            tailsrc = qtail if kind == "q" else ktail
            tok_tc = P.do("vector", (lambda v, wg4=wg4, tailsrc=tailsrc, h0=h0: v.tensor_copy(
                wg4[:, :, :, 120:128],
                tailsrc[:, :].rearrange("p (c f) -> p c f", c=NCH)[:, :, h0 * 8:(h0 + GF) * 8]
                .rearrange("p c (g f) -> p c g f", g=GF))),
                waits=[tok_w] + tok_tails + [tok_grp_used[gb]], inc="ve")
            grp_weight_toks[gi] = [tok_w, tok_tc]

    emit_group_weights(0)
    emit_group_weights(1)
    emit_x2t_load(0)
    emit_x2t_load(1)
    for idx, (gi, w) in enumerate(items):
        gb = gi % 2
        wbx = idx % 2
        grp = chunks[gi * GF:(gi + 1) * GF]
        wg = wagrp[gb]
        tok_wready = grp_weight_toks[gi]
        tok_this_win = tok_x2w[wbx]
        for g_in, (knd, hh) in enumerate(grp):
            pb = (idx * GF + g_in) % 4
            pst = proj2_ps[pb]

            def mmg2(t, pst=pst, wg=wg, g_in=g_in, wbx=wbx):
                last = None
                for c in range(NCH):
                    last = t.matmul(pst[:, :],
                                    wg[:, c * (GF * 128) + g_in * 128: c * (GF * 128) + (g_in + 1) * 128],
                                    x2t_win[wbx][:, c * SW2:(c + 1) * SW2],
                                    start=(c == 0), stop=(c == NCH - 1))
                return last
            tok_mm2 = P.do("tensor", mmg2,
                           waits=tok_wready + [tok_this_win, tok_ev2[pb]], inc="mm")
            tok_x2w_used[wbx] = tok_mm2
            tok_grp_used[gb] = tok_mm2
            if knd == "k":
                tok_ev = P.do("scalar", (lambda s, pst=pst, hh=hh, w=w: s.activation(
                    kf2[:, hh * S + w * SW2: hh * S + (w + 1) * SW2], pst[:, :],
                    AF.Copy, scale=colscale[:, 16 + hh:17 + hh])),
                    waits=[tok_mm2] + colscale_ready, inc="ac")
                kf2_ready.append(tok_ev)
            elif knd == "q":
                qb2 = (idx * GF + g_in) % 2
                tok_ev = P.do("scalar", (lambda s, pst=pst, hh=hh, qb2=qb2: s.activation(
                    qstage[qb2][:, 0:SW2], pst[:, :], AF.Copy, scale=colscale[:, hh:hh + 1])),
                    waits=[tok_mm2, tok_qsp2[qb2]] + colscale_ready, inc="ac")
                tok_qsp2[qb2] = P.dmas("sync", f"q2s{qb2}",
                                       qf2_d[hh * 128:(hh + 1) * 128, w * SW2:(w + 1) * SW2],
                                       qstage[qb2][:, 0:SW2], waits=[tok_ev])
                if w == NW2 - 1:
                    q2_spill_tok[hh] = [tok_qsp2[0], tok_qsp2[1]]
            else:
                tok_ev = P.do("scalar", (lambda s, pst=pst: s.copy(vstage2[:, :], pst[:, :])),
                              waits=[tok_mm2, tok_vtp2], inc="ac")
                tok_vtp2 = P.dmas(
                    "sync", "v2t",
                    vtok2[:, :].rearrange("p (c f) -> p c f", c=NJ)[:, 4 * w:4 * w + 4,
                                                                    hh * 128:(hh + 1) * 128],
                    vstage2[:, :].rearrange("p (t s) -> p t s", t=4),
                    waits=[tok_ev], transpose=True)
                vtok2_ready.append(tok_vtp2)
            tok_ev2[pb] = tok_ev
        if idx + 2 < len(items):
            emit_x2t_load(idx + 2)
        if w == NW2 - 1 and gi + 2 < ngroups:
            emit_group_weights(gi + 2)

    p2a_stack.close()
    att2_stack = contextlib.ExitStack()

    def kf2_slice(h, c):
        return kf2[:, h * S + c * 128: h * S + (c + 1) * 128]

    def ksub2_src(h):
        return kf2[:, h * S: (h + 1) * S: SUB]

    def v2_slice(h, c):
        return vtok2[:, c * D + h * 128: c * D + (h + 1) * 128]

    P.set_fence()
    wp_a = sbR("wp_a", [128, NCH * 1024], BF16)
    wp_a_tok = [None, None]
    for ew in range(2):
        for c in range(NCH):
            wp_a_tok[ew] = P.dmas("gpsimd", f"wpa{ew}",
                                  wp_a[:, c * 1024 + ew * 512: c * 1024 + (ew + 1) * 512],
                                  wp_d[c * 128:(c + 1) * 128, ew * 512:(ew + 1) * 512])
    ctx2_done = attn_block("a2", kf2_slice, ksub2_src, v2_slice, qf2_d, ctx2_d,
                           q_ready=lambda h: list(q2_spill_tok[h] or []),
                           kv_ready=kf2_ready[-1:] + vtok2_ready[-1:],
                           att_stack=att2_stack)
    att2_stack.close()
    kv2_stack.close()

    c2_stack = contextlib.ExitStack()
    sbC2, _ = mk_alloc(c2_stack)
    wp_b = sbC2("wp_b", [128, NCH * 1024], BF16)
    P.set_fence()
    wp_ew_tok = [None] * 4
    wp_ew_tok[0] = wp_a_tok[0]
    wp_ew_tok[1] = wp_a_tok[1]
    for ew in range(2, 4):
        for c in range(NCH):
            wp_ew_tok[ew] = P.dmas("gpsimd", f"wpb{ew}",
                                   wp_b[:, c * 1024 + (ew - 2) * 512: c * 1024 + (ew - 1) * 512],
                                   wp_d[c * 128:(c + 1) * 128, ew * 512:(ew + 1) * 512])

    def w2_slice_2(h, ew):
        if ew < 2:
            return wp_a[:, h * 1024 + ew * 512: h * 1024 + (ew + 1) * 512]
        return wp_b[:, h * 1024 + (ew - 2) * 512: h * 1024 + (ew - 1) * 512]

    out_stored = []

    def dest2(j, tok_y, y_rows):
        tok_store = P.dmas("sync", "outs", out_d[j * 128:(j + 1) * 128, :], y_rows[:, :],
                           waits=[tok_y])
        out_stored.append(tok_store)
        return [tok_store]

    outproj_ln("c2", ctx2_d, x2_d, w2_slice_2, None, dest2,
               ctx_ready=ctx2_done, w2_ready=lambda ew: [wp_ew_tok[ew]],
               resid_ready=x2_ready, c_stack=c2_stack)

    P.do("sync", lambda e: None, waits=[out_stored[-1]])
    P.do("gpsimd", lambda e: None, waits=[out_stored[-1]])

    P.emit()
    c2_stack.close()
    root.close()
    return nc


_NC_CACHE = None


def kernel(L0_attn_x, L0_attn_Wqkv_w, L0_attn_Wqkv_b, L0_attn_out_w, L0_attn_out_b,
           L0_ffn_Wa, L0_ffn_Wp, L0_ffn_w_std, L0_ffn_w_rec):
    global _NC_CACHE
    if _NC_CACHE is None:
        _NC_CACHE = build_program()
    nc = _NC_CACHE

    x = np.ascontiguousarray(np.asarray(L0_attn_x, dtype=np.float32))
    base = {
        "wqkv": np.ascontiguousarray(np.asarray(L0_attn_Wqkv_w, dtype=np.float32)),
        "bqkv": np.ascontiguousarray(np.asarray(L0_attn_Wqkv_b, dtype=np.float32)).reshape(1, EQ),
        "wout": np.ascontiguousarray(np.asarray(L0_attn_out_w, dtype=np.float32)),
        "bout": np.ascontiguousarray(np.asarray(L0_attn_out_b, dtype=np.float32)).reshape(1, D),
        "wa": np.ascontiguousarray(np.asarray(L0_ffn_Wa, dtype=np.float32)),
        "wp": np.ascontiguousarray(np.asarray(L0_ffn_Wp, dtype=np.float32)),
        "wstd": np.ascontiguousarray(np.asarray(L0_ffn_w_std, dtype=np.float32)).reshape(1, H),
        "wrec": np.ascontiguousarray(np.asarray(L0_ffn_w_rec, dtype=np.float32)).reshape(1, H),
    }
    in_maps = []
    for b in range(8):
        m = dict(base)
        m["x"] = np.ascontiguousarray(x[b])
        in_maps.append(m)
    res = run_bass_kernel_spmd(nc, in_maps, core_ids=list(range(8)))
    out = np.stack([res.results[b]["out"] for b in range(8)], axis=0).astype(np.float32)
    return out
